# version 06
# baseline (speedup 1.0000x reference)
import contextlib
import numpy as np
import ml_dtypes

import concourse.bass as bass
import concourse.mybir as mybir
from concourse.bass_utils import run_bass_kernel_spmd

F32 = mybir.dt.float32
BF16 = mybir.dt.bfloat16
I32 = mybir.dt.int32
AF = mybir.ActivationFunctionType
ALU = mybir.AluOpType
AX = mybir.AxisListType

NCORES = 8
TOK = 4096
D = 1024
DC = 8
NPAIR = 8
NQB = 8
QBS = 512
NKC = 32
NTT = 32
FFN = 4096
FC = 32
NEXP = 8
CAP = 10240
NST = CAP // 128
TT_ALL = 32768
EPS = 1e-5
INVF = float(1 << 24)
BIS_ROUNDS = 34
SLOTB = 256
SLOT_E = NQB * SLOTB
NST_E = SLOT_E // 128
NSB = SLOT_E // 512


class Sem:
    def __init__(self, nc, name):
        self.h = nc.alloc_semaphore(name)
        self.v = 0

    def inc(self, n=1):
        self.v += n
        return self.v


NONE_MS = None


class Prog:
    ENGINES = ("sync", "tensor", "vector", "scalar", "gpsimd")

    def __init__(self, nc):
        self.nc = nc
        self.items = []
        self.e = {n: Sem(nc, "se_" + n) for n in self.ENGINES}
        self.d = {n: Sem(nc, "sd_" + n) for n in ("sync", "scalar", "gpsimd")}

    @staticmethod
    def _clean(waits):
        return tuple(w for w in waits if w is not None and w[1] > 0)

    def op(self, eng, fn, waits=()):
        s = self.e[eng]
        v = s.inc(1)
        self.items.append((eng, self._clean(waits), fn, ((s, 1),)))
        return (s, v)

    def dma(self, eng, out, in_, waits=(), **kw):
        s = self.d[eng]
        v = s.inc(16)
        self.items.append((eng, self._clean(waits),
                           lambda e, out=out, in_=in_, kw=kw: e.dma_start(out=out, in_=in_, **kw),
                           ((s, 16),)))
        return (s, v)

    def idma(self, out, out_off, in_, in_off, bounds, waits=(), **kw):
        s = self.d["gpsimd"]
        v = s.inc(16)

        def fn(e, out=out, oo=out_off, in_=in_, io=in_off, b=bounds, kw=kw):
            try:
                return e.indirect_dma_start(out, oo, in_, io, bounds_check=b,
                                            oob_is_err=False, **kw)
            except Exception as ex:
                raise RuntimeError(
                    f"idma failed: out={out.tensor.name if hasattr(out, 'tensor') else out} "
                    f"in={in_.tensor.name if hasattr(in_, 'tensor') else in_} "
                    f"kw={kw}: {ex}") from ex
        self.items.append(("gpsimd", self._clean(waits), fn, ((s, 16),)))
        return (s, v)

    def wt(self, eng, *ms):
        self.items.append((eng, self._clean(ms), lambda e: e.nop(), ()))

    def drain(self, eng):
        s = self.d[eng]
        return (s, s.v)

    def barrier(self):
        for n in ("sync", "scalar", "gpsimd"):
            self.wt(n, (self.d[n], self.d[n].v))
        vals = []
        for n in self.ENGINES:
            s = self.e[n]
            v = s.inc(1)
            self.items.append((n, (), lambda e: e.nop(), ((s, 1),)))
            vals.append((s, v))
        for n in self.ENGINES:
            self.items.append((n, tuple(vals), lambda e: e.nop(), ()))

    def emit(self, block):
        for ename in self.ENGINES:
            def make(ename):
                def run(eng):
                    for (e, waits, fn, incs) in self.items:
                        if e != ename:
                            continue
                        for (sem, val) in waits:
                            eng.wait_ge(sem.h, val)
                        ins = fn(eng)
                        for (sem, n) in incs:
                            ins = ins.then_inc(sem.h, n)
                return run
            getattr(block, ename)(make(ename))


def build_nc():
    nc = bass.Bass("TRN2", target_bir_lowering=False, debug=False)

    x_own = nc.dram_tensor("x_own", [TOK, D], F32, kind="ExternalInput").ap()
    x_fm = nc.dram_tensor("x_fm", [D, TOK], F32, kind="ExternalInput").ap()
    wqkv_pk = nc.dram_tensor("wqkv_pk", [24, DC, 128, 128], F32, kind="ExternalInput").ap()
    w_out = nc.dram_tensor("w_out", [D, D], F32, kind="ExternalInput").ap()
    b_out = nc.dram_tensor("b_out", [1, D], F32, kind="ExternalInput").ap()
    wr_in = nc.dram_tensor("wr_in", [D, NEXP], F32, kind="ExternalInput").ap()
    w1_pk = nc.dram_tensor("w1_pk", [NEXP, FC, DC, 128, 128], F32, kind="ExternalInput").ap()
    w2_in = nc.dram_tensor("w2_in", [NEXP, FFN, D], F32, kind="ExternalInput").ap()
    out_t = nc.dram_tensor("out", [TOK, D], F32, kind="ExternalOutput").ap()

    q_dram = nc.dram_tensor("q_dram", [NPAIR, 128, TOK], BF16, kind="Internal").ap()
    k_dram = nc.dram_tensor("k_dram", [NPAIR, 128, TOK], BF16, kind="Internal").ap()
    v_dram = nc.dram_tensor("v_dram", [NPAIR, NKC, 128, 130], BF16, kind="Internal").ap()
    nb_dram = nc.dram_tensor("nb_dram", [16, 1], F32, kind="Internal").ap()
    cw_dram = nc.dram_tensor("cw_dram", [1, NEXP], F32, kind="Internal").ap()
    cin_g = nc.dram_tensor("cin_g", [128, 256], F32, kind="Internal").ap()
    cout_g = nc.dram_tensor("cout_g", [NCORES * 128, 256], F32, kind="Internal").ap()
    ctot_dram = nc.dram_tensor("ctot_dram", [1, NEXP], F32, kind="Internal").ap()
    xn_dram = nc.dram_tensor("xn_dram", [128, DC * TOK], BF16, kind="Internal").ap()
    xn_tm_dram = nc.dram_tensor("xn_tm_dram", [TOK, D], BF16, kind="Internal").ap()
    xc_dram = nc.dram_tensor("xc_dram", [NEXP, 128, DC * SLOT_E], BF16,
                             kind="Internal").ap()
    y_dram = nc.dram_tensor("y_dram", [NEXP, NST_E, 128, D], BF16,
                            kind="Internal").ap()

    ident_c = nc.inline_tensor(np.eye(128, dtype=ml_dtypes.bfloat16), name="ident_c").ap()
    pairsel_np = np.zeros((128, 2), dtype=ml_dtypes.bfloat16)
    pairsel_np[0:64, 0] = 1
    pairsel_np[64:128, 1] = 1
    pairsel_c = nc.inline_tensor(pairsel_np, name="pairsel_c").ap()
    masks_np = np.zeros((128, 4 * QBS), dtype=ml_dtypes.bfloat16)
    kparr = np.arange(128)[:, None]
    qfarr = np.arange(QBS)[None, :]
    for m in range(4):
        masks_np[:, m * QBS:(m + 1) * QBS] = (qfarr >= kparr + 128 * m).astype(ml_dtypes.bfloat16)
    masks_c = nc.inline_tensor(masks_np, name="masks_c").ap()
    esel_np = np.zeros((16, 8 * 128), dtype=ml_dtypes.bfloat16)
    for c in range(8):
        esel_np[2 * c, c * 128:c * 128 + 64] = 1
        esel_np[2 * c + 1, c * 128 + 64:c * 128 + 128] = 1
    esel_c = nc.inline_tensor(esel_np, name="esel_c").ap()
    ones_row_c = nc.inline_tensor(np.ones((1, 128), np.float32), name="ones_row_c").ap()
    tri_np = (np.arange(128)[:, None] < np.arange(128)[None, :]).astype(np.float32)
    tri_c = nc.inline_tensor(tri_np, name="tri_c").ap()
    iota2_c = nc.inline_tensor(np.arange(SLOTB, dtype=np.float32)[None, :],
                               name="iota2_c").ap()

    p = Prog(nc)
    PE, VE, AC, GP = p.e["tensor"], p.e["vector"], p.e["scalar"], p.e["gpsimd"]
    DS, DSC, DG = p.d["sync"], p.d["scalar"], p.d["gpsimd"]

    es0 = contextlib.ExitStack()
    sb0 = lambda name, shape, dt: es0.enter_context(nc.sbuf_tensor(name, shape, dt))

    pairsel_s = sb0("pairsel_s", [128, 2], BF16)
    ident_s = sb0("ident_s", [128, 128], BF16)
    onesr_s = sb0("onesr_s", [1, 128], F32)
    negb_bc = sb0("negb_bc", [128, 16], F32)
    negb_pair = sb0("negb_pair", [128, 8], F32)
    stats_m = sb0("stats_m", [128, NTT], F32)
    stats_r = sb0("stats_r", [128, NTT], F32)
    eps_col = sb0("eps_col", [128, 1], F32)
    G_own = sb0("G_own", [128, 256], F32)
    wsel = sb0("wsel", [128, 256], F32)

    p.dma("sync", pairsel_s[:, :], pairsel_c[:, :])
    p.dma("sync", ident_s[:, :], ident_c[:, :])
    c_ld = p.dma("sync", onesr_s[:, :], ones_row_c[:, :])
    eps_ms = p.op("vector", lambda e: e.memset(eps_col[:, :], EPS))

    esA = contextlib.ExitStack()
    sbA = lambda name, shape, dt: esA.enter_context(nc.sbuf_tensor(name, shape, dt))
    xn_fm = sbA("xn_fm", [128, DC * TOK], BF16)
    xt = [sbA(f"xt{i}", [128, D], F32) for i in range(3)]
    scrA = sbA("scrA", [128, D], F32)
    sq_s = sbA("sq_s", [128, D], F32)
    xn32 = sbA("xn32", [128, D], F32)
    xnbf = [sbA(f"xnbf{i}", [128, D], BF16) for i in range(2)]
    st_sum = sbA("st_sum", [128, NTT], F32)
    st_sq = sbA("st_sq", [128, NTT], F32)
    st_var = sbA("st_var", [128, NTT], F32)
    st_sd = sbA("st_sd", [128, NTT], F32)
    m2_s = sbA("m2_s", [128, NTT], F32)
    ps_tr = [esA.enter_context(nc.psum_tensor(f"ps_tr{i}", [128, 128], BF16)) for i in range(2)]
    ps_qkv = [esA.enter_context(nc.psum_tensor(f"ps_qkv{i}", [128, QBS], F32)) for i in range(2)]
    ps_nrm = esA.enter_context(nc.psum_tensor("ps_nrm", [128, QBS], F32))


    ld_x = [NONE_MS] * NTT
    xn_done = [NONE_MS] * NTT
    xnbf_free = [NONE_MS, NONE_MS]
    trev = [NONE_MS, NONE_MS]
    ld_gate = [c_ld]
    for i in range(NTT):
        ld_x[i] = p.dma("sync", xt[i % 3][:, :], x_own[i * 128:(i + 1) * 128, :],
                        waits=ld_gate + ([] if i < 3 else [xn_done[i - 3]]))
        v_sum = p.op("vector", lambda e, i=i: e.tensor_scalar(
            scrA[:, :], xt[i % 3][:, :], 1.0, None, ALU.mult, ALU.add,
            accum_out=st_sum[:, i:i + 1]), waits=[ld_x[i]])
        a_sq = p.op("scalar", lambda e, i=i: e.activation(
            sq_s[:, :], xt[i % 3][:, :], AF.Square, accum_out=st_sq[:, i:i + 1]),
            waits=[ld_x[i]])
        ld_gate = [v_sum, a_sq]
        v_m = p.op("vector", lambda e, i=i: e.tensor_scalar(
            stats_m[:, i:i + 1], st_sum[:, i:i + 1], 1.0 / D, None, ALU.mult),
            waits=[v_sum])
        v_e2 = p.op("vector", lambda e, i=i: e.tensor_scalar(
            st_var[:, i:i + 1], st_sq[:, i:i + 1], 1.0 / D, None, ALU.mult),
            waits=[a_sq])
        v_m2 = p.op("vector", lambda e, i=i: e.tensor_tensor(
            m2_s[:, i:i + 1], stats_m[:, i:i + 1], stats_m[:, i:i + 1], ALU.mult),
            waits=[v_m])
        v_var = p.op("vector", lambda e, i=i: e.tensor_tensor(
            st_var[:, i:i + 1], st_var[:, i:i + 1], m2_s[:, i:i + 1], ALU.subtract),
            waits=[v_m2, v_e2])
        a_sd = p.op("scalar", lambda e, i=i: e.activation(
            st_sd[:, i:i + 1], st_var[:, i:i + 1], AF.Sqrt, bias=eps_col[:, 0:1]),
            waits=[v_var, eps_ms])
        v_r = p.op("vector", lambda e, i=i: e.reciprocal(
            stats_r[:, i:i + 1], st_sd[:, i:i + 1]), waits=[a_sd])
        v_xn = p.op("vector", lambda e, i=i: e.tensor_scalar(
            xn32[:, :], xt[i % 3][:, :], stats_m[:, i:i + 1], stats_r[:, i:i + 1],
            ALU.subtract, ALU.mult), waits=[v_r])
        v_bf = p.op("vector", lambda e, i=i: e.tensor_copy(
            xnbf[i % 2][:, :], xn32[:, :]),
            waits=[v_xn, xnbf_free[i % 2], p.drain("scalar")])
        p.dma("scalar", xn_tm_dram[i * 128:(i + 1) * 128, :], xnbf[i % 2][:, :],
              waits=[v_bf])
        last = NONE_MS
        for dc in range(DC):
            tp = p.op("tensor", lambda e, i=i, dc=dc: e.transpose(
                ps_tr[dc % 2][:, :], xnbf[i % 2][:, dc * 128:(dc + 1) * 128], ident_s[:, :]),
                waits=[v_bf, trev[dc % 2]])
            trev[dc % 2] = p.op("vector", lambda e, i=i, dc=dc: e.tensor_copy(
                xn_fm[:, dc * TOK + i * 128: dc * TOK + (i + 1) * 128], ps_tr[dc % 2][:, :]),
                waits=[tp])
            last = trev[dc % 2]
            if dc == DC - 1:
                xnbf_free[i % 2] = tp
        xn_done[i] = last
    xn_all = xn_done[NTT - 1]

    esA2 = contextlib.ExitStack()
    sbA2 = lambda name, shape, dt: esA2.enter_context(nc.sbuf_tensor(name, shape, dt))
    xfmb = [sbA2(f"xfmb{i}", [128, TOK], F32) for i in range(2)]
    wr_s = sbA2("wr_s", [128, DC * NEXP], F32)
    ones_col = sbA2("ones_col", [128, 1], F32)
    cwb = sbA2("cwb", [128, NEXP], F32)
    lgraw = sbA2("lgraw", [128, 256], F32)
    lgev = sbA2("lgev", [128, 256], F32)
    cwev = sbA2("cwev", [1, NEXP], F32)
    exps_a = sbA2("exps_a", [128, NEXP], F32)
    probs_a = sbA2("probs_a", [128, NEXP], F32)
    ge8_a = sbA2("ge8_a", [128, NEXP], F32)
    pm8_a = sbA2("pm8_a", [128, NEXP], F32)
    gm8_a = sbA2("gm8_a", [128, NEXP], F32)
    scr8 = sbA2("scr8", [128, NEXP], F32)
    lg2 = sbA2("lg2", [128, NEXP], F32)
    rt = sbA2("rt", [128, 8 * NTT], F32)
    ps_lg = [ps_qkv[0][:, 0:256], ps_qkv[1][:, 0:256]]
    ps_cw = ps_nrm

    wr_ld = NONE_MS
    for dc in range(DC):
        wr_ld = p.dma("sync", wr_s[:, dc * NEXP:(dc + 1) * NEXP],
                      wr_in[dc * 128:(dc + 1) * 128, :])
    oc_ms = p.op("vector", lambda e: e.memset(ones_col[:, :], 1.0))
    lg_zero = p.op("vector", lambda e: e.memset(lgraw[:, :], 0.0))

    xfm_ld = [NONE_MS, NONE_MS]
    xfm_free = [NONE_MS, NONE_MS]
    xfm_gate = wr_ld
    lg_acc = lg_zero
    cw_mm = NONE_MS
    psg_free = [NONE_MS, NONE_MS]
    for dc in range(DC):
        xfm_ld[dc % 2] = p.dma("sync", xfmb[dc % 2][:, :],
                               x_fm[dc * 128:(dc + 1) * 128, :],
                               waits=[xfm_free[dc % 2], xfm_gate])
        cw_mm = p.op("tensor", lambda e, dc=dc: e.matmul(
            ps_cw[0:1, 0:NEXP], ones_col[:, :], wr_s[:, dc * NEXP:(dc + 1) * NEXP],
            start=(dc == 0), stop=(dc == DC - 1)),
            waits=([wr_ld, oc_ms, xn_all] if dc == 0 else []))
        mmlast = NONE_MS
        for t in range(NTT):
            mmlast = p.op("tensor", lambda e, dc=dc, t=t: e.matmul(
                ps_lg[dc % 2][:, t * NEXP:(t + 1) * NEXP],
                xfmb[dc % 2][:, t * 128:(t + 1) * 128],
                wr_s[:, dc * NEXP:(dc + 1) * NEXP], start=True, stop=True),
                waits=([xfm_ld[dc % 2], psg_free[dc % 2]] if t == 0 else []))
            if t == 0:
                xfm_gate = mmlast
        xfm_free[dc % 2] = mmlast
        evg = p.op("vector", lambda e, dc=dc: e.tensor_copy(
            lgev[:, :], ps_lg[dc % 2][:, :]), waits=[mmlast])
        psg_free[dc % 2] = evg
        lg_acc = p.op("vector", lambda e: e.tensor_tensor(
            lgraw[:, :], lgraw[:, :], lgev[:, :], ALU.add), waits=[evg, lg_acc])
    cw_ev = p.op("vector", lambda e: e.tensor_copy(cwev[0:1, :], ps_cw[0:1, 0:NEXP]),
                 waits=[cw_mm])
    cw_st = p.dma("gpsimd", cw_dram[0:1, :], cwev[0:1, :], waits=[cw_ev])
    cw_bc = p.dma("sync", cwb[:, :], cw_dram[0:1, :].to_broadcast([128, NEXP]),
                  waits=[cw_st])

    g_done = NONE_MS
    for t in range(NTT):
        c0 = 8 * t
        w0 = [lg_acc, cw_bc] if t == 0 else [g_done]
        v1 = p.op("vector", lambda e, t=t: e.tensor_scalar(
            scr8[:, :], cwb[:, :], stats_m[:, t:t + 1], None, ALU.mult), waits=w0)
        v2 = p.op("vector", lambda e, t=t: e.tensor_tensor(
            lg2[:, :], lgraw[:, t * NEXP:(t + 1) * NEXP], scr8[:, :], ALU.subtract),
            waits=[v1])
        v3 = p.op("vector", lambda e, t=t: e.tensor_scalar(
            lg2[:, :], lg2[:, :], stats_r[:, t:t + 1], None, ALU.mult), waits=[v2])
        v4 = p.op("vector", lambda e, t=t, c0=c0: e.reduce_max(
            rt[:, c0:c0 + 1], lg2[:, :], axis=AX.X, negate=True), waits=[v3])
        a1 = p.op("scalar", lambda e, t=t, c0=c0: e.activation(
            exps_a[:, :], lg2[:, :], AF.Exp, bias=rt[:, c0:c0 + 1],
            accum_out=rt[:, c0 + 1:c0 + 2]), waits=[v4])
        v5 = p.op("vector", lambda e, c0=c0: e.reciprocal(
            rt[:, c0 + 2:c0 + 3], rt[:, c0 + 1:c0 + 2]), waits=[a1])
        v6 = p.op("vector", lambda e, c0=c0: e.tensor_scalar(
            probs_a[:, :], exps_a[:, :], rt[:, c0 + 2:c0 + 3], None, ALU.mult),
            waits=[v5])
        v7 = p.op("vector", lambda e, c0=c0: e.reduce_max(
            rt[:, c0 + 3:c0 + 4], probs_a[:, :], axis=AX.X), waits=[v6])
        v8 = p.op("vector", lambda e, c0=c0: e.tensor_scalar(
            ge8_a[:, :], probs_a[:, :], rt[:, c0 + 3:c0 + 4], None, ALU.is_ge,
            ALU.add, accum_out=rt[:, c0 + 4:c0 + 5]), waits=[v7])
        v9 = p.op("vector", lambda e: e.tensor_tensor(
            pm8_a[:, :], probs_a[:, :], ge8_a[:, :], ALU.subtract), waits=[v8])
        v10 = p.op("vector", lambda e, c0=c0: e.reduce_max(
            rt[:, c0 + 5:c0 + 6], pm8_a[:, :], axis=AX.X), waits=[v9])
        v11 = p.op("vector", lambda e, c0=c0: e.tensor_scalar(
            rt[:, c0 + 4:c0 + 5], rt[:, c0 + 4:c0 + 5], 2.0, None, ALU.is_ge),
            waits=[v10])
        v12 = p.op("vector", lambda e, c0=c0: e.tensor_tensor(
            rt[:, c0 + 7:c0 + 8], rt[:, c0 + 3:c0 + 4], rt[:, c0 + 5:c0 + 6],
            ALU.subtract), waits=[v11])
        v13 = p.op("vector", lambda e, c0=c0: e.tensor_tensor(
            rt[:, c0 + 7:c0 + 8], rt[:, c0 + 4:c0 + 5], rt[:, c0 + 7:c0 + 8],
            ALU.mult), waits=[v12])
        v14 = p.op("vector", lambda e, c0=c0: e.tensor_tensor(
            rt[:, c0 + 6:c0 + 7], rt[:, c0 + 5:c0 + 6], rt[:, c0 + 7:c0 + 8],
            ALU.add), waits=[v13])
        v15 = p.op("vector", lambda e, c0=c0: e.tensor_scalar(
            gm8_a[:, :], probs_a[:, :], rt[:, c0 + 6:c0 + 7], None, ALU.is_ge),
            waits=[v14])
        g_done = p.op("vector", lambda e, t=t: e.tensor_tensor(
            G_own[:, :].rearrange("q (e t) -> q e t", e=NEXP)[:, :, t],
            probs_a[:, :], gm8_a[:, :], ALU.mult), waits=[v15])

    g_ship = p.dma("gpsimd", cin_g[:, :], G_own[:, :], waits=[g_done])
    cc1 = p.op("gpsimd", lambda e: e.collective_compute(
        "AllGather", ALU.bypass, replica_groups=[list(range(NCORES))],
        ins=[cin_g], outs=[cout_g]), waits=[g_ship])

    esB = contextlib.ExitStack()
    sbB = lambda name, shape, dt: esB.enter_context(nc.sbuf_tensor(name, shape, dt))
    wsl = [sbB(f"wsl{i}", [128, DC * 128], BF16) for i in range(2)]
    ev_bf = [sbB(f"ev_bf{i}", [128, QBS], BF16) for i in range(2)]
    sq_bf = [sbB(f"sq_bf{i}", [128, QBS], BF16) for i in range(2)]
    nmx = sbB("nmx", [2, 128], F32)
    vas = [sbB(f"vas{i}", [128, 4 * 130], BF16) for i in range(2)]
    nb16 = sbB("nb16", [16, 4], F32)
    ps_vtr = ps_tr[0]

    xn_st = p.dma("sync", xn_dram[:, :], xn_fm[:, :], waits=[xn_all])

    wsl_ld = [NONE_MS, NONE_MS]
    wsl_gate = NONE_MS
    nmm_pend = None
    sq_guard = [NONE_MS, NONE_MS]
    ev_tp_guard = [NONE_MS, NONE_MS]
    psq_free = [[psg_free[0]], [psg_free[1]]]
    nrm_mm_prev = NONE_MS
    nrm_red_prev = cw_ev
    vtr_prev = NONE_MS

    for s in range(24):
        buf = s % 2
        wsl_ld[buf] = p.dma("gpsimd",
                            wsl[buf][:, :].rearrange("r (dc c) -> r dc c", dc=DC),
                            wqkv_pk[s].rearrange("dc r c -> r dc c"),
                            waits=([] if s == 0 else [wsl_gate]))
        for qb in range(NQB):
            par = qb % 2
            mm = NONE_MS
            for dc in range(DC):
                w_mm = []
                if dc == 0:
                    w_mm = [wsl_ld[buf]] + psq_free[par]
                    if s == 0 and qb == 0:
                        w_mm.append(xn_all)
                mm = p.op("tensor", lambda e, buf=buf, qb=qb, dc=dc, par=par: e.matmul(
                    ps_qkv[par][:, :], wsl[buf][:, dc * 128:(dc + 1) * 128],
                    xn_fm[:, dc * TOK + qb * QBS: dc * TOK + (qb + 1) * QBS],
                    start=(dc == 0), stop=(dc == DC - 1)), waits=w_mm)
                if qb == 0 and dc == 0:
                    wsl_gate = mm
            if s < 16:
                dst = q_dram if s < 8 else k_dram
                pr = s % 8
                ev = p.op("vector", lambda e, par=par: e.tensor_copy(
                    ev_bf[par][:, :], ps_qkv[par][:, :]),
                    waits=[mm, p.drain("scalar")])
                p.dma("scalar", dst[pr][:, qb * QBS:(qb + 1) * QBS], ev_bf[par][:, :],
                      waits=[ev])
                sqb = (s * 8 + qb) % 2
                a_sq2 = p.op("scalar", lambda e, par=par, sqb=sqb: e.activation(
                    sq_bf[sqb][:, :], ps_qkv[par][:, :], AF.Square),
                    waits=[mm, ev, sq_guard[sqb]])
                psq_free[par] = [ev, a_sq2]
                if nmm_pend is not None:
                    (pa, psqb, pcol) = nmm_pend
                    nmm = p.op("tensor", lambda e, psqb=psqb: e.matmul(
                        ps_nrm[0:2, :], pairsel_s[:, :], sq_bf[psqb][:, :],
                        start=True, stop=True), waits=[pa, nrm_red_prev])
                    sq_guard[psqb] = nmm
                    nrm_red_prev = p.op("vector", lambda e, pcol=pcol: e.reduce_max(
                        nmx[0:2, pcol:pcol + 1], ps_nrm[0:2, :], axis=AX.X),
                        waits=[nmm])
                nmm_pend = (a_sq2, sqb, s * 8 + qb)
            else:
                if nmm_pend is not None:
                    (pa, psqb, pcol) = nmm_pend
                    nmm = p.op("tensor", lambda e, psqb=psqb: e.matmul(
                        ps_nrm[0:2, :], pairsel_s[:, :], sq_bf[psqb][:, :],
                        start=True, stop=True), waits=[pa, nrm_red_prev])
                    sq_guard[psqb] = nmm
                    nrm_red_prev = p.op("vector", lambda e, pcol=pcol: e.reduce_max(
                        nmx[0:2, pcol:pcol + 1], ps_nrm[0:2, :], axis=AX.X),
                        waits=[nmm])
                    nmm_pend = None
                pr = s - 16
                ev = p.op("vector", lambda e, par=par: e.tensor_copy(
                    ev_bf[par][:, :], ps_qkv[par][:, :]),
                    waits=[mm, p.drain("scalar"), ev_tp_guard[par]])
                psq_free[par] = [ev]
                vb = par
                mset = p.op("vector", lambda e, vb=vb: e.memset(
                    vas[vb][:, :].rearrange("q (kc h x) -> q kc h x", kc=4, h=2)[:, :, :, 64:65],
                    1.0), waits=[p.drain("scalar")])
                lastv = NONE_MS
                last_tp = NONE_MS
                for tt in range(4):
                    tp = p.op("tensor", lambda e, par=par, tt=tt: e.transpose(
                        ps_vtr[:, :], ev_bf[par][:, tt * 128:(tt + 1) * 128], ident_s[:, :]),
                        waits=[ev, vtr_prev])
                    vtr_prev = p.op("vector", lambda e, vb=vb, tt=tt: e.tensor_copy(
                        vas[vb][:, tt * 130:(tt + 1) * 130].rearrange(
                            "q (h x) -> q h x", h=2)[:, :, 0:64],
                        ps_vtr[:, :].rearrange("q (h x) -> q h x", h=2)),
                        waits=[tp, mset])
                    lastv = vtr_prev
                    last_tp = tp
                ev_tp_guard[par] = last_tp
                p.dma("scalar", v_dram[pr, qb * 4:(qb + 1) * 4].rearrange("kc q c -> q kc c"),
                      vas[vb][:, :].rearrange("q (kc c) -> q kc c", kc=4), waits=[lastv])

    prev_ms = nrm_red_prev
    prev_dma = NONE_MS
    for pr in range(8):
        m1 = p.op("vector", lambda e, pr=pr: e.reduce_max(
            nb16[0:2, 0:1], nmx[0:2, pr * 8:(pr + 1) * 8], axis=AX.X),
            waits=[prev_ms, prev_dma])
        m2 = p.op("vector", lambda e, pr=pr: e.reduce_max(
            nb16[0:2, 1:2], nmx[0:2, 64 + pr * 8: 64 + (pr + 1) * 8], axis=AX.X),
            waits=[m1])
        m3 = p.op("vector", lambda e: e.tensor_tensor(
            nb16[0:2, 2:3], nb16[0:2, 0:1], nb16[0:2, 1:2], ALU.mult), waits=[m2])
        aB = p.op("scalar", lambda e: e.activation(
            nb16[0:2, 3:4], nb16[0:2, 2:3], AF.Sqrt), waits=[m3])
        m4 = p.op("vector", lambda e: e.tensor_scalar(
            nb16[0:2, 3:4], nb16[0:2, 3:4], -0.125, None, ALU.mult), waits=[aB])
        prev_dma = p.dma("gpsimd", nb_dram[2 * pr:2 * pr + 2, :], nb16[0:2, 3:4],
                         waits=[m4])
        prev_ms = m4
    nb_ld = p.dma("sync", negb_bc[:, :],
                  nb_dram.rearrange("h one -> one h").to_broadcast([128, 16]),
                  waits=[prev_dma])
    p.op("vector", lambda e: e.tensor_tensor(
        negb_pair[:, :],
        negb_bc[:, :].rearrange("q (p h) -> q p h", h=2)[:, :, 0],
        negb_bc[:, :].rearrange("q (p h) -> q p h", h=2)[:, :, 1], ALU.min),
        waits=[nb_ld])

    p.barrier()
    esB.close()
    esA2.close()
    esA.close()

    esM = contextlib.ExitStack()
    sbM = lambda name, shape, dt: esM.enter_context(nc.sbuf_tensor(name, shape, dt))
    masks_s = sbM("masks_s", [128, 4 * QBS], BF16)
    b_bc = sbM("b_bc", [128, D], F32)
    esel_s = sbM("esel_s", [16, 8 * 128], BF16)
    Gs = sbM("Gs", [128, 2048], F32)
    th_lo = sbM("th_lo", [128, NEXP], F32)
    th_hi = sbM("th_hi", [128, NEXP], F32)
    th_md = sbM("th_md", [128, NEXP], F32)
    cnt_p = sbM("cnt_p", [128, NEXP], F32)
    f_scr = sbM("f_scr", [128, 256], F32)
    cnt_tot = sbM("cnt_tot", [1, NEXP], F32)
    cnt_b = sbM("cnt_b", [128, NEXP], F32)
    ge_b = sbM("ge_b", [128, NEXP], F32)
    u_s = sbM("u_s", [128, NEXP], F32)
    F_e = sbM("F_e", [128, 256], F32)
    Fp_e = sbM("Fp_e", [128, 256], F32)

    esC = contextlib.ExitStack()
    sbC = lambda name, shape, dt: esC.enter_context(nc.sbuf_tensor(name, shape, dt))
    qs = [sbC(f"qs{i}", [128, QBS], BF16) for i in range(2)]
    ks = [sbC(f"ks{i}", [128, TOK], BF16) for i in range(2)]
    vs = [sbC(f"vs{i}", [128, NKC * 130], BF16) for i in range(2)]
    pt = [sbC(f"pt{i}", [128, 1024], BF16) for i in range(3)]
    attn_raw = sbC("attn_raw", [128, 8 * QBS], BF16)
    attn_nrm = sbC("attn_nrm", [128, 8 * QBS], BF16)
    dmat = sbC("dmat", [16, QBS], F32)
    dtmp = [sbC(f"dtmp{i}", [1, QBS], F32) for i in range(2)]
    rbf = sbC("rbf", [16, QBS], BF16)
    rexp_bf = sbC("rexp_bf", [128, QBS], BF16)
    xres = [sbC(f"xres{i}", [128, D], F32) for i in range(2)]
    out32 = [sbC(f"out32_{i}", [128, D], F32) for i in range(2)]
    wout_bf = sbC("wout_bf", [128, DC * D], BF16)
    ps_s = [esC.enter_context(nc.psum_tensor(f"ps_s{i}", [128, 1024], F32)) for i in range(2)]
    ps_o = [esC.enter_context(nc.psum_tensor(f"ps_o{i}", [128, QBS], F32)) for i in range(2)]
    ps_op = [esC.enter_context(nc.psum_tensor(f"ps_op{i}", [128, QBS], F32)) for i in range(2)]

    bld = p.dma("sync", b_bc[:, :], b_out[0:1, :].to_broadcast([128, D]))
    masks_ld = p.dma("sync", masks_s[:, :], masks_c[:, :])
    esel_ld = p.dma("sync", esel_s[:, :], esel_c[:, :])
    wout_ld = NONE_MS
    for c in range(DC):
        wout_ld = p.dma("gpsimd", wout_bf[:, c * D:(c + 1) * D],
                        w_out[c * 128:(c + 1) * 128, :])
    gs_ld = p.dma("gpsimd", Gs[:, :].rearrange("q (r c) -> q r c", r=NCORES),
                  cout_g.rearrange("(r q) c -> q r c", q=128),
                  waits=[cc1, wout_ld])
    gm0 = p.op("vector", lambda e: e.memset(th_lo[:, :], 0.0))
    gm1 = p.op("vector", lambda e: e.memset(th_hi[:, :], 1.0))
    gm2 = p.op("vector", lambda e: e.memset(th_md[:, :], 0.5))

    def gs_e(e_):
        return Gs[:, :].rearrange("q (r e t) -> q r e t", r=NCORES, e=NEXP)[:, :, e_, :]

    bis = {"prev": gm2, "bcast": gs_ld, "d3": NONE_MS}
    bis_steps = []

    def _mk_count(rnd):
        def fn():
            pr = bis["prev"]
            for e_ in range(NEXP):
                pr = p.op("vector", lambda en, e_=e_: en.tensor_scalar(
                    f_scr[:, :].rearrange("q (r t) -> q r t", r=NCORES),
                    gs_e(e_), th_md[:, e_:e_ + 1], None, ALU.is_ge,
                    ALU.add, accum_out=cnt_p[:, e_:e_ + 1]),
                    waits=([bis["prev"], bis["bcast"], gs_ld, gm0, gm1]
                           if e_ == 0 else []))
            rc = p.op("gpsimd", lambda en: en.reduce_sum(
                cnt_tot[0:1, :], cnt_p[:, :], axis=AX.C),
                waits=[pr, bis["d3"]])
            d3 = p.dma("gpsimd", ctot_dram[0:1, :], cnt_tot[0:1, :], waits=[rc])
            bis["d3"] = d3
            bis["bcast"] = p.dma("gpsimd", cnt_b[:, :],
                                 ctot_dram[0:1, :].to_broadcast([128, NEXP]),
                                 waits=[d3])
            bis["prev"] = pr
        return fn

    def _mk_update(rnd):
        def fn():
            g1 = p.op("vector", lambda en: en.tensor_scalar(
                ge_b[:, :], cnt_b[:, :], float(CAP), None, ALU.is_ge),
                waits=[bis["bcast"], bis["prev"]])
            g2_ = p.op("vector", lambda en: en.tensor_tensor(
                u_s[:, :], th_md[:, :], th_lo[:, :], ALU.subtract), waits=[g1])
            g3 = p.op("vector", lambda en: en.tensor_tensor(
                u_s[:, :], ge_b[:, :], u_s[:, :], ALU.mult), waits=[g2_])
            g4 = p.op("vector", lambda en: en.tensor_tensor(
                th_lo[:, :], th_lo[:, :], u_s[:, :], ALU.add), waits=[g3])
            g5 = p.op("vector", lambda en: en.tensor_tensor(
                u_s[:, :], th_hi[:, :], th_md[:, :], ALU.subtract), waits=[g4])
            g6 = p.op("vector", lambda en: en.tensor_tensor(
                u_s[:, :], ge_b[:, :], u_s[:, :], ALU.mult), waits=[g5])
            g7 = p.op("vector", lambda en: en.tensor_tensor(
                th_hi[:, :], th_md[:, :], u_s[:, :], ALU.add), waits=[g6])
            g8 = p.op("vector", lambda en: en.tensor_tensor(
                u_s[:, :], th_lo[:, :], th_hi[:, :], ALU.add), waits=[g7])
            bis["prev"] = p.op("vector", lambda en: en.tensor_scalar(
                th_md[:, :], u_s[:, :], 0.5, None, ALU.mult), waits=[g8])
        return fn

    for rnd in range(BIS_ROUNDS):
        bis_steps.append(_mk_count(rnd))
        bis_steps.append(_mk_update(rnd))


    strip_free = [NONE_MS, NONE_MS]
    pt_free = [NONE_MS, NONE_MS, NONE_MS]
    s_free = [NONE_MS, NONE_MS]
    o_free = [NONE_MS, NONE_MS]
    dmat_free = NONE_MS
    op_free = [NONE_MS, NONE_MS]

    lq_gate = NONE_MS
    for qb in range(NQB):
        nkc = 4 * (qb + 1)
        for pr in range(NPAIR):
            sp = pr % 2
            lq = p.dma("sync", qs[sp][:, :], q_dram[pr][:, qb * QBS:(qb + 1) * QBS],
                       waits=[strip_free[sp], lq_gate])
            lk = p.dma("sync", ks[sp][:, 0:nkc * 128], k_dram[pr][:, 0:nkc * 128])
            lv = p.dma("sync",
                       vs[sp][:, 0:nkc * 130].rearrange("q (kc c) -> q kc c", kc=nkc),
                       v_dram[pr, 0:nkc].rearrange("kc q c -> q kc c"))
            last_av = NONE_MS
            av_pend = []
            for kc in range(nkc):
                par = kc % 2
                pp = kc % 3
                w_s = [s_free[par]]
                if kc == 0:
                    w_s += [p.drain("sync")]
                mmA = p.op("tensor", lambda e, sp=sp, kc=kc, par=par: e.matmul(
                    ps_s[par][:, 0:512], ks[sp][0:64, kc * 128:(kc + 1) * 128],
                    qs[sp][0:64, :], start=True, stop=True, tile_position=(0, 0)),
                    waits=w_s)
                if kc == 0:
                    lq_gate = mmA
                mmB = p.op("tensor", lambda e, sp=sp, kc=kc, par=par: e.matmul(
                    ps_s[par][:, 512:1024], ks[sp][64:128, kc * 128:(kc + 1) * 128],
                    qs[sp][64:128, :], start=True, stop=True, tile_position=(64, 0)))
                eB = p.op("scalar", lambda e, pp=pp, par=par, pr=pr: e.activation(
                    pt[pp][:, 0:1024], ps_s[par][:, 0:1024], AF.Exp,
                    bias=negb_pair[:, pr:pr + 1], scale=0.125),
                    waits=[mmB, pt_free[pp]])
                s_free[par] = eB
                w_av = [eB]
                if kc >= 4 * qb:
                    m = kc - 4 * qb
                    p.op("vector", lambda e, pp=pp, m=m: e.tensor_tensor(
                        pt[pp][:, 0:512], pt[pp][:, 0:512],
                        masks_s[:, m * QBS:(m + 1) * QBS], ALU.mult),
                        waits=[eB])
                    mk2 = p.op("vector", lambda e, pp=pp, m=m: e.tensor_tensor(
                        pt[pp][:, 512:1024], pt[pp][:, 512:1024],
                        masks_s[:, m * QBS:(m + 1) * QBS], ALU.mult))
                    w_av = [mk2]
                if kc == 0:
                    w_av = w_av + [o_free[0], o_free[1]]
                av_pend.append((kc, pp, w_av))
                for (kcp, ppp, w_avp) in av_pend if kc == nkc - 1 else av_pend[:-2]:
                    avA = p.op("tensor", lambda e, sp=sp, kc=kcp, nkc=nkc, pp=ppp: e.matmul(
                        ps_o[0][0:65, :], vs[sp][:, kc * 130: kc * 130 + 65],
                        pt[pp][:, 0:512], start=(kc == 0), stop=(kc == nkc - 1)),
                        waits=w_avp)
                    avB = p.op("tensor", lambda e, sp=sp, kc=kcp, nkc=nkc, pp=ppp: e.matmul(
                        ps_o[1][0:65, :], vs[sp][:, kc * 130 + 65: kc * 130 + 130],
                        pt[pp][:, 512:1024], start=(kc == 0), stop=(kc == nkc - 1)))
                    pt_free[ppp] = avB
                    last_av = avB
                av_pend = [] if kc == nkc - 1 else av_pend[-2:]
            strip_free[sp] = last_av
            ev1 = p.op("vector", lambda e, pr=pr: e.tensor_copy(
                attn_raw[0:64, pr * QBS:(pr + 1) * QBS], ps_o[0][0:64, :]),
                waits=[last_av])
            ev2 = p.op("vector", lambda e, pr=pr: e.tensor_copy(
                attn_raw[64:128, pr * QBS:(pr + 1) * QBS], ps_o[1][0:64, :]))
            d1 = p.op("vector", lambda e: e.tensor_copy(
                dtmp[0][0:1, :], ps_o[0][64:65, :]), waits=[p.drain("scalar")])
            d2 = p.op("vector", lambda e: e.tensor_copy(
                dtmp[1][0:1, :], ps_o[1][64:65, :]))
            o_free[0] = d1
            o_free[1] = d2
            p.dma("scalar", dmat[2 * pr:2 * pr + 1, :], dtmp[0][0:1, :],
                  waits=[d1, dmat_free])
            p.dma("scalar", dmat[2 * pr + 1:2 * pr + 2, :], dtmp[1][0:1, :],
                  waits=[d2])
            if bis_steps:
                bis_steps.pop(0)()
        vR = p.op("vector", lambda e: e.reciprocal(dmat[0:16, :], dmat[0:16, :]),
                  waits=[p.drain("scalar")])
        vRb = p.op("vector", lambda e: e.tensor_copy(rbf[0:16, :], dmat[0:16, :]),
                   waits=[vR])
        dmat_free = vRb
        nrm_done = NONE_MS
        for c in range(DC):
            mmr = p.op("tensor", lambda e, c=c: e.matmul(
                ps_op[0][:, :], esel_s[:, c * 128:(c + 1) * 128], rbf[0:16, :],
                start=True, stop=True), waits=[vRb, op_free[0]])
            rev = p.op("vector", lambda e: e.tensor_copy(rexp_bf[:, :], ps_op[0][:, :]),
                       waits=[mmr])
            nrm_done = p.op("vector", lambda e, c=c: e.tensor_tensor(
                attn_nrm[:, c * QBS:(c + 1) * QBS], attn_raw[:, c * QBS:(c + 1) * QBS],
                rexp_bf[:, :], ALU.mult), waits=[rev])
            op_free[0] = nrm_done
        xres_gate = vR
        for tt in range(4):
            ti = qb * 4 + tt
            ob = tt % 2
            xld = p.dma("scalar", xres[ob][:, :], x_own[ti * 128:(ti + 1) * 128, :],
                        waits=[xres_gate])
            mm_h = [NONE_MS, NONE_MS]
            for half in range(2):
                for c in range(DC):
                    w_op = []
                    if c == 0:
                        w_op = [nrm_done, op_free[half]]
                        if qb == 0 and tt == 0:
                            w_op.append(wout_ld)
                    mm_h[half] = p.op("tensor", lambda e, tt=tt, c=c, half=half: e.matmul(
                        ps_op[half][:, :],
                        attn_nrm[:, c * QBS + tt * 128: c * QBS + (tt + 1) * 128],
                        wout_bf[:, c * D + half * 512: c * D + (half + 1) * 512],
                        start=(c == 0), stop=(c == DC - 1)), waits=w_op)
            va = p.op("vector", lambda e, tt=tt, ob=ob: e.tensor_tensor(
                out32[ob][:, 0:512], ps_op[0][:, :], xres[ob][:, 0:512], ALU.add),
                waits=[mm_h[0], p.drain("scalar"), bld])
            xres_gate = va
            vb2 = p.op("vector", lambda e, tt=tt, ob=ob: e.tensor_tensor(
                out32[ob][:, 512:1024], ps_op[1][:, :], xres[ob][:, 512:1024], ALU.add),
                waits=[mm_h[1]])
            vc1 = p.op("vector", lambda e, ob=ob: e.tensor_tensor(
                out32[ob][:, 0:512], out32[ob][:, 0:512], b_bc[:, 0:512], ALU.add),
                waits=[va])
            vc2 = p.op("vector", lambda e, ob=ob: e.tensor_tensor(
                out32[ob][:, 512:1024], out32[ob][:, 512:1024], b_bc[:, 512:1024],
                ALU.add), waits=[vb2])
            op_free[0] = vc1
            op_free[1] = vc2
            p.dma("scalar", out_t[ti * 128:(ti + 1) * 128, :],
                  out32[ob][:, :], waits=[vc2])

    while bis_steps:
        bis_steps.pop(0)()
    wprev = bis["prev"]
    for e_ in range(NEXP):
        s1 = p.op("vector", lambda en, e_=e_: en.tensor_scalar(
            F_e[:, 0:32], G_own[:, e_ * 32:(e_ + 1) * 32], th_lo[:, e_:e_ + 1],
            None, ALU.is_ge), waits=[wprev])
        s2 = p.op("vector", lambda en, e_=e_: en.tensor_scalar(
            Fp_e[:, 0:32], G_own[:, e_ * 32:(e_ + 1) * 32], 0.0, None, ALU.is_gt),
            waits=[s1])
        s3 = p.op("vector", lambda en: en.tensor_tensor(
            F_e[:, 0:32], F_e[:, 0:32], Fp_e[:, 0:32], ALU.mult), waits=[s2])
        wprev = p.op("vector", lambda en, e_=e_: en.tensor_tensor(
            wsel[:, e_ * 32:(e_ + 1) * 32], G_own[:, e_ * 32:(e_ + 1) * 32],
            F_e[:, 0:32], ALU.mult), waits=[s3])
    wsel_done = wprev

    p.barrier()
    esC.close()
    esM.close()

    esDR = contextlib.ExitStack()
    sbDR = lambda name, shape, dt: esDR.enter_context(nc.sbuf_tensor(name, shape, dt))
    slotidx_sb = sbDR("slotidx_sb", [128, 256], F32)
    gc_sb = sbDR("gc_sb", [128, 128], F32)
    iota2b = sbDR("iota2b", [128, SLOTB], F32)
    ind_all = sbDR("ind_all", [128, 256], F32)

    esD0 = contextlib.ExitStack()
    sbD0 = lambda name, shape, dt: esD0.enter_context(nc.sbuf_tensor(name, shape, dt))
    xn_tm_sb = sbD0("xn_tm_sb", [128, 4 * D], BF16)
    tri_s = sbD0("tri_s", [128, 128], F32)
    onescol_d = sbD0("onescol_d", [128, 1], F32)
    ones32 = sbD0("ones32", [128, 32], F32)
    base_sb = sbD0("base_sb", [128, 64], F32)

    tri_ld = p.dma("sync", tri_s[:, :], tri_c[:, :])
    iota_ld = p.dma("sync", iota2b[:, :],
                    iota2_c[0:1, :].to_broadcast([128, SLOTB]))
    on_ms = p.op("vector", lambda e: e.memset(ones32[:, :], 1.0))
    on1_ms = p.op("vector", lambda e: e.memset(onescol_d[:, :], 1.0))

    esD0i = contextlib.ExitStack()
    sbD0i = lambda name, shape, dt: esD0i.enter_context(nc.sbuf_tensor(name, shape, dt))
    scn_all = sbD0i("scn_all", [128, 256], F32)
    segb = sbD0i("segb", [128, 256], F32)
    rc_all = sbD0i("rc_all", [128, 64], F32)
    esD0p = contextlib.ExitStack()
    ps_xc = [esD0p.enter_context(nc.psum_tensor(f"ps_xc{i}", [128, 512], F32))
             for i in range(2)]
    ps_bs = esD0p.enter_context(nc.psum_tensor("ps_bs", [128, 512], F32))
    ps_gc = [esD0p.enter_context(nc.psum_tensor(f"ps_gc{i}", [128, 512], F32))
             for i in range(2)]

    ia0 = p.op("vector", lambda e: e.tensor_scalar(
        ind_all[:, :], wsel[:, :], 0.0, None, ALU.is_gt), waits=[wsel_done])
    pv = p.op("vector", lambda e: e.memset(segb[:, :], 0.0), waits=[ia0])
    for e_ in range(NEXP):
        pv = p.op("vector", lambda e, e_=e_: e.tensor_tensor_scan(
            scn_all[:, e_ * 32:(e_ + 1) * 32], ind_all[:, e_ * 32:(e_ + 1) * 32],
            ones32[:, :], 0.0, ALU.add, ALU.mult),
            waits=([pv, on_ms] if e_ == 0 else [pv]))
    for e_ in range(NEXP):
        sv = scn_all[:, e_ * 32:(e_ + 1) * 32].rearrange("q (b f) -> q b f", f=4)
        gv = segb[:, e_ * 32:(e_ + 1) * 32].rearrange("q (b f) -> q b f", f=4)
        for j in range(4):
            pv = p.op("vector", lambda e, sv=sv, gv=gv, j=j: e.tensor_copy(
                gv[:, 1:8, j], sv[:, 0:7, 3]), waits=[pv])
        pv = p.op("vector", lambda e, e_=e_, sv=sv, gv=gv: e.tensor_tensor(
            rc_all[:, e_ * 8:(e_ + 1) * 8], sv[:, :, 3], gv[:, :, 0],
            ALU.subtract), waits=[pv])
    bmm0 = p.op("tensor", lambda e: e.matmul(
        ps_bs[:, 0:64], tri_s[:, :], rc_all[:, 0:64], start=True, stop=True),
        waits=[pv, iota_ld])
    bev0 = p.op("vector", lambda e: e.tensor_copy(
        base_sb[:, :], ps_bs[:, 0:64]), waits=[bmm0])
    pv = p.op("vector", lambda e: e.tensor_tensor(
        slotidx_sb[:, :], scn_all[:, :], ind_all[:, :], ALU.subtract),
        waits=[bev0])
    pv = p.op("vector", lambda e: e.tensor_tensor(
        slotidx_sb[:, :], slotidx_sb[:, :], segb[:, :], ALU.subtract),
        waits=[pv])
    for e_ in range(NEXP):
        for blk in range(NQB):
            c0 = e_ * 32 + blk * 4
            pv = p.op("vector", lambda e, e_=e_, blk=blk, c0=c0: e.tensor_scalar(
                slotidx_sb[:, c0:c0 + 4], slotidx_sb[:, c0:c0 + 4],
                base_sb[:, e_ * 8 + blk:e_ * 8 + blk + 1], None, ALU.add),
                waits=[pv])
    slfin = pv
    esD0i.close()

    selT_sb = [sbD0(f"selT_sb{i}", [128, 4 * SLOTB], BF16) for i in range(2)]
    gpk2 = [sbD0(f"gpk2{i}", [128, 8], BF16) for i in range(2)]
    ghi_f = sbD0("ghi_f", [128, 4], F32)
    glo_f = sbD0("glo_f", [128, 4], F32)
    xc_ev = [sbD0(f"xc_ev{i}", [128, SLOTB], BF16) for i in range(2)]

    xn_free = iota_ld
    psxc_free = [NONE_MS, NONE_MS]
    gc_free = [bev0, NONE_MS]
    ghi_free = [NONE_MS, NONE_MS]
    selT_free = [NONE_MS, NONE_MS]
    xcev_free = [NONE_MS, NONE_MS]

    def emit_selT(e_, blk):
        eb = e_ % 2
        last = NONE_MS
        for tc in range(4):
            for st in range(2):
                c0 = e_ * 32 + blk * 4
                last = p.op("vector", lambda e, tc=tc, st=st, c0=c0, eb=eb: e.tensor_scalar(
                    selT_sb[eb][:, tc * SLOTB + st * 128: tc * SLOTB + (st + 1) * 128],
                    iota2b[:, st * 128:(st + 1) * 128],
                    slotidx_sb[:, c0 + tc:c0 + tc + 1], ind_all[:, c0 + tc:c0 + tc + 1],
                    ALU.is_equal, ALU.mult),
                    waits=([slfin, selT_free[eb]] if tc == 0 and st == 0 else []))
        return last

    for blk in range(NQB):
        xn_ld = p.dma("sync", xn_tm_sb[:, :].rearrange("q (tc f) -> q tc f", tc=4),
                      xn_tm_dram[blk * 512:(blk + 1) * 512, :].rearrange(
                          "(tc q) f -> q tc f", q=128),
                      waits=[xn_free])
        bld_last = emit_selT(0, blk)
        gc_pend = None

        def emit_gc(pend):
            (pe_, peb, pg4) = pend
            lg = NONE_MS
            for st in range(2):
                gp = st % 2
                gmm = NONE_MS
                for tc in range(4):
                    gmm = p.op("tensor", lambda e, st=st, tc=tc, peb=peb, gp=gp: e.matmul(
                        ps_gc[gp][:, 0:2],
                        selT_sb[peb][:, tc * SLOTB + st * 128: tc * SLOTB + (st + 1) * 128],
                        gpk2[peb][:, tc * 2:(tc + 1) * 2], start=(tc == 0), stop=(tc == 3)),
                        waits=([pg4, gc_free[gp]] if tc == 0 else []))
                lg = gmm
                gcol = gc_sb[:, pe_ * 16 + blk * 2 + st: pe_ * 16 + blk * 2 + st + 1]
                gv1 = p.op("vector", lambda e, gcol=gcol, gp=gp: e.tensor_copy(
                    gcol, ps_gc[gp][:, 0:1]), waits=[gmm])
                gev = p.op("vector", lambda e, gcol=gcol, gp=gp: e.tensor_tensor(
                    gcol, gcol, ps_gc[gp][:, 1:2], ALU.add), waits=[gv1])
                gc_free[gp] = gev
            return lg

        for e_ in range(NEXP):
            eb = e_ % 2
            c0 = e_ * 32 + blk * 4
            mm_xc = NONE_MS
            for featc in range(DC):
                fb = featc % 2
                for tc in range(4):
                    w_m = []
                    if tc == 0:
                        w_m = [bld_last, psxc_free[fb]]
                        if featc == 0:
                            w_m.append(xn_ld)
                    mm_xc = p.op("tensor", lambda e, featc=featc, tc=tc, fb=fb, eb=eb: e.matmul(
                        ps_xc[fb][:, 0:SLOTB],
                        xn_tm_sb[:, tc * D + featc * 128: tc * D + (featc + 1) * 128],
                        selT_sb[eb][:, tc * SLOTB:(tc + 1) * SLOTB],
                        start=(tc == 0), stop=(tc == 3)), waits=w_m)
                xce = p.op("vector", lambda e, fb=fb: e.tensor_copy(
                    xc_ev[fb][:, :], ps_xc[fb][:, 0:SLOTB]),
                    waits=[mm_xc, xcev_free[fb], p.drain("scalar")])
                psxc_free[fb] = xce
                xcev_free[fb] = p.dma(
                    "scalar",
                    xc_dram[e_][:, featc * SLOT_E + blk * SLOTB:
                                featc * SLOT_E + (blk + 1) * SLOTB],
                    xc_ev[fb][:, :], waits=[xce])
            if gc_pend is not None:
                lgm = emit_gc(gc_pend)
                selT_free[gc_pend[1]] = lgm
                ghi_free[gc_pend[1]] = lgm
            else:
                selT_free[eb] = mm_xc
            if e_ + 1 < NEXP:
                bld_last = emit_selT(e_ + 1, blk)
            gview = gpk2[eb][:, :].rearrange("q (tc two) -> q tc two", two=2)
            g1_ = p.op("vector", lambda e, c0=c0, gview=gview: e.tensor_copy(
                gview[:, :, 0], wsel[:, c0:c0 + 4]), waits=[ghi_free[eb]])
            g2_ = p.op("vector", lambda e, gview=gview: e.tensor_copy(
                ghi_f[:, :], gview[:, :, 0]), waits=[g1_])
            g3_ = p.op("vector", lambda e, c0=c0: e.tensor_tensor(
                glo_f[:, :], wsel[:, c0:c0 + 4], ghi_f[:, :], ALU.subtract),
                waits=[g2_])
            g4_ = p.op("vector", lambda e, gview=gview: e.tensor_copy(
                gview[:, :, 1], glo_f[:, :]), waits=[g3_])
            gc_pend = (e_, eb, g4_)
        lgm = emit_gc(gc_pend)
        selT_free[gc_pend[1]] = lgm
        ghi_free[gc_pend[1]] = lgm
        xn_free = (PE, PE.v)
    gc_done = gc_free[1]
    xc_done = xcev_free[1]
    d0_gate = (PE, PE.v)
    esD0p.close()

    esD1 = contextlib.ExitStack()
    sbD1 = lambda name, shape, dt: esD1.enter_context(nc.sbuf_tensor(name, shape, dt))
    w1_sb = sbD1("w1_sb", [128, FC * 1024], BF16)
    w2_sb = sbD1("w2_sb", [128, FC * D], BF16)
    xnb = [sbD1(f"xnb{i}", [128, DC * 512], BF16) for i in range(2)]
    h_sb = sbD1("h_sb", [128, FC * 512], BF16)
    yst = [sbD1(f"yst{i}", [128, D], F32) for i in range(2)]
    ps_h = [esD1.enter_context(nc.psum_tensor(f"ps_h{i}", [128, 512], F32)) for i in range(4)]
    ps_y = [esD1.enter_context(nc.psum_tensor(f"ps_y{i}", [128, 512], F32)) for i in range(2)]

    w1_free = NONE_MS
    w2_free = NONE_MS
    xnb_free = [NONE_MS, NONE_MS]
    psy_free = [NONE_MS, NONE_MS]
    bank_guard = [NONE_MS] * 4
    mmh_gate = d0_gate
    y2_gate = NONE_MS
    w1_next_free = NONE_MS

    for e_ in range(NEXP):
        w1_lds = []
        for fc in range(FC):
            w1_lds.append(p.dma(
                "gpsimd",
                w1_sb[:, fc * 1024:(fc + 1) * 1024].rearrange("r (dc c) -> r dc c", dc=DC),
                w1_pk[e_, fc].rearrange("dc r c -> r dc c"),
                waits=([w1_free, y2_gate] if fc == 0 else [])))
        w2_lds = []
        for sb_ in range(NSB):
            xld = p.dma("sync",
                        xnb[sb_ % 2][:, :].rearrange("r (dc c) -> r dc c", dc=DC),
                        xc_dram[e_].rearrange("r (dc s) -> r dc s", dc=DC)[
                            :, :, sb_ * 512:(sb_ + 1) * 512],
                        waits=([xnb_free[sb_ % 2], mmh_gate]
                               + ([xc_done, gc_done] if e_ == 0 and sb_ == 0 else [])))
            relu_ms = {}
            for fc in range(FC):
                mmh = NONE_MS
                for dc in range(DC):
                    w_m = []
                    if dc == 0:
                        w_m = [xld, bank_guard[fc % 4]]
                        if sb_ == 0 and fc == 0:
                            w_m.append(w1_lds[FC - 1])
                    mmh = p.op("tensor", lambda e, fc=fc, dc=dc, sb_=sb_: e.matmul(
                        ps_h[fc % 4][:, :],
                        w1_sb[:, fc * 1024 + dc * 128: fc * 1024 + (dc + 1) * 128],
                        xnb[sb_ % 2][:, dc * 512:(dc + 1) * 512],
                        start=(dc == 0), stop=(dc == DC - 1)), waits=w_m)
                    if fc == 0 and dc == 0:
                        mmh_gate = mmh
                        if sb_ == 0:
                            for fc2 in range(FC):
                                w2_lds.append(p.dma(
                                    "gpsimd", w2_sb[:, fc2 * D:(fc2 + 1) * D],
                                    w2_in[e_, fc2 * 128:(fc2 + 1) * 128, :],
                                    waits=([w2_free, mmh] if fc2 == 0 else [])))
                relu_ms[fc] = p.op("scalar", lambda e, fc=fc: e.activation(
                    h_sb[:, fc * 512:(fc + 1) * 512], ps_h[fc % 4][:, :], AF.Relu),
                    waits=[mmh])
                bank_guard[fc % 4] = relu_ms[fc]
            xnb_free[sb_ % 2] = (PE, PE.v)
            if sb_ == NSB - 1:
                w1_next_free = (PE, PE.v)
            for tt in range(4):
                ti = sb_ * 4 + tt
                mmy = [NONE_MS, NONE_MS]
                for half in range(2):
                    for fc in range(FC):
                        w_m = [relu_ms[fc]]
                        if fc == 0:
                            w_m.append(psy_free[half])
                            if sb_ == 0 and tt == 0:
                                w_m.append(w2_lds[FC - 1])
                        mmy[half] = p.op("tensor", lambda e, fc=fc, half=half, tt=tt: e.matmul(
                            ps_y[half][:, :],
                            h_sb[:, fc * 512 + tt * 128: fc * 512 + (tt + 1) * 128],
                            w2_sb[:, fc * D + half * 512: fc * D + (half + 1) * 512],
                            start=(fc == 0), stop=(fc == FC - 1)), waits=w_m)
                y1 = p.op("vector", lambda e, ti=ti, e_=e_: e.tensor_scalar(
                    yst[ti % 2][:, 0:512], ps_y[0][:, :],
                    gc_sb[:, e_ * 16 + ti: e_ * 16 + ti + 1], None, ALU.mult),
                    waits=[mmy[0], p.drain("gpsimd"), gc_done])
                y2 = p.op("vector", lambda e, ti=ti, e_=e_: e.tensor_scalar(
                    yst[ti % 2][:, 512:1024], ps_y[1][:, :],
                    gc_sb[:, e_ * 16 + ti: e_ * 16 + ti + 1], None, ALU.mult),
                    waits=[mmy[1]])
                psy_free[0] = y1
                psy_free[1] = y2
                if ti == NST_E - 3:
                    y2_gate = y2
                p.dma("gpsimd", y_dram[e_, ti], yst[ti % 2][:, :],
                      waits=[y1, y2])
        w1_free = w1_next_free
        w2_free = (PE, PE.v)

    p.barrier()
    esD1.close()
    esD0.close()

    esD2 = contextlib.ExitStack()
    sbD2 = lambda name, shape, dt: esD2.enter_context(nc.sbuf_tensor(name, shape, dt))
    y_sb = [sbD2(f"y_sb{i}", [128, 16 * D], BF16) for i in range(2)]
    selT_d = [sbD2(f"selT_d{i}", [128, 128], BF16) for i in range(2)]
    sel_ev = [sbD2(f"sel_ev{i}", [128, 128], BF16) for i in range(2)]
    oadd = [sbD2(f"oadd{i}", [128, D], F32) for i in range(2)]
    ps_sc = [esD2.enter_context(nc.psum_tensor(f"ps_sc{i}", [128, 512], F32))
             for i in range(4)]
    ps_tT = [esD2.enter_context(nc.psum_tensor(f"ps_tT{i}", [128, 1024], BF16))
             for i in range(2)]

    ysb_free = [NONE_MS, NONE_MS]
    y_gate = NONE_MS
    bld_free = [NONE_MS, NONE_MS]
    sev_free = [NONE_MS, NONE_MS]
    psc_free = [NONE_MS] * 4
    oadd_free = [NONE_MS, NONE_MS]
    for blk in range(NQB):
        bb = blk % 2
        y_lds = NONE_MS
        for e_ in range(NEXP):
            y_lds = p.dma(
                "sync",
                y_sb[bb][:, e_ * 2 * D:(e_ + 1) * 2 * D].rearrange(
                    "q (st f) -> q st f", st=2),
                y_dram[e_, blk * 2:(blk + 1) * 2].rearrange("st q f -> q st f"),
                waits=([ysb_free[bb], y_gate] if e_ == 0 else []))
        for tc in range(4):
            mm_sc = NONE_MS
            pend = None
            for idx in range(17):
                if idx < 16:
                    e_, st = idx // 2, idx % 2
                    ib = idx % 2
                    c0 = e_ * 32 + blk * 4 + tc
                    bld = p.op("vector", lambda e, st=st, c0=c0, ib=ib: e.tensor_scalar(
                        selT_d[ib][:, :], iota2b[:, st * 128:(st + 1) * 128],
                        slotidx_sb[:, c0:c0 + 1], ind_all[:, c0:c0 + 1],
                        ALU.is_equal, ALU.mult),
                        waits=[bld_free[ib]])
                    tp = p.op("tensor", lambda e, ib=ib: e.transpose(
                        ps_tT[ib][:, 0:128], selT_d[ib][:, :], ident_s[:, :]),
                        waits=[bld, sev_free[ib]])
                    bld_free[ib] = tp
                    sev = p.op("vector", lambda e, ib=ib: e.tensor_copy(
                        sel_ev[ib][:, :], ps_tT[ib][:, 0:128]), waits=[tp])
                if pend is not None:
                    (pidx, pib, pe_, pst, psev) = pend
                    tb = tc % 2
                    for half in range(2):
                        w_m = [psev]
                        if pidx == 0:
                            w_m += [psc_free[2 * tb + half], y_lds, p.drain("sync")]
                        mm_sc = p.op("tensor", lambda e, pidx=pidx, half=half, pib=pib,
                                     bb=bb, pe_=pe_, pst=pst, tb=tb: e.matmul(
                            ps_sc[2 * tb + half][:, :], sel_ev[pib][:, :],
                            y_sb[bb][:, (pe_ * 2 + pst) * D + half * 512:
                                      (pe_ * 2 + pst) * D + (half + 1) * 512],
                            start=(pidx == 0), stop=(pidx == 15)), waits=w_m)
                    sev_free[pib] = mm_sc
                    if tc == 0 and pidx == 0:
                        y_gate = mm_sc
                if idx < 16:
                    pend = (idx, ib, e_, st, sev)
            ob = tc % 2
            tb = tc % 2
            o1 = p.op("vector", lambda e, ob=ob, tb=tb: e.tensor_copy(
                oadd[ob][:, 0:512], ps_sc[2 * tb][:, :]),
                waits=[mm_sc, oadd_free[ob]])
            o2 = p.op("vector", lambda e, ob=ob, tb=tb: e.tensor_copy(
                oadd[ob][:, 512:1024], ps_sc[2 * tb + 1][:, :]))
            psc_free[2 * tb] = o1
            psc_free[2 * tb + 1] = o2
            ti = blk * 4 + tc
            acc = p.dma("gpsimd", out_t[ti * 128:(ti + 1) * 128, :],
                        oadd[ob][:, :], accum_op=ALU.add,
                        waits=[o1, o2, p.drain("gpsimd")])
            oadd_free[ob] = acc
        ysb_free[bb] = (PE, PE.v)

    p.wt("gpsimd", (p.d["gpsimd"], p.d["gpsimd"].v))
    p.wt("sync", (DS, DS.v))
    p.wt("scalar", (DSC, DSC.v))
    esD2.close()
    esDR.close()

    es0.close()

    with nc.Block() as block:
        p.emit(block)
    return nc


_NC_CACHE = None


def kernel(L0_attn_x, L0_attn_w_qkv, L0_attn_w_out, L0_attn_b_out,
           L0_ffn_Wr, L0_ffn_W1, L0_ffn_W2):
    global _NC_CACHE
    X = np.ascontiguousarray(np.asarray(L0_attn_x, dtype=np.float32))
    Wqkv = np.ascontiguousarray(np.asarray(L0_attn_w_qkv, dtype=np.float32))
    Wout = np.ascontiguousarray(np.asarray(L0_attn_w_out, dtype=np.float32))
    Bout = np.ascontiguousarray(np.asarray(L0_attn_b_out, dtype=np.float32)).reshape(1, D)
    Wr = np.ascontiguousarray(np.asarray(L0_ffn_Wr, dtype=np.float32))
    W1 = np.asarray(L0_ffn_W1, dtype=np.float32)
    W2 = np.ascontiguousarray(np.asarray(L0_ffn_W2, dtype=np.float32))
    wqkv_pk = np.ascontiguousarray(Wqkv.reshape(DC, 128, 24, 128).transpose(2, 0, 1, 3))
    w1_pk = np.ascontiguousarray(
        W1.reshape(NEXP, DC, 128, FC, 128).transpose(0, 3, 1, 2, 4))
    if _NC_CACHE is None:
        _NC_CACHE = build_nc()
    nc = _NC_CACHE
    in_maps = []
    for c in range(NCORES):
        xc = np.ascontiguousarray(X[c])
        in_maps.append({
            "x_own": xc,
            "x_fm": np.ascontiguousarray(xc.T),
            "wqkv_pk": wqkv_pk,
            "w_out": Wout,
            "b_out": Bout,
            "wr_in": Wr,
            "w1_pk": w1_pk,
            "w2_in": W2,
        })
    res = run_bass_kernel_spmd(nc, in_maps, core_ids=list(range(NCORES)))
    out = np.stack([np.asarray(res.results[c]["out"]) for c in range(NCORES)], axis=0)
    return out


# version 07
# speedup vs baseline: 1.1323x; 1.1323x over previous
import contextlib
import numpy as np
import ml_dtypes

import concourse.bass as bass
import concourse.mybir as mybir
from concourse.bass_utils import run_bass_kernel_spmd

F32 = mybir.dt.float32
BF16 = mybir.dt.bfloat16
I32 = mybir.dt.int32
AF = mybir.ActivationFunctionType
ALU = mybir.AluOpType
AX = mybir.AxisListType

NCORES = 8
TOK = 4096
D = 1024
DC = 8
NPAIR = 8
NQB = 8
QBS = 512
NKC = 32
NTT = 32
FFN = 4096
FC = 32
NEXP = 8
CAP = 10240
NST = CAP // 128
TT_ALL = 32768
EPS = 1e-5
INVF = float(1 << 24)
BIS_ROUNDS = 34
SLOTB = 256
SLOT_E = NQB * SLOTB
NST_E = SLOT_E // 128
NSB = SLOT_E // 512


class Sem:
    def __init__(self, nc, name):
        self.h = nc.alloc_semaphore(name)
        self.v = 0

    def inc(self, n=1):
        self.v += n
        return self.v


NONE_MS = None


class Prog:
    ENGINES = ("sync", "tensor", "vector", "scalar", "gpsimd")

    def __init__(self, nc):
        self.nc = nc
        self.items = []
        self.e = {n: Sem(nc, "se_" + n) for n in self.ENGINES}
        self.d = {n: Sem(nc, "sd_" + n) for n in ("sync", "scalar", "gpsimd")}

    @staticmethod
    def _clean(waits):
        return tuple(w for w in waits if w is not None and w[1] > 0)

    def op(self, eng, fn, waits=()):
        s = self.e[eng]
        v = s.inc(1)
        self.items.append((eng, self._clean(waits), fn, ((s, 1),)))
        return (s, v)

    def dma(self, eng, out, in_, waits=(), **kw):
        s = self.d[eng]
        v = s.inc(16)
        self.items.append((eng, self._clean(waits),
                           lambda e, out=out, in_=in_, kw=kw: e.dma_start(out=out, in_=in_, **kw),
                           ((s, 16),)))
        return (s, v)

    def idma(self, out, out_off, in_, in_off, bounds, waits=(), **kw):
        s = self.d["gpsimd"]
        v = s.inc(16)

        def fn(e, out=out, oo=out_off, in_=in_, io=in_off, b=bounds, kw=kw):
            try:
                return e.indirect_dma_start(out, oo, in_, io, bounds_check=b,
                                            oob_is_err=False, **kw)
            except Exception as ex:
                raise RuntimeError(
                    f"idma failed: out={out.tensor.name if hasattr(out, 'tensor') else out} "
                    f"in={in_.tensor.name if hasattr(in_, 'tensor') else in_} "
                    f"kw={kw}: {ex}") from ex
        self.items.append(("gpsimd", self._clean(waits), fn, ((s, 16),)))
        return (s, v)

    def wt(self, eng, *ms):
        self.items.append((eng, self._clean(ms), lambda e: e.nop(), ()))

    def drain(self, eng):
        s = self.d[eng]
        return (s, s.v)

    def barrier(self):
        for n in ("sync", "scalar", "gpsimd"):
            self.wt(n, (self.d[n], self.d[n].v))
        vals = []
        for n in self.ENGINES:
            s = self.e[n]
            v = s.inc(1)
            self.items.append((n, (), lambda e: e.nop(), ((s, 1),)))
            vals.append((s, v))
        for n in self.ENGINES:
            self.items.append((n, tuple(vals), lambda e: e.nop(), ()))

    def emit(self, block):
        for ename in self.ENGINES:
            def make(ename):
                def run(eng):
                    for (e, waits, fn, incs) in self.items:
                        if e != ename:
                            continue
                        for (sem, val) in waits:
                            eng.wait_ge(sem.h, val)
                        ins = fn(eng)
                        for (sem, n) in incs:
                            ins = ins.then_inc(sem.h, n)
                return run
            getattr(block, ename)(make(ename))


def build_nc():
    nc = bass.Bass("TRN2", target_bir_lowering=False, debug=False)

    x_own = nc.dram_tensor("x_own", [TOK, D], F32, kind="ExternalInput").ap()
    x_fm = nc.dram_tensor("x_fm", [D, TOK], F32, kind="ExternalInput").ap()
    wqkv_pk = nc.dram_tensor("wqkv_pk", [24, DC, 128, 128], F32, kind="ExternalInput").ap()
    w_out = nc.dram_tensor("w_out", [D, D], F32, kind="ExternalInput").ap()
    b_out = nc.dram_tensor("b_out", [1, D], F32, kind="ExternalInput").ap()
    wr_in = nc.dram_tensor("wr_in", [D, NEXP], F32, kind="ExternalInput").ap()
    w1_pk = nc.dram_tensor("w1_pk", [NEXP, FC, DC, 128, 128], F32, kind="ExternalInput").ap()
    w2_in = nc.dram_tensor("w2_in", [NEXP, FFN, D], F32, kind="ExternalInput").ap()
    out_t = nc.dram_tensor("out", [TOK, D], F32, kind="ExternalOutput").ap()

    q_dram = nc.dram_tensor("q_dram", [NPAIR, 128, TOK], BF16, kind="Internal").ap()
    k_dram = nc.dram_tensor("k_dram", [NPAIR, 128, TOK], BF16, kind="Internal").ap()
    v_dram = nc.dram_tensor("v_dram", [NPAIR, NKC, 128, 130], BF16, kind="Internal").ap()
    nb_dram = nc.dram_tensor("nb_dram", [16, 1], F32, kind="Internal").ap()
    cw_dram = nc.dram_tensor("cw_dram", [1, NEXP], F32, kind="Internal").ap()
    cin_g = nc.dram_tensor("cin_g", [128, 256], F32, kind="Internal").ap()
    cout_g = nc.dram_tensor("cout_g", [NCORES * 128, 256], F32, kind="Internal").ap()
    ctot_dram = nc.dram_tensor("ctot_dram", [1, NEXP], F32, kind="Internal").ap()
    xn_dram = nc.dram_tensor("xn_dram", [128, DC * TOK], BF16, kind="Internal").ap()
    xn_tm_dram = nc.dram_tensor("xn_tm_dram", [TOK, D], BF16, kind="Internal").ap()
    xc_dram = nc.dram_tensor("xc_dram", [NEXP, 128, DC * SLOT_E], BF16,
                             kind="Internal").ap()
    y_dram = nc.dram_tensor("y_dram", [NEXP, NST_E, 128, D], BF16,
                            kind="Internal").ap()

    ident_c = nc.inline_tensor(np.eye(128, dtype=ml_dtypes.bfloat16), name="ident_c").ap()
    pairsel_np = np.zeros((128, 2), dtype=ml_dtypes.bfloat16)
    pairsel_np[0:64, 0] = 1
    pairsel_np[64:128, 1] = 1
    pairsel_c = nc.inline_tensor(pairsel_np, name="pairsel_c").ap()
    masks_np = np.zeros((128, 4 * QBS), dtype=ml_dtypes.bfloat16)
    kparr = np.arange(128)[:, None]
    qfarr = np.arange(QBS)[None, :]
    for m in range(4):
        masks_np[:, m * QBS:(m + 1) * QBS] = (qfarr >= kparr + 128 * m).astype(ml_dtypes.bfloat16)
    masks_c = nc.inline_tensor(masks_np, name="masks_c").ap()
    esel_np = np.zeros((16, 8 * 128), dtype=ml_dtypes.bfloat16)
    for c in range(8):
        esel_np[2 * c, c * 128:c * 128 + 64] = 1
        esel_np[2 * c + 1, c * 128 + 64:c * 128 + 128] = 1
    esel_c = nc.inline_tensor(esel_np, name="esel_c").ap()
    ones_row_c = nc.inline_tensor(np.ones((1, 128), np.float32), name="ones_row_c").ap()
    tri_np = (np.arange(128)[:, None] < np.arange(128)[None, :]).astype(np.float32)
    tri_c = nc.inline_tensor(tri_np, name="tri_c").ap()
    iota2_c = nc.inline_tensor(np.arange(SLOTB, dtype=np.float32)[None, :],
                               name="iota2_c").ap()

    p = Prog(nc)
    PE, VE, AC, GP = p.e["tensor"], p.e["vector"], p.e["scalar"], p.e["gpsimd"]
    DS, DSC, DG = p.d["sync"], p.d["scalar"], p.d["gpsimd"]

    es0 = contextlib.ExitStack()
    sb0 = lambda name, shape, dt: es0.enter_context(nc.sbuf_tensor(name, shape, dt))

    pairsel_s = sb0("pairsel_s", [128, 2], BF16)
    ident_s = sb0("ident_s", [128, 128], BF16)
    onesr_s = sb0("onesr_s", [1, 128], F32)
    negb_bc = sb0("negb_bc", [128, 16], F32)
    negb_pair = sb0("negb_pair", [128, 8], F32)
    stats_m = sb0("stats_m", [128, NTT], F32)
    stats_r = sb0("stats_r", [128, NTT], F32)
    eps_col = sb0("eps_col", [128, 1], F32)
    G_own = sb0("G_own", [128, 256], F32)
    wsel = sb0("wsel", [128, 256], F32)

    p.dma("sync", pairsel_s[:, :], pairsel_c[:, :])
    p.dma("sync", ident_s[:, :], ident_c[:, :])
    c_ld = p.dma("sync", onesr_s[:, :], ones_row_c[:, :])
    eps_ms = p.op("vector", lambda e: e.memset(eps_col[:, :], EPS))

    esA = contextlib.ExitStack()
    sbA = lambda name, shape, dt: esA.enter_context(nc.sbuf_tensor(name, shape, dt))
    xn_fm = sbA("xn_fm", [128, DC * TOK], BF16)
    xt = [sbA(f"xt{i}", [128, D], F32) for i in range(3)]
    scrA = sbA("scrA", [128, D], F32)
    sq_s = sbA("sq_s", [128, D], F32)
    xn32 = sbA("xn32", [128, D], F32)
    xnbf = [sbA(f"xnbf{i}", [128, D], BF16) for i in range(2)]
    st_sum = sbA("st_sum", [128, NTT], F32)
    st_sq = sbA("st_sq", [128, NTT], F32)
    st_var = sbA("st_var", [128, NTT], F32)
    st_sd = sbA("st_sd", [128, NTT], F32)
    m2_s = sbA("m2_s", [128, NTT], F32)
    ps_tr = [esA.enter_context(nc.psum_tensor(f"ps_tr{i}", [128, 128], BF16)) for i in range(2)]
    ps_qkv = [esA.enter_context(nc.psum_tensor(f"ps_qkv{i}", [128, QBS], F32)) for i in range(4)]
    ps_nrm = esA.enter_context(nc.psum_tensor("ps_nrm", [128, QBS], F32))


    ld_x = [NONE_MS] * NTT
    xn_done = [NONE_MS] * NTT
    xnbf_free = [NONE_MS, NONE_MS]
    trev = [NONE_MS, NONE_MS]
    ld_gate = [c_ld]
    for i in range(NTT):
        ld_x[i] = p.dma("sync", xt[i % 3][:, :], x_own[i * 128:(i + 1) * 128, :],
                        waits=ld_gate + ([] if i < 3 else [xn_done[i - 3]]))
        v_sum = p.op("vector", lambda e, i=i: e.tensor_scalar(
            scrA[:, :], xt[i % 3][:, :], 1.0, None, ALU.mult, ALU.add,
            accum_out=st_sum[:, i:i + 1]), waits=[ld_x[i]])
        a_sq = p.op("scalar", lambda e, i=i: e.activation(
            sq_s[:, :], xt[i % 3][:, :], AF.Square, accum_out=st_sq[:, i:i + 1]),
            waits=[ld_x[i]])
        ld_gate = [v_sum, a_sq]
        v_m = p.op("vector", lambda e, i=i: e.tensor_scalar(
            stats_m[:, i:i + 1], st_sum[:, i:i + 1], 1.0 / D, None, ALU.mult),
            waits=[v_sum])
        v_e2 = p.op("vector", lambda e, i=i: e.tensor_scalar(
            st_var[:, i:i + 1], st_sq[:, i:i + 1], 1.0 / D, None, ALU.mult),
            waits=[a_sq])
        v_m2 = p.op("vector", lambda e, i=i: e.tensor_tensor(
            m2_s[:, i:i + 1], stats_m[:, i:i + 1], stats_m[:, i:i + 1], ALU.mult),
            waits=[v_m])
        v_var = p.op("vector", lambda e, i=i: e.tensor_tensor(
            st_var[:, i:i + 1], st_var[:, i:i + 1], m2_s[:, i:i + 1], ALU.subtract),
            waits=[v_m2, v_e2])
        a_sd = p.op("scalar", lambda e, i=i: e.activation(
            st_sd[:, i:i + 1], st_var[:, i:i + 1], AF.Sqrt, bias=eps_col[:, 0:1]),
            waits=[v_var, eps_ms])
        v_r = p.op("vector", lambda e, i=i: e.reciprocal(
            stats_r[:, i:i + 1], st_sd[:, i:i + 1]), waits=[a_sd])
        v_xn = p.op("vector", lambda e, i=i: e.tensor_scalar(
            xn32[:, :], xt[i % 3][:, :], stats_m[:, i:i + 1], stats_r[:, i:i + 1],
            ALU.subtract, ALU.mult), waits=[v_r])
        v_bf = p.op("vector", lambda e, i=i: e.tensor_copy(
            xnbf[i % 2][:, :], xn32[:, :]),
            waits=[v_xn, xnbf_free[i % 2], p.drain("scalar")])
        p.dma("scalar", xn_tm_dram[i * 128:(i + 1) * 128, :], xnbf[i % 2][:, :],
              waits=[v_bf])
        last = NONE_MS
        for dc in range(DC):
            tp = p.op("tensor", lambda e, i=i, dc=dc: e.transpose(
                ps_tr[dc % 2][:, :], xnbf[i % 2][:, dc * 128:(dc + 1) * 128], ident_s[:, :]),
                waits=[v_bf, trev[dc % 2]])
            trev[dc % 2] = p.op("vector", lambda e, i=i, dc=dc: e.tensor_copy(
                xn_fm[:, dc * TOK + i * 128: dc * TOK + (i + 1) * 128], ps_tr[dc % 2][:, :]),
                waits=[tp])
            last = trev[dc % 2]
            if dc == DC - 1:
                xnbf_free[i % 2] = tp
        xn_done[i] = last
    xn_all = xn_done[NTT - 1]

    esA2 = contextlib.ExitStack()
    sbA2 = lambda name, shape, dt: esA2.enter_context(nc.sbuf_tensor(name, shape, dt))
    xfmb = [sbA2(f"xfmb{i}", [128, TOK], F32) for i in range(2)]
    wr_s = sbA2("wr_s", [128, DC * NEXP], F32)
    ones_col = sbA2("ones_col", [128, 1], F32)
    cwb = sbA2("cwb", [128, NEXP], F32)
    lgraw = sbA2("lgraw", [128, 256], F32)
    lgev = sbA2("lgev", [128, 256], F32)
    cwev = sbA2("cwev", [1, NEXP], F32)
    exps_a = sbA2("exps_a", [128, NEXP], F32)
    probs_a = sbA2("probs_a", [128, NEXP], F32)
    ge8_a = sbA2("ge8_a", [128, NEXP], F32)
    pm8_a = sbA2("pm8_a", [128, NEXP], F32)
    gm8_a = sbA2("gm8_a", [128, NEXP], F32)
    scr8 = sbA2("scr8", [128, NEXP], F32)
    lg2 = sbA2("lg2", [128, NEXP], F32)
    rt = sbA2("rt", [128, 8 * NTT], F32)
    ps_lg = [ps_qkv[0][:, 0:256], ps_qkv[1][:, 0:256]]
    ps_cw = ps_nrm

    wr_ld = NONE_MS
    for dc in range(DC):
        wr_ld = p.dma("sync", wr_s[:, dc * NEXP:(dc + 1) * NEXP],
                      wr_in[dc * 128:(dc + 1) * 128, :])
    oc_ms = p.op("vector", lambda e: e.memset(ones_col[:, :], 1.0))
    lg_zero = p.op("vector", lambda e: e.memset(lgraw[:, :], 0.0))

    xfm_ld = [NONE_MS, NONE_MS]
    xfm_free = [NONE_MS, NONE_MS]
    xfm_gate = wr_ld
    lg_acc = lg_zero
    cw_mm = NONE_MS
    psg_free = [NONE_MS, NONE_MS]
    for dc in range(DC):
        xfm_ld[dc % 2] = p.dma("sync", xfmb[dc % 2][:, :],
                               x_fm[dc * 128:(dc + 1) * 128, :],
                               waits=[xfm_free[dc % 2], xfm_gate])
        cw_mm = p.op("tensor", lambda e, dc=dc: e.matmul(
            ps_cw[0:1, 0:NEXP], ones_col[:, :], wr_s[:, dc * NEXP:(dc + 1) * NEXP],
            start=(dc == 0), stop=(dc == DC - 1)),
            waits=([wr_ld, oc_ms, xn_all] if dc == 0 else []))
        mmlast = NONE_MS
        for t in range(NTT):
            mmlast = p.op("tensor", lambda e, dc=dc, t=t: e.matmul(
                ps_lg[dc % 2][:, t * NEXP:(t + 1) * NEXP],
                xfmb[dc % 2][:, t * 128:(t + 1) * 128],
                wr_s[:, dc * NEXP:(dc + 1) * NEXP], start=True, stop=True),
                waits=([xfm_ld[dc % 2], psg_free[dc % 2]] if t == 0 else []))
            if t == 0:
                xfm_gate = mmlast
        xfm_free[dc % 2] = mmlast
        evg = p.op("vector", lambda e, dc=dc: e.tensor_copy(
            lgev[:, :], ps_lg[dc % 2][:, :]), waits=[mmlast])
        psg_free[dc % 2] = evg
        lg_acc = p.op("vector", lambda e: e.tensor_tensor(
            lgraw[:, :], lgraw[:, :], lgev[:, :], ALU.add), waits=[evg, lg_acc])
    cw_ev = p.op("vector", lambda e: e.tensor_copy(cwev[0:1, :], ps_cw[0:1, 0:NEXP]),
                 waits=[cw_mm])
    cw_st = p.dma("gpsimd", cw_dram[0:1, :], cwev[0:1, :], waits=[cw_ev])
    cw_bc = p.dma("sync", cwb[:, :], cw_dram[0:1, :].to_broadcast([128, NEXP]),
                  waits=[cw_st])

    g_done = NONE_MS
    for t in range(NTT):
        c0 = 8 * t
        w0 = [lg_acc, cw_bc] if t == 0 else [g_done]
        v1 = p.op("vector", lambda e, t=t: e.tensor_scalar(
            scr8[:, :], cwb[:, :], stats_m[:, t:t + 1], None, ALU.mult), waits=w0)
        v2 = p.op("vector", lambda e, t=t: e.tensor_tensor(
            lg2[:, :], lgraw[:, t * NEXP:(t + 1) * NEXP], scr8[:, :], ALU.subtract),
            waits=[v1])
        v3 = p.op("vector", lambda e, t=t: e.tensor_scalar(
            lg2[:, :], lg2[:, :], stats_r[:, t:t + 1], None, ALU.mult), waits=[v2])
        v4 = p.op("vector", lambda e, t=t, c0=c0: e.reduce_max(
            rt[:, c0:c0 + 1], lg2[:, :], axis=AX.X, negate=True), waits=[v3])
        a1 = p.op("scalar", lambda e, t=t, c0=c0: e.activation(
            exps_a[:, :], lg2[:, :], AF.Exp, bias=rt[:, c0:c0 + 1],
            accum_out=rt[:, c0 + 1:c0 + 2]), waits=[v4])
        v5 = p.op("vector", lambda e, c0=c0: e.reciprocal(
            rt[:, c0 + 2:c0 + 3], rt[:, c0 + 1:c0 + 2]), waits=[a1])
        v6 = p.op("vector", lambda e, c0=c0: e.tensor_scalar(
            probs_a[:, :], exps_a[:, :], rt[:, c0 + 2:c0 + 3], None, ALU.mult),
            waits=[v5])
        v7 = p.op("vector", lambda e, c0=c0: e.reduce_max(
            rt[:, c0 + 3:c0 + 4], probs_a[:, :], axis=AX.X), waits=[v6])
        v8 = p.op("vector", lambda e, c0=c0: e.tensor_scalar(
            ge8_a[:, :], probs_a[:, :], rt[:, c0 + 3:c0 + 4], None, ALU.is_ge,
            ALU.add, accum_out=rt[:, c0 + 4:c0 + 5]), waits=[v7])
        v9 = p.op("vector", lambda e: e.tensor_tensor(
            pm8_a[:, :], probs_a[:, :], ge8_a[:, :], ALU.subtract), waits=[v8])
        v10 = p.op("vector", lambda e, c0=c0: e.reduce_max(
            rt[:, c0 + 5:c0 + 6], pm8_a[:, :], axis=AX.X), waits=[v9])
        v11 = p.op("vector", lambda e, c0=c0: e.tensor_scalar(
            rt[:, c0 + 4:c0 + 5], rt[:, c0 + 4:c0 + 5], 2.0, None, ALU.is_ge),
            waits=[v10])
        v12 = p.op("vector", lambda e, c0=c0: e.tensor_tensor(
            rt[:, c0 + 7:c0 + 8], rt[:, c0 + 3:c0 + 4], rt[:, c0 + 5:c0 + 6],
            ALU.subtract), waits=[v11])
        v13 = p.op("vector", lambda e, c0=c0: e.tensor_tensor(
            rt[:, c0 + 7:c0 + 8], rt[:, c0 + 4:c0 + 5], rt[:, c0 + 7:c0 + 8],
            ALU.mult), waits=[v12])
        v14 = p.op("vector", lambda e, c0=c0: e.tensor_tensor(
            rt[:, c0 + 6:c0 + 7], rt[:, c0 + 5:c0 + 6], rt[:, c0 + 7:c0 + 8],
            ALU.add), waits=[v13])
        v15 = p.op("vector", lambda e, c0=c0: e.tensor_scalar(
            gm8_a[:, :], probs_a[:, :], rt[:, c0 + 6:c0 + 7], None, ALU.is_ge),
            waits=[v14])
        g_done = p.op("vector", lambda e, t=t: e.tensor_tensor(
            G_own[:, :].rearrange("q (e t) -> q e t", e=NEXP)[:, :, t],
            probs_a[:, :], gm8_a[:, :], ALU.mult), waits=[v15])

    g_ship = p.dma("gpsimd", cin_g[:, :], G_own[:, :], waits=[g_done])
    cc1 = p.op("gpsimd", lambda e: e.collective_compute(
        "AllGather", ALU.bypass, replica_groups=[list(range(NCORES))],
        ins=[cin_g], outs=[cout_g]), waits=[g_ship])

    esB = contextlib.ExitStack()
    sbB = lambda name, shape, dt: esB.enter_context(nc.sbuf_tensor(name, shape, dt))
    wsl = [sbB(f"wsl{i}", [128, DC * 128], BF16) for i in range(2)]
    ev_bf = [sbB(f"ev_bf{i}", [128, QBS], BF16) for i in range(4)]
    sq_bf = [sbB(f"sq_bf{i}", [128, QBS], BF16) for i in range(2)]
    nmx = sbB("nmx", [2, 128], F32)
    vas = [sbB(f"vas{i}", [128, 4 * 130], BF16) for i in range(2)]
    nb16 = sbB("nb16", [16, 4], F32)
    ps_vtr = ps_tr[0]

    xn_st = p.dma("sync", xn_dram[:, :], xn_fm[:, :], waits=[xn_all])

    wsl_ld = [NONE_MS, NONE_MS]
    wsl_gate = NONE_MS
    nmm_pend = None
    sq_guard = [NONE_MS, NONE_MS]
    ev_tp_guard = [NONE_MS] * 4
    psq_free = [[psg_free[0]], [psg_free[1]], [NONE_MS], [NONE_MS]]
    nrm_mm_prev = NONE_MS
    nrm_red_prev = cw_ev
    vtr_prev = NONE_MS

    for s in range(24):
        buf = s % 2
        wsl_ld[buf] = p.dma("gpsimd",
                            wsl[buf][:, :].rearrange("r (dc c) -> r dc c", dc=DC),
                            wqkv_pk[s].rearrange("dc r c -> r dc c"),
                            waits=([] if s == 0 else [wsl_gate]))
        for qb in range(NQB):
            par = (s * 8 + qb) % 4
            mm = NONE_MS
            for dc in range(DC):
                w_mm = []
                if dc == 0:
                    w_mm = [wsl_ld[buf]] + psq_free[par]
                    if s == 0 and qb == 0:
                        w_mm.append(xn_all)
                mm = p.op("tensor", lambda e, buf=buf, qb=qb, dc=dc, par=par: e.matmul(
                    ps_qkv[par][:, :], wsl[buf][:, dc * 128:(dc + 1) * 128],
                    xn_fm[:, dc * TOK + qb * QBS: dc * TOK + (qb + 1) * QBS],
                    start=(dc == 0), stop=(dc == DC - 1)), waits=w_mm)
                if qb == 0 and dc == 0:
                    wsl_gate = mm
            if s < 16:
                dst = q_dram if s < 8 else k_dram
                pr = s % 8
                ev = p.op("vector", lambda e, par=par: e.tensor_copy(
                    ev_bf[par][:, :], ps_qkv[par][:, :]),
                    waits=[mm, p.drain("scalar")])
                p.dma("scalar", dst[pr][:, qb * QBS:(qb + 1) * QBS], ev_bf[par][:, :],
                      waits=[ev])
                sqb = (s * 8 + qb) % 2
                a_sq2 = p.op("scalar", lambda e, par=par, sqb=sqb: e.activation(
                    sq_bf[sqb][:, :], ps_qkv[par][:, :], AF.Square),
                    waits=[mm, ev, sq_guard[sqb]])
                psq_free[par] = [ev, a_sq2]
                if nmm_pend is not None:
                    (pa, psqb, pcol) = nmm_pend
                    nmm = p.op("tensor", lambda e, psqb=psqb: e.matmul(
                        ps_nrm[0:2, :], pairsel_s[:, :], sq_bf[psqb][:, :],
                        start=True, stop=True), waits=[pa, nrm_red_prev])
                    sq_guard[psqb] = nmm
                    nrm_red_prev = p.op("vector", lambda e, pcol=pcol: e.reduce_max(
                        nmx[0:2, pcol:pcol + 1], ps_nrm[0:2, :], axis=AX.X),
                        waits=[nmm])
                nmm_pend = (a_sq2, sqb, s * 8 + qb)
            else:
                if nmm_pend is not None:
                    (pa, psqb, pcol) = nmm_pend
                    nmm = p.op("tensor", lambda e, psqb=psqb: e.matmul(
                        ps_nrm[0:2, :], pairsel_s[:, :], sq_bf[psqb][:, :],
                        start=True, stop=True), waits=[pa, nrm_red_prev])
                    sq_guard[psqb] = nmm
                    nrm_red_prev = p.op("vector", lambda e, pcol=pcol: e.reduce_max(
                        nmx[0:2, pcol:pcol + 1], ps_nrm[0:2, :], axis=AX.X),
                        waits=[nmm])
                    nmm_pend = None
                pr = s - 16
                ev = p.op("vector", lambda e, par=par: e.tensor_copy(
                    ev_bf[par][:, :], ps_qkv[par][:, :]),
                    waits=[mm, p.drain("scalar"), ev_tp_guard[par]])
                psq_free[par] = [ev]
                vb = par % 2
                mset = p.op("vector", lambda e, vb=vb: e.memset(
                    vas[vb][:, :].rearrange("q (kc h x) -> q kc h x", kc=4, h=2)[:, :, :, 64:65],
                    1.0), waits=[p.drain("scalar")])
                lastv = NONE_MS
                last_tp = NONE_MS
                for tt in range(4):
                    tp = p.op("tensor", lambda e, par=par, tt=tt: e.transpose(
                        ps_vtr[:, :], ev_bf[par][:, tt * 128:(tt + 1) * 128], ident_s[:, :]),
                        waits=[ev, vtr_prev])
                    vtr_prev = p.op("vector", lambda e, vb=vb, tt=tt: e.tensor_copy(
                        vas[vb][:, tt * 130:(tt + 1) * 130].rearrange(
                            "q (h x) -> q h x", h=2)[:, :, 0:64],
                        ps_vtr[:, :].rearrange("q (h x) -> q h x", h=2)),
                        waits=[tp, mset])
                    lastv = vtr_prev
                    last_tp = tp
                ev_tp_guard[par] = last_tp
                p.dma("scalar", v_dram[pr, qb * 4:(qb + 1) * 4].rearrange("kc q c -> q kc c"),
                      vas[vb][:, :].rearrange("q (kc c) -> q kc c", kc=4), waits=[lastv])

    prev_ms = nrm_red_prev
    prev_dma = NONE_MS
    for pr in range(8):
        m1 = p.op("vector", lambda e, pr=pr: e.reduce_max(
            nb16[0:2, 0:1], nmx[0:2, pr * 8:(pr + 1) * 8], axis=AX.X),
            waits=[prev_ms, prev_dma])
        m2 = p.op("vector", lambda e, pr=pr: e.reduce_max(
            nb16[0:2, 1:2], nmx[0:2, 64 + pr * 8: 64 + (pr + 1) * 8], axis=AX.X),
            waits=[m1])
        m3 = p.op("vector", lambda e: e.tensor_tensor(
            nb16[0:2, 2:3], nb16[0:2, 0:1], nb16[0:2, 1:2], ALU.mult), waits=[m2])
        aB = p.op("scalar", lambda e: e.activation(
            nb16[0:2, 3:4], nb16[0:2, 2:3], AF.Sqrt), waits=[m3])
        m4 = p.op("vector", lambda e: e.tensor_scalar(
            nb16[0:2, 3:4], nb16[0:2, 3:4], -0.125, None, ALU.mult), waits=[aB])
        prev_dma = p.dma("gpsimd", nb_dram[2 * pr:2 * pr + 2, :], nb16[0:2, 3:4],
                         waits=[m4])
        prev_ms = m4
    nb_ld = p.dma("sync", negb_bc[:, :],
                  nb_dram.rearrange("h one -> one h").to_broadcast([128, 16]),
                  waits=[prev_dma])
    p.op("vector", lambda e: e.tensor_tensor(
        negb_pair[:, :],
        negb_bc[:, :].rearrange("q (p h) -> q p h", h=2)[:, :, 0],
        negb_bc[:, :].rearrange("q (p h) -> q p h", h=2)[:, :, 1], ALU.min),
        waits=[nb_ld])

    p.barrier()
    esB.close()
    esA2.close()
    esA.close()

    esM = contextlib.ExitStack()
    sbM = lambda name, shape, dt: esM.enter_context(nc.sbuf_tensor(name, shape, dt))
    masks_s = sbM("masks_s", [128, 4 * QBS], BF16)
    b_bc = sbM("b_bc", [128, D], F32)
    esel_s = sbM("esel_s", [16, 8 * 128], BF16)
    Gs = sbM("Gs", [128, 2048], F32)
    th_lo = sbM("th_lo", [128, NEXP], F32)
    th_hi = sbM("th_hi", [128, NEXP], F32)
    th_md = sbM("th_md", [128, NEXP], F32)
    cnt_p = sbM("cnt_p", [128, NEXP], F32)
    f_scr = sbM("f_scr", [128, 256], F32)
    cnt_tot = sbM("cnt_tot", [1, NEXP], F32)
    cnt_b = sbM("cnt_b", [128, NEXP], F32)
    ge_b = sbM("ge_b", [128, NEXP], F32)
    u_s = sbM("u_s", [128, NEXP], F32)
    F_e = sbM("F_e", [128, 256], F32)
    Fp_e = sbM("Fp_e", [128, 256], F32)

    esC = contextlib.ExitStack()
    sbC = lambda name, shape, dt: esC.enter_context(nc.sbuf_tensor(name, shape, dt))
    qs = [sbC(f"qs{i}", [128, QBS], BF16) for i in range(2)]
    ks = [sbC(f"ks{i}", [128, TOK], BF16) for i in range(2)]
    vs = [sbC(f"vs{i}", [128, NKC * 130], BF16) for i in range(2)]
    pt = [sbC(f"pt{i}", [128, 1024], BF16) for i in range(3)]
    attn_raw = sbC("attn_raw", [128, 8 * QBS], BF16)
    attn_nrm = sbC("attn_nrm", [128, 8 * QBS], BF16)
    dmat = sbC("dmat", [16, QBS], F32)
    dtmp = [sbC(f"dtmp{i}", [1, QBS], F32) for i in range(2)]
    rbf = sbC("rbf", [16, QBS], BF16)
    rexp_bf = sbC("rexp_bf", [128, QBS], BF16)
    xres = [sbC(f"xres{i}", [128, D], F32) for i in range(2)]
    out32 = [sbC(f"out32_{i}", [128, D], F32) for i in range(2)]
    wout_bf = sbC("wout_bf", [128, DC * D], BF16)
    ps_s = [esC.enter_context(nc.psum_tensor(f"ps_s{i}", [128, 1024], F32)) for i in range(2)]
    ps_o = [esC.enter_context(nc.psum_tensor(f"ps_o{i}", [128, QBS], F32)) for i in range(2)]
    ps_op = [esC.enter_context(nc.psum_tensor(f"ps_op{i}", [128, QBS], F32)) for i in range(2)]

    bld = p.dma("sync", b_bc[:, :], b_out[0:1, :].to_broadcast([128, D]))
    masks_ld = p.dma("sync", masks_s[:, :], masks_c[:, :])
    esel_ld = p.dma("sync", esel_s[:, :], esel_c[:, :])
    wout_ld = NONE_MS
    for c in range(DC):
        wout_ld = p.dma("gpsimd", wout_bf[:, c * D:(c + 1) * D],
                        w_out[c * 128:(c + 1) * 128, :])
    gs_ld = p.dma("gpsimd", Gs[:, :].rearrange("q (r c) -> q r c", r=NCORES),
                  cout_g.rearrange("(r q) c -> q r c", q=128),
                  waits=[cc1, wout_ld])
    gm0 = p.op("vector", lambda e: e.memset(th_lo[:, :], 0.0))
    gm1 = p.op("vector", lambda e: e.memset(th_hi[:, :], 1.0))
    gm2 = p.op("vector", lambda e: e.memset(th_md[:, :], 0.5))

    def gs_e(e_):
        return Gs[:, :].rearrange("q (r e t) -> q r e t", r=NCORES, e=NEXP)[:, :, e_, :]

    bis = {"prev": gm2, "bcast": gs_ld, "d3": NONE_MS}
    bis_steps = []

    def _mk_count(rnd):
        def fn():
            pr = bis["prev"]
            for e_ in range(NEXP):
                pr = p.op("vector", lambda en, e_=e_: en.tensor_scalar(
                    f_scr[:, :].rearrange("q (r t) -> q r t", r=NCORES),
                    gs_e(e_), th_md[:, e_:e_ + 1], None, ALU.is_ge,
                    ALU.add, accum_out=cnt_p[:, e_:e_ + 1]),
                    waits=([bis["prev"], bis["bcast"], gs_ld, gm0, gm1]
                           if e_ == 0 else []))
            rc = p.op("gpsimd", lambda en: en.reduce_sum(
                cnt_tot[0:1, :], cnt_p[:, :], axis=AX.C),
                waits=[pr, bis["d3"]])
            d3 = p.dma("gpsimd", ctot_dram[0:1, :], cnt_tot[0:1, :], waits=[rc])
            bis["d3"] = d3
            bis["bcast"] = p.dma("gpsimd", cnt_b[:, :],
                                 ctot_dram[0:1, :].to_broadcast([128, NEXP]),
                                 waits=[d3])
            bis["prev"] = pr
        return fn

    def _mk_update(rnd):
        def fn():
            g1 = p.op("vector", lambda en: en.tensor_scalar(
                ge_b[:, :], cnt_b[:, :], float(CAP), None, ALU.is_ge),
                waits=[bis["bcast"], bis["prev"]])
            g2_ = p.op("vector", lambda en: en.tensor_tensor(
                u_s[:, :], th_md[:, :], th_lo[:, :], ALU.subtract), waits=[g1])
            g3 = p.op("vector", lambda en: en.tensor_tensor(
                u_s[:, :], ge_b[:, :], u_s[:, :], ALU.mult), waits=[g2_])
            g4 = p.op("vector", lambda en: en.tensor_tensor(
                th_lo[:, :], th_lo[:, :], u_s[:, :], ALU.add), waits=[g3])
            g5 = p.op("vector", lambda en: en.tensor_tensor(
                u_s[:, :], th_hi[:, :], th_md[:, :], ALU.subtract), waits=[g4])
            g6 = p.op("vector", lambda en: en.tensor_tensor(
                u_s[:, :], ge_b[:, :], u_s[:, :], ALU.mult), waits=[g5])
            g7 = p.op("vector", lambda en: en.tensor_tensor(
                th_hi[:, :], th_md[:, :], u_s[:, :], ALU.add), waits=[g6])
            g8 = p.op("vector", lambda en: en.tensor_tensor(
                u_s[:, :], th_lo[:, :], th_hi[:, :], ALU.add), waits=[g7])
            bis["prev"] = p.op("vector", lambda en: en.tensor_scalar(
                th_md[:, :], u_s[:, :], 0.5, None, ALU.mult), waits=[g8])
        return fn

    for rnd in range(BIS_ROUNDS):
        bis_steps.append(_mk_count(rnd))
        bis_steps.append(_mk_update(rnd))


    strip_free = [NONE_MS, NONE_MS]
    pt_free = [NONE_MS, NONE_MS, NONE_MS]
    s_free = [NONE_MS, NONE_MS]
    o_free = [NONE_MS, NONE_MS]
    dmat_free = NONE_MS
    op_free = [NONE_MS, NONE_MS]

    lq_gate = NONE_MS
    for qb in range(NQB):
        nkc = 4 * (qb + 1)
        for pr in range(NPAIR):
            sp = pr % 2
            lq = p.dma("sync", qs[sp][:, :], q_dram[pr][:, qb * QBS:(qb + 1) * QBS],
                       waits=[strip_free[sp], lq_gate])
            lk = p.dma("sync", ks[sp][:, 0:nkc * 128], k_dram[pr][:, 0:nkc * 128])
            lv = p.dma("sync",
                       vs[sp][:, 0:nkc * 130].rearrange("q (kc c) -> q kc c", kc=nkc),
                       v_dram[pr, 0:nkc].rearrange("kc q c -> q kc c"))
            last_av = NONE_MS
            av_pend = []
            for kc in range(nkc):
                par = kc % 2
                pp = kc % 3
                w_s = [s_free[par]]
                if kc == 0:
                    w_s += [p.drain("sync")]
                mmA = p.op("tensor", lambda e, sp=sp, kc=kc, par=par: e.matmul(
                    ps_s[par][:, 0:512], ks[sp][0:64, kc * 128:(kc + 1) * 128],
                    qs[sp][0:64, :], start=True, stop=True, tile_position=(0, 0)),
                    waits=w_s)
                if kc == 0:
                    lq_gate = mmA
                mmB = p.op("tensor", lambda e, sp=sp, kc=kc, par=par: e.matmul(
                    ps_s[par][:, 512:1024], ks[sp][64:128, kc * 128:(kc + 1) * 128],
                    qs[sp][64:128, :], start=True, stop=True, tile_position=(64, 0)))
                eB = p.op("scalar", lambda e, pp=pp, par=par, pr=pr: e.activation(
                    pt[pp][:, 0:1024], ps_s[par][:, 0:1024], AF.Exp,
                    bias=negb_pair[:, pr:pr + 1], scale=0.125),
                    waits=[mmB, pt_free[pp]])
                s_free[par] = eB
                w_av = [eB]
                if kc >= 4 * qb:
                    m = kc - 4 * qb
                    p.op("vector", lambda e, pp=pp, m=m: e.tensor_tensor(
                        pt[pp][:, 0:512], pt[pp][:, 0:512],
                        masks_s[:, m * QBS:(m + 1) * QBS], ALU.mult),
                        waits=[eB])
                    mk2 = p.op("vector", lambda e, pp=pp, m=m: e.tensor_tensor(
                        pt[pp][:, 512:1024], pt[pp][:, 512:1024],
                        masks_s[:, m * QBS:(m + 1) * QBS], ALU.mult))
                    w_av = [mk2]
                if kc == 0:
                    w_av = w_av + [o_free[0], o_free[1]]
                av_pend.append((kc, pp, w_av))
                for (kcp, ppp, w_avp) in av_pend if kc == nkc - 1 else av_pend[:-2]:
                    avA = p.op("tensor", lambda e, sp=sp, kc=kcp, nkc=nkc, pp=ppp: e.matmul(
                        ps_o[0][0:65, :], vs[sp][:, kc * 130: kc * 130 + 65],
                        pt[pp][:, 0:512], start=(kc == 0), stop=(kc == nkc - 1)),
                        waits=w_avp)
                    avB = p.op("tensor", lambda e, sp=sp, kc=kcp, nkc=nkc, pp=ppp: e.matmul(
                        ps_o[1][0:65, :], vs[sp][:, kc * 130 + 65: kc * 130 + 130],
                        pt[pp][:, 512:1024], start=(kc == 0), stop=(kc == nkc - 1)))
                    pt_free[ppp] = avB
                    last_av = avB
                av_pend = [] if kc == nkc - 1 else av_pend[-2:]
            strip_free[sp] = last_av
            ev1 = p.op("vector", lambda e, pr=pr: e.tensor_copy(
                attn_raw[0:64, pr * QBS:(pr + 1) * QBS], ps_o[0][0:64, :]),
                waits=[last_av])
            ev2 = p.op("vector", lambda e, pr=pr: e.tensor_copy(
                attn_raw[64:128, pr * QBS:(pr + 1) * QBS], ps_o[1][0:64, :]))
            d1 = p.op("vector", lambda e: e.tensor_copy(
                dtmp[0][0:1, :], ps_o[0][64:65, :]), waits=[p.drain("scalar")])
            d2 = p.op("vector", lambda e: e.tensor_copy(
                dtmp[1][0:1, :], ps_o[1][64:65, :]))
            o_free[0] = d1
            o_free[1] = d2
            p.dma("scalar", dmat[2 * pr:2 * pr + 1, :], dtmp[0][0:1, :],
                  waits=[d1, dmat_free])
            p.dma("scalar", dmat[2 * pr + 1:2 * pr + 2, :], dtmp[1][0:1, :],
                  waits=[d2])
            if bis_steps:
                bis_steps.pop(0)()
        vR = p.op("vector", lambda e: e.reciprocal(dmat[0:16, :], dmat[0:16, :]),
                  waits=[p.drain("scalar")])
        vRb = p.op("vector", lambda e: e.tensor_copy(rbf[0:16, :], dmat[0:16, :]),
                   waits=[vR])
        dmat_free = vRb
        nrm_done = NONE_MS
        for c in range(DC):
            mmr = p.op("tensor", lambda e, c=c: e.matmul(
                ps_op[0][:, :], esel_s[:, c * 128:(c + 1) * 128], rbf[0:16, :],
                start=True, stop=True), waits=[vRb, op_free[0]])
            rev = p.op("vector", lambda e: e.tensor_copy(rexp_bf[:, :], ps_op[0][:, :]),
                       waits=[mmr])
            nrm_done = p.op("vector", lambda e, c=c: e.tensor_tensor(
                attn_nrm[:, c * QBS:(c + 1) * QBS], attn_raw[:, c * QBS:(c + 1) * QBS],
                rexp_bf[:, :], ALU.mult), waits=[rev])
            op_free[0] = nrm_done
        xres_gate = vR
        for tt in range(4):
            ti = qb * 4 + tt
            ob = tt % 2
            xld = p.dma("scalar", xres[ob][:, :], x_own[ti * 128:(ti + 1) * 128, :],
                        waits=[xres_gate])
            mm_h = [NONE_MS, NONE_MS]
            for half in range(2):
                for c in range(DC):
                    w_op = []
                    if c == 0:
                        w_op = [nrm_done, op_free[half]]
                        if qb == 0 and tt == 0:
                            w_op.append(wout_ld)
                    mm_h[half] = p.op("tensor", lambda e, tt=tt, c=c, half=half: e.matmul(
                        ps_op[half][:, :],
                        attn_nrm[:, c * QBS + tt * 128: c * QBS + (tt + 1) * 128],
                        wout_bf[:, c * D + half * 512: c * D + (half + 1) * 512],
                        start=(c == 0), stop=(c == DC - 1)), waits=w_op)
            va = p.op("vector", lambda e, tt=tt, ob=ob: e.tensor_tensor(
                out32[ob][:, 0:512], ps_op[0][:, :], xres[ob][:, 0:512], ALU.add),
                waits=[mm_h[0], p.drain("scalar"), bld])
            xres_gate = va
            vb2 = p.op("vector", lambda e, tt=tt, ob=ob: e.tensor_tensor(
                out32[ob][:, 512:1024], ps_op[1][:, :], xres[ob][:, 512:1024], ALU.add),
                waits=[mm_h[1]])
            vc1 = p.op("vector", lambda e, ob=ob: e.tensor_tensor(
                out32[ob][:, 0:512], out32[ob][:, 0:512], b_bc[:, 0:512], ALU.add),
                waits=[va])
            vc2 = p.op("vector", lambda e, ob=ob: e.tensor_tensor(
                out32[ob][:, 512:1024], out32[ob][:, 512:1024], b_bc[:, 512:1024],
                ALU.add), waits=[vb2])
            op_free[0] = vc1
            op_free[1] = vc2
            p.dma("scalar", out_t[ti * 128:(ti + 1) * 128, :],
                  out32[ob][:, :], waits=[vc2])

    while bis_steps:
        bis_steps.pop(0)()
    wprev = bis["prev"]
    for e_ in range(NEXP):
        s1 = p.op("vector", lambda en, e_=e_: en.tensor_scalar(
            F_e[:, 0:32], G_own[:, e_ * 32:(e_ + 1) * 32], th_lo[:, e_:e_ + 1],
            None, ALU.is_ge), waits=[wprev])
        s2 = p.op("vector", lambda en, e_=e_: en.tensor_scalar(
            Fp_e[:, 0:32], G_own[:, e_ * 32:(e_ + 1) * 32], 0.0, None, ALU.is_gt),
            waits=[s1])
        s3 = p.op("vector", lambda en: en.tensor_tensor(
            F_e[:, 0:32], F_e[:, 0:32], Fp_e[:, 0:32], ALU.mult), waits=[s2])
        wprev = p.op("vector", lambda en, e_=e_: en.tensor_tensor(
            wsel[:, e_ * 32:(e_ + 1) * 32], G_own[:, e_ * 32:(e_ + 1) * 32],
            F_e[:, 0:32], ALU.mult), waits=[s3])
    wsel_done = wprev

    p.barrier()
    esC.close()
    esM.close()

    esDR = contextlib.ExitStack()
    sbDR = lambda name, shape, dt: esDR.enter_context(nc.sbuf_tensor(name, shape, dt))
    slotidx_sb = sbDR("slotidx_sb", [128, 256], F32)
    gc_sb = sbDR("gc_sb", [128, 128], F32)
    iota2b = sbDR("iota2b", [128, SLOTB], F32)
    ind_all = sbDR("ind_all", [128, 256], F32)

    esD0 = contextlib.ExitStack()
    sbD0 = lambda name, shape, dt: esD0.enter_context(nc.sbuf_tensor(name, shape, dt))
    xn_tm_sb = sbD0("xn_tm_sb", [128, 4 * D], BF16)
    tri_s = sbD0("tri_s", [128, 128], F32)
    onescol_d = sbD0("onescol_d", [128, 1], F32)
    ones32 = sbD0("ones32", [128, 32], F32)
    base_sb = sbD0("base_sb", [128, 64], F32)

    tri_ld = p.dma("sync", tri_s[:, :], tri_c[:, :])
    iota_ld = p.dma("sync", iota2b[:, :],
                    iota2_c[0:1, :].to_broadcast([128, SLOTB]))
    on_ms = p.op("vector", lambda e: e.memset(ones32[:, :], 1.0))
    on1_ms = p.op("vector", lambda e: e.memset(onescol_d[:, :], 1.0))

    esD0i = contextlib.ExitStack()
    sbD0i = lambda name, shape, dt: esD0i.enter_context(nc.sbuf_tensor(name, shape, dt))
    scn_all = sbD0i("scn_all", [128, 256], F32)
    segb = sbD0i("segb", [128, 256], F32)
    rc_all = sbD0i("rc_all", [128, 64], F32)
    esD0p = contextlib.ExitStack()
    ps_xc = [esD0p.enter_context(nc.psum_tensor(f"ps_xc{i}", [128, 512], F32))
             for i in range(4)]
    ps_bs = esD0p.enter_context(nc.psum_tensor("ps_bs", [128, 512], F32))
    ps_gc = [esD0p.enter_context(nc.psum_tensor(f"ps_gc{i}", [128, 512], F32))
             for i in range(2)]

    ia0 = p.op("vector", lambda e: e.tensor_scalar(
        ind_all[:, :], wsel[:, :], 0.0, None, ALU.is_gt), waits=[wsel_done])
    pv = p.op("vector", lambda e: e.memset(segb[:, :], 0.0), waits=[ia0])
    for e_ in range(NEXP):
        pv = p.op("vector", lambda e, e_=e_: e.tensor_tensor_scan(
            scn_all[:, e_ * 32:(e_ + 1) * 32], ind_all[:, e_ * 32:(e_ + 1) * 32],
            ones32[:, :], 0.0, ALU.add, ALU.mult),
            waits=([pv, on_ms] if e_ == 0 else [pv]))
    for e_ in range(NEXP):
        sv = scn_all[:, e_ * 32:(e_ + 1) * 32].rearrange("q (b f) -> q b f", f=4)
        gv = segb[:, e_ * 32:(e_ + 1) * 32].rearrange("q (b f) -> q b f", f=4)
        for j in range(4):
            pv = p.op("vector", lambda e, sv=sv, gv=gv, j=j: e.tensor_copy(
                gv[:, 1:8, j], sv[:, 0:7, 3]), waits=[pv])
        pv = p.op("vector", lambda e, e_=e_, sv=sv, gv=gv: e.tensor_tensor(
            rc_all[:, e_ * 8:(e_ + 1) * 8], sv[:, :, 3], gv[:, :, 0],
            ALU.subtract), waits=[pv])
    bmm0 = p.op("tensor", lambda e: e.matmul(
        ps_bs[:, 0:64], tri_s[:, :], rc_all[:, 0:64], start=True, stop=True),
        waits=[pv, iota_ld])
    bev0 = p.op("vector", lambda e: e.tensor_copy(
        base_sb[:, :], ps_bs[:, 0:64]), waits=[bmm0])
    pv = p.op("vector", lambda e: e.tensor_tensor(
        slotidx_sb[:, :], scn_all[:, :], ind_all[:, :], ALU.subtract),
        waits=[bev0])
    pv = p.op("vector", lambda e: e.tensor_tensor(
        slotidx_sb[:, :], slotidx_sb[:, :], segb[:, :], ALU.subtract),
        waits=[pv])
    for e_ in range(NEXP):
        for blk in range(NQB):
            c0 = e_ * 32 + blk * 4
            pv = p.op("vector", lambda e, e_=e_, blk=blk, c0=c0: e.tensor_scalar(
                slotidx_sb[:, c0:c0 + 4], slotidx_sb[:, c0:c0 + 4],
                base_sb[:, e_ * 8 + blk:e_ * 8 + blk + 1], None, ALU.add),
                waits=[pv])
    slfin = pv
    esD0i.close()

    selT_sb = [sbD0(f"selT_sb{i}", [128, 4 * SLOTB], BF16) for i in range(2)]
    gpk2 = [sbD0(f"gpk2{i}", [128, 8], BF16) for i in range(2)]
    ghi_f = sbD0("ghi_f", [128, 4], F32)
    glo_f = sbD0("glo_f", [128, 4], F32)
    xc_ev = [sbD0(f"xc_ev{i}", [128, SLOTB], BF16) for i in range(4)]

    xn_free = iota_ld
    psxc_free = [NONE_MS] * 4
    gc_free = [bev0, NONE_MS]
    ghi_free = [NONE_MS, NONE_MS]
    selT_free = [NONE_MS, NONE_MS]
    xcev_free = [NONE_MS] * 4

    def emit_selT(e_, blk):
        eb = e_ % 2
        last = NONE_MS
        for tc in range(4):
            for st in range(2):
                c0 = e_ * 32 + blk * 4
                last = p.op("vector", lambda e, tc=tc, st=st, c0=c0, eb=eb: e.tensor_scalar(
                    selT_sb[eb][:, tc * SLOTB + st * 128: tc * SLOTB + (st + 1) * 128],
                    iota2b[:, st * 128:(st + 1) * 128],
                    slotidx_sb[:, c0 + tc:c0 + tc + 1], ind_all[:, c0 + tc:c0 + tc + 1],
                    ALU.is_equal, ALU.mult),
                    waits=([slfin, selT_free[eb]] if tc == 0 and st == 0 else []))
        return last

    for blk in range(NQB):
        xn_ld = p.dma("sync", xn_tm_sb[:, :].rearrange("q (tc f) -> q tc f", tc=4),
                      xn_tm_dram[blk * 512:(blk + 1) * 512, :].rearrange(
                          "(tc q) f -> q tc f", q=128),
                      waits=[xn_free])
        bld_last = emit_selT(0, blk)
        gc_pend = None

        def emit_gc(pend):
            (pe_, peb, pg4) = pend
            lg = NONE_MS
            for st in range(2):
                gp = st % 2
                gmm = NONE_MS
                for tc in range(4):
                    gmm = p.op("tensor", lambda e, st=st, tc=tc, peb=peb, gp=gp: e.matmul(
                        ps_gc[gp][:, 0:2],
                        selT_sb[peb][:, tc * SLOTB + st * 128: tc * SLOTB + (st + 1) * 128],
                        gpk2[peb][:, tc * 2:(tc + 1) * 2], start=(tc == 0), stop=(tc == 3)),
                        waits=([pg4, gc_free[gp]] if tc == 0 else []))
                lg = gmm
                gcol = gc_sb[:, pe_ * 16 + blk * 2 + st: pe_ * 16 + blk * 2 + st + 1]
                gv1 = p.op("vector", lambda e, gcol=gcol, gp=gp: e.tensor_copy(
                    gcol, ps_gc[gp][:, 0:1]), waits=[gmm])
                gev = p.op("vector", lambda e, gcol=gcol, gp=gp: e.tensor_tensor(
                    gcol, gcol, ps_gc[gp][:, 1:2], ALU.add), waits=[gv1])
                gc_free[gp] = gev
            return lg

        for e_ in range(NEXP):
            eb = e_ % 2
            c0 = e_ * 32 + blk * 4
            mm_xc = NONE_MS
            for featc in range(DC):
                fb = featc % 4
                for tc in range(4):
                    w_m = []
                    if tc == 0:
                        w_m = [bld_last, psxc_free[fb]]
                        if featc == 0:
                            w_m.append(xn_ld)
                    mm_xc = p.op("tensor", lambda e, featc=featc, tc=tc, fb=fb, eb=eb: e.matmul(
                        ps_xc[fb][:, 0:SLOTB],
                        xn_tm_sb[:, tc * D + featc * 128: tc * D + (featc + 1) * 128],
                        selT_sb[eb][:, tc * SLOTB:(tc + 1) * SLOTB],
                        start=(tc == 0), stop=(tc == 3)), waits=w_m)
                xce = p.op("vector", lambda e, fb=fb: e.tensor_copy(
                    xc_ev[fb][:, :], ps_xc[fb][:, 0:SLOTB]),
                    waits=[mm_xc, xcev_free[fb], p.drain("scalar")])
                psxc_free[fb] = xce
                xcev_free[fb] = p.dma(
                    "scalar",
                    xc_dram[e_][:, featc * SLOT_E + blk * SLOTB:
                                featc * SLOT_E + (blk + 1) * SLOTB],
                    xc_ev[fb][:, :], waits=[xce])
            if gc_pend is not None:
                lgm = emit_gc(gc_pend)
                selT_free[gc_pend[1]] = lgm
                ghi_free[gc_pend[1]] = lgm
            else:
                selT_free[eb] = mm_xc
            if e_ + 1 < NEXP:
                bld_last = emit_selT(e_ + 1, blk)
            gview = gpk2[eb][:, :].rearrange("q (tc two) -> q tc two", two=2)
            g1_ = p.op("vector", lambda e, c0=c0, gview=gview: e.tensor_copy(
                gview[:, :, 0], wsel[:, c0:c0 + 4]), waits=[ghi_free[eb]])
            g2_ = p.op("vector", lambda e, gview=gview: e.tensor_copy(
                ghi_f[:, :], gview[:, :, 0]), waits=[g1_])
            g3_ = p.op("vector", lambda e, c0=c0: e.tensor_tensor(
                glo_f[:, :], wsel[:, c0:c0 + 4], ghi_f[:, :], ALU.subtract),
                waits=[g2_])
            g4_ = p.op("vector", lambda e, gview=gview: e.tensor_copy(
                gview[:, :, 1], glo_f[:, :]), waits=[g3_])
            gc_pend = (e_, eb, g4_)
        lgm = emit_gc(gc_pend)
        selT_free[gc_pend[1]] = lgm
        ghi_free[gc_pend[1]] = lgm
        xn_free = (PE, PE.v)
    gc_done = gc_free[1]
    xc_done = xcev_free[3]
    d0_gate = (PE, PE.v)
    esD0p.close()

    esD1 = contextlib.ExitStack()
    sbD1 = lambda name, shape, dt: esD1.enter_context(nc.sbuf_tensor(name, shape, dt))
    w1_sb = sbD1("w1_sb", [128, FC * 1024], BF16)
    w2_sb = sbD1("w2_sb", [128, FC * D], BF16)
    xnb = [sbD1(f"xnb{i}", [128, DC * 512], BF16) for i in range(2)]
    h_sb = sbD1("h_sb", [128, FC * 512], BF16)
    yst = [sbD1(f"yst{i}", [128, D], F32) for i in range(2)]
    ps_h = [esD1.enter_context(nc.psum_tensor(f"ps_h{i}", [128, 512], F32)) for i in range(4)]
    ps_y = [esD1.enter_context(nc.psum_tensor(f"ps_y{i}", [128, 512], F32)) for i in range(2)]

    w1_free = NONE_MS
    w2_free = NONE_MS
    xnb_free = [NONE_MS, NONE_MS]
    psy_free = [NONE_MS, NONE_MS]
    bank_guard = [NONE_MS] * 4
    mmh_gate = d0_gate
    y2_gate = NONE_MS
    w1_next_free = NONE_MS

    for e_ in range(NEXP):
        w1_lds = []
        for fc in range(FC):
            w1_lds.append(p.dma(
                "gpsimd",
                w1_sb[:, fc * 1024:(fc + 1) * 1024].rearrange("r (dc c) -> r dc c", dc=DC),
                w1_pk[e_, fc].rearrange("dc r c -> r dc c"),
                waits=([w1_free, y2_gate] if fc == 0 else [])))
        w2_lds = []
        for sb_ in range(NSB):
            xld = p.dma("sync",
                        xnb[sb_ % 2][:, :].rearrange("r (dc c) -> r dc c", dc=DC),
                        xc_dram[e_].rearrange("r (dc s) -> r dc s", dc=DC)[
                            :, :, sb_ * 512:(sb_ + 1) * 512],
                        waits=([xnb_free[sb_ % 2], mmh_gate]
                               + ([xc_done, gc_done] if e_ == 0 and sb_ == 0 else [])))
            relu_ms = {}
            for fc in range(FC):
                mmh = NONE_MS
                for dc in range(DC):
                    w_m = []
                    if dc == 0:
                        w_m = [xld, bank_guard[fc % 4]]
                        if sb_ == 0 and fc == 0:
                            w_m.append(w1_lds[FC - 1])
                    mmh = p.op("tensor", lambda e, fc=fc, dc=dc, sb_=sb_: e.matmul(
                        ps_h[fc % 4][:, :],
                        w1_sb[:, fc * 1024 + dc * 128: fc * 1024 + (dc + 1) * 128],
                        xnb[sb_ % 2][:, dc * 512:(dc + 1) * 512],
                        start=(dc == 0), stop=(dc == DC - 1)), waits=w_m)
                    if fc == 0 and dc == 0:
                        mmh_gate = mmh
                        if sb_ == 0:
                            for fc2 in range(FC):
                                w2_lds.append(p.dma(
                                    "gpsimd", w2_sb[:, fc2 * D:(fc2 + 1) * D],
                                    w2_in[e_, fc2 * 128:(fc2 + 1) * 128, :],
                                    waits=([w2_free, mmh] if fc2 == 0 else [])))
                relu_ms[fc] = p.op("scalar", lambda e, fc=fc: e.activation(
                    h_sb[:, fc * 512:(fc + 1) * 512], ps_h[fc % 4][:, :], AF.Relu),
                    waits=[mmh])
                bank_guard[fc % 4] = relu_ms[fc]
            xnb_free[sb_ % 2] = (PE, PE.v)
            if sb_ == NSB - 1:
                w1_next_free = (PE, PE.v)
            for tt in range(4):
                ti = sb_ * 4 + tt
                mmy = [NONE_MS, NONE_MS]
                for half in range(2):
                    for fc in range(FC):
                        w_m = [relu_ms[fc]]
                        if fc == 0:
                            w_m.append(psy_free[half])
                            if sb_ == 0 and tt == 0:
                                w_m.append(w2_lds[FC - 1])
                        mmy[half] = p.op("tensor", lambda e, fc=fc, half=half, tt=tt: e.matmul(
                            ps_y[half][:, :],
                            h_sb[:, fc * 512 + tt * 128: fc * 512 + (tt + 1) * 128],
                            w2_sb[:, fc * D + half * 512: fc * D + (half + 1) * 512],
                            start=(fc == 0), stop=(fc == FC - 1)), waits=w_m)
                y1 = p.op("vector", lambda e, ti=ti, e_=e_: e.tensor_scalar(
                    yst[ti % 2][:, 0:512], ps_y[0][:, :],
                    gc_sb[:, e_ * 16 + ti: e_ * 16 + ti + 1], None, ALU.mult),
                    waits=[mmy[0], p.drain("gpsimd"), gc_done])
                y2 = p.op("vector", lambda e, ti=ti, e_=e_: e.tensor_scalar(
                    yst[ti % 2][:, 512:1024], ps_y[1][:, :],
                    gc_sb[:, e_ * 16 + ti: e_ * 16 + ti + 1], None, ALU.mult),
                    waits=[mmy[1]])
                psy_free[0] = y1
                psy_free[1] = y2
                if ti == NST_E - 3:
                    y2_gate = y2
                p.dma("gpsimd", y_dram[e_, ti], yst[ti % 2][:, :],
                      waits=[y1, y2])
        w1_free = w1_next_free
        w2_free = (PE, PE.v)

    p.barrier()
    esD1.close()
    esD0.close()

    esD2 = contextlib.ExitStack()
    sbD2 = lambda name, shape, dt: esD2.enter_context(nc.sbuf_tensor(name, shape, dt))
    y_sb = [sbD2(f"y_sb{i}", [128, 16 * D], BF16) for i in range(2)]
    selT_d = [sbD2(f"selT_d{i}", [128, 128], BF16) for i in range(2)]
    sel_ev = [sbD2(f"sel_ev{i}", [128, 128], BF16) for i in range(2)]
    oadd = [sbD2(f"oadd{i}", [128, D], F32) for i in range(2)]
    ps_sc = [esD2.enter_context(nc.psum_tensor(f"ps_sc{i}", [128, 512], F32))
             for i in range(4)]
    ps_tT = [esD2.enter_context(nc.psum_tensor(f"ps_tT{i}", [128, 1024], BF16))
             for i in range(2)]

    ysb_free = [NONE_MS, NONE_MS]
    y_gate = NONE_MS
    bld_free = [NONE_MS, NONE_MS]
    sev_free = [NONE_MS, NONE_MS]
    psc_free = [NONE_MS] * 4
    oadd_free = [NONE_MS, NONE_MS]
    for blk in range(NQB):
        bb = blk % 2
        y_lds = NONE_MS
        for e_ in range(NEXP):
            y_lds = p.dma(
                "sync",
                y_sb[bb][:, e_ * 2 * D:(e_ + 1) * 2 * D].rearrange(
                    "q (st f) -> q st f", st=2),
                y_dram[e_, blk * 2:(blk + 1) * 2].rearrange("st q f -> q st f"),
                waits=([ysb_free[bb], y_gate] if e_ == 0 else []))
        for tc in range(4):
            mm_sc = NONE_MS
            pend = None
            for idx in range(17):
                if idx < 16:
                    e_, st = idx // 2, idx % 2
                    ib = idx % 2
                    c0 = e_ * 32 + blk * 4 + tc
                    bld = p.op("vector", lambda e, st=st, c0=c0, ib=ib: e.tensor_scalar(
                        selT_d[ib][:, :], iota2b[:, st * 128:(st + 1) * 128],
                        slotidx_sb[:, c0:c0 + 1], ind_all[:, c0:c0 + 1],
                        ALU.is_equal, ALU.mult),
                        waits=[bld_free[ib]])
                    tp = p.op("tensor", lambda e, ib=ib: e.transpose(
                        ps_tT[ib][:, 0:128], selT_d[ib][:, :], ident_s[:, :]),
                        waits=[bld, sev_free[ib]])
                    bld_free[ib] = tp
                    sev = p.op("vector", lambda e, ib=ib: e.tensor_copy(
                        sel_ev[ib][:, :], ps_tT[ib][:, 0:128]), waits=[tp])
                if pend is not None:
                    (pidx, pib, pe_, pst, psev) = pend
                    tb = tc % 2
                    for half in range(2):
                        w_m = [psev]
                        if pidx == 0:
                            w_m += [psc_free[2 * tb + half], y_lds, p.drain("sync")]
                        mm_sc = p.op("tensor", lambda e, pidx=pidx, half=half, pib=pib,
                                     bb=bb, pe_=pe_, pst=pst, tb=tb: e.matmul(
                            ps_sc[2 * tb + half][:, :], sel_ev[pib][:, :],
                            y_sb[bb][:, (pe_ * 2 + pst) * D + half * 512:
                                      (pe_ * 2 + pst) * D + (half + 1) * 512],
                            start=(pidx == 0), stop=(pidx == 15)), waits=w_m)
                    sev_free[pib] = mm_sc
                    if tc == 0 and pidx == 0:
                        y_gate = mm_sc
                if idx < 16:
                    pend = (idx, ib, e_, st, sev)
            ob = tc % 2
            tb = tc % 2
            o1 = p.op("vector", lambda e, ob=ob, tb=tb: e.tensor_copy(
                oadd[ob][:, 0:512], ps_sc[2 * tb][:, :]),
                waits=[mm_sc, oadd_free[ob]])
            o2 = p.op("vector", lambda e, ob=ob, tb=tb: e.tensor_copy(
                oadd[ob][:, 512:1024], ps_sc[2 * tb + 1][:, :]))
            psc_free[2 * tb] = o1
            psc_free[2 * tb + 1] = o2
            ti = blk * 4 + tc
            acc = p.dma("gpsimd", out_t[ti * 128:(ti + 1) * 128, :],
                        oadd[ob][:, :], accum_op=ALU.add,
                        waits=[o1, o2, p.drain("gpsimd")])
            oadd_free[ob] = acc
        ysb_free[bb] = (PE, PE.v)

    p.wt("gpsimd", (p.d["gpsimd"], p.d["gpsimd"].v))
    p.wt("sync", (DS, DS.v))
    p.wt("scalar", (DSC, DSC.v))
    esD2.close()
    esDR.close()

    es0.close()

    with nc.Block() as block:
        p.emit(block)
    return nc


_NC_CACHE = None


def kernel(L0_attn_x, L0_attn_w_qkv, L0_attn_w_out, L0_attn_b_out,
           L0_ffn_Wr, L0_ffn_W1, L0_ffn_W2):
    global _NC_CACHE
    X = np.ascontiguousarray(np.asarray(L0_attn_x, dtype=np.float32))
    Wqkv = np.ascontiguousarray(np.asarray(L0_attn_w_qkv, dtype=np.float32))
    Wout = np.ascontiguousarray(np.asarray(L0_attn_w_out, dtype=np.float32))
    Bout = np.ascontiguousarray(np.asarray(L0_attn_b_out, dtype=np.float32)).reshape(1, D)
    Wr = np.ascontiguousarray(np.asarray(L0_ffn_Wr, dtype=np.float32))
    W1 = np.asarray(L0_ffn_W1, dtype=np.float32)
    W2 = np.ascontiguousarray(np.asarray(L0_ffn_W2, dtype=np.float32))
    wqkv_pk = np.ascontiguousarray(Wqkv.reshape(DC, 128, 24, 128).transpose(2, 0, 1, 3))
    w1_pk = np.ascontiguousarray(
        W1.reshape(NEXP, DC, 128, FC, 128).transpose(0, 3, 1, 2, 4))
    if _NC_CACHE is None:
        _NC_CACHE = build_nc()
    nc = _NC_CACHE
    in_maps = []
    for c in range(NCORES):
        xc = np.ascontiguousarray(X[c])
        in_maps.append({
            "x_own": xc,
            "x_fm": np.ascontiguousarray(xc.T),
            "wqkv_pk": wqkv_pk,
            "w_out": Wout,
            "b_out": Bout,
            "wr_in": Wr,
            "w1_pk": w1_pk,
            "w2_in": W2,
        })
    res = run_bass_kernel_spmd(nc, in_maps, core_ids=list(range(NCORES)))
    out = np.stack([np.asarray(res.results[c]["out"]) for c in range(NCORES)], axis=0)
    return out


# version 08
# speedup vs baseline: 1.1363x; 1.0035x over previous
import contextlib
import numpy as np
import ml_dtypes

import concourse.bass as bass
import concourse.mybir as mybir
from concourse.bass_utils import run_bass_kernel_spmd

F32 = mybir.dt.float32
BF16 = mybir.dt.bfloat16
I32 = mybir.dt.int32
AF = mybir.ActivationFunctionType
ALU = mybir.AluOpType
AX = mybir.AxisListType

NCORES = 8
TOK = 4096
D = 1024
DC = 8
NPAIR = 8
NQB = 8
QBS = 512
NKC = 32
NTT = 32
FFN = 4096
FC = 32
NEXP = 8
CAP = 10240
NST = CAP // 128
TT_ALL = 32768
EPS = 1e-5
INVF = float(1 << 24)
BIS_ROUNDS = 34
SLOTB = 256
SLOT_E = NQB * SLOTB
NST_E = SLOT_E // 128
NSB = SLOT_E // 512


class Sem:
    def __init__(self, nc, name):
        self.h = nc.alloc_semaphore(name)
        self.v = 0

    def inc(self, n=1):
        self.v += n
        return self.v


NONE_MS = None


class Prog:
    ENGINES = ("sync", "tensor", "vector", "scalar", "gpsimd")

    def __init__(self, nc):
        self.nc = nc
        self.items = []
        self.e = {n: Sem(nc, "se_" + n) for n in self.ENGINES}
        self.d = {n: Sem(nc, "sd_" + n) for n in ("sync", "scalar", "gpsimd")}

    @staticmethod
    def _clean(waits):
        return tuple(w for w in waits if w is not None and w[1] > 0)

    def op(self, eng, fn, waits=()):
        s = self.e[eng]
        v = s.inc(1)
        self.items.append((eng, self._clean(waits), fn, ((s, 1),)))
        return (s, v)

    def dma(self, eng, out, in_, waits=(), **kw):
        s = self.d[eng]
        v = s.inc(16)
        self.items.append((eng, self._clean(waits),
                           lambda e, out=out, in_=in_, kw=kw: e.dma_start(out=out, in_=in_, **kw),
                           ((s, 16),)))
        return (s, v)

    def idma(self, out, out_off, in_, in_off, bounds, waits=(), **kw):
        s = self.d["gpsimd"]
        v = s.inc(16)

        def fn(e, out=out, oo=out_off, in_=in_, io=in_off, b=bounds, kw=kw):
            try:
                return e.indirect_dma_start(out, oo, in_, io, bounds_check=b,
                                            oob_is_err=False, **kw)
            except Exception as ex:
                raise RuntimeError(
                    f"idma failed: out={out.tensor.name if hasattr(out, 'tensor') else out} "
                    f"in={in_.tensor.name if hasattr(in_, 'tensor') else in_} "
                    f"kw={kw}: {ex}") from ex
        self.items.append(("gpsimd", self._clean(waits), fn, ((s, 16),)))
        return (s, v)

    def wt(self, eng, *ms):
        self.items.append((eng, self._clean(ms), lambda e: e.nop(), ()))

    def drain(self, eng):
        s = self.d[eng]
        return (s, s.v)

    def barrier(self):
        for n in ("sync", "scalar", "gpsimd"):
            self.wt(n, (self.d[n], self.d[n].v))
        vals = []
        for n in self.ENGINES:
            s = self.e[n]
            v = s.inc(1)
            self.items.append((n, (), lambda e: e.nop(), ((s, 1),)))
            vals.append((s, v))
        for n in self.ENGINES:
            self.items.append((n, tuple(vals), lambda e: e.nop(), ()))

    def emit(self, block):
        for ename in self.ENGINES:
            def make(ename):
                def run(eng):
                    for (e, waits, fn, incs) in self.items:
                        if e != ename:
                            continue
                        for (sem, val) in waits:
                            eng.wait_ge(sem.h, val)
                        ins = fn(eng)
                        for (sem, n) in incs:
                            ins = ins.then_inc(sem.h, n)
                return run
            getattr(block, ename)(make(ename))


def build_nc():
    nc = bass.Bass("TRN2", target_bir_lowering=False, debug=False)

    x_own = nc.dram_tensor("x_own", [TOK, D], F32, kind="ExternalInput").ap()
    x_fm = nc.dram_tensor("x_fm", [D, TOK], F32, kind="ExternalInput").ap()
    wqkv_pk = nc.dram_tensor("wqkv_pk", [24, DC, 128, 128], F32, kind="ExternalInput").ap()
    w_out = nc.dram_tensor("w_out", [D, D], F32, kind="ExternalInput").ap()
    b_out = nc.dram_tensor("b_out", [1, D], F32, kind="ExternalInput").ap()
    wr_in = nc.dram_tensor("wr_in", [D, NEXP], F32, kind="ExternalInput").ap()
    w1_pk = nc.dram_tensor("w1_pk", [NEXP, FC, DC, 128, 128], F32, kind="ExternalInput").ap()
    w2_in = nc.dram_tensor("w2_in", [NEXP, FFN, D], F32, kind="ExternalInput").ap()
    out_t = nc.dram_tensor("out", [TOK, D], F32, kind="ExternalOutput").ap()

    q_dram = nc.dram_tensor("q_dram", [NPAIR, 128, TOK], BF16, kind="Internal").ap()
    k_dram = nc.dram_tensor("k_dram", [NPAIR, 128, TOK], BF16, kind="Internal").ap()
    v_dram = nc.dram_tensor("v_dram", [NPAIR, NKC, 128, 130], BF16, kind="Internal").ap()
    nb_dram = nc.dram_tensor("nb_dram", [16, 1], F32, kind="Internal").ap()
    cw_dram = nc.dram_tensor("cw_dram", [1, NEXP], F32, kind="Internal").ap()
    cin_g = nc.dram_tensor("cin_g", [128, 256], F32, kind="Internal").ap()
    cout_g = nc.dram_tensor("cout_g", [NCORES * 128, 256], F32, kind="Internal").ap()
    ctot_dram = nc.dram_tensor("ctot_dram", [1, NEXP], F32, kind="Internal").ap()
    xn_dram = nc.dram_tensor("xn_dram", [128, DC * TOK], BF16, kind="Internal").ap()
    xn_tm_dram = nc.dram_tensor("xn_tm_dram", [TOK, D], BF16, kind="Internal").ap()
    xc_dram = nc.dram_tensor("xc_dram", [NEXP, 128, DC * SLOT_E], BF16,
                             kind="Internal").ap()
    y_dram = nc.dram_tensor("y_dram", [NEXP, NST_E, 128, D], BF16,
                            kind="Internal").ap()

    ident_c = nc.inline_tensor(np.eye(128, dtype=ml_dtypes.bfloat16), name="ident_c").ap()
    pairsel_np = np.zeros((128, 2), dtype=ml_dtypes.bfloat16)
    pairsel_np[0:64, 0] = 1
    pairsel_np[64:128, 1] = 1
    pairsel_c = nc.inline_tensor(pairsel_np, name="pairsel_c").ap()
    masks_np = np.zeros((128, 4 * QBS), dtype=ml_dtypes.bfloat16)
    kparr = np.arange(128)[:, None]
    qfarr = np.arange(QBS)[None, :]
    for m in range(4):
        masks_np[:, m * QBS:(m + 1) * QBS] = (qfarr >= kparr + 128 * m).astype(ml_dtypes.bfloat16)
    masks_c = nc.inline_tensor(masks_np, name="masks_c").ap()
    esel_np = np.zeros((16, 8 * 128), dtype=ml_dtypes.bfloat16)
    for c in range(8):
        esel_np[2 * c, c * 128:c * 128 + 64] = 1
        esel_np[2 * c + 1, c * 128 + 64:c * 128 + 128] = 1
    esel_c = nc.inline_tensor(esel_np, name="esel_c").ap()
    ones_row_c = nc.inline_tensor(np.ones((1, 128), np.float32), name="ones_row_c").ap()
    tri_np = (np.arange(128)[:, None] < np.arange(128)[None, :]).astype(np.float32)
    tri_c = nc.inline_tensor(tri_np, name="tri_c").ap()
    iota2_c = nc.inline_tensor(np.arange(SLOTB, dtype=np.float32)[None, :],
                               name="iota2_c").ap()

    p = Prog(nc)
    PE, VE, AC, GP = p.e["tensor"], p.e["vector"], p.e["scalar"], p.e["gpsimd"]
    DS, DSC, DG = p.d["sync"], p.d["scalar"], p.d["gpsimd"]

    es0 = contextlib.ExitStack()
    sb0 = lambda name, shape, dt: es0.enter_context(nc.sbuf_tensor(name, shape, dt))

    pairsel_s = sb0("pairsel_s", [128, 2], BF16)
    ident_s = sb0("ident_s", [128, 128], BF16)
    onesr_s = sb0("onesr_s", [1, 128], F32)
    negb_bc = sb0("negb_bc", [128, 16], F32)
    negb_pair = sb0("negb_pair", [128, 8], F32)
    stats_m = sb0("stats_m", [128, NTT], F32)
    stats_r = sb0("stats_r", [128, NTT], F32)
    eps_col = sb0("eps_col", [128, 1], F32)
    G_own = sb0("G_own", [128, 256], F32)
    wsel = sb0("wsel", [128, 256], F32)

    p.dma("sync", pairsel_s[:, :], pairsel_c[:, :])
    p.dma("sync", ident_s[:, :], ident_c[:, :])
    c_ld = p.dma("sync", onesr_s[:, :], ones_row_c[:, :])
    eps_ms = p.op("vector", lambda e: e.memset(eps_col[:, :], EPS))

    esA = contextlib.ExitStack()
    sbA = lambda name, shape, dt: esA.enter_context(nc.sbuf_tensor(name, shape, dt))
    xn_fm = sbA("xn_fm", [128, DC * TOK], BF16)
    xt = [sbA(f"xt{i}", [128, D], F32) for i in range(3)]
    scrA = sbA("scrA", [128, D], F32)
    sq_s = sbA("sq_s", [128, D], F32)
    xn32 = sbA("xn32", [128, D], F32)
    xnbf = [sbA(f"xnbf{i}", [128, D], BF16) for i in range(2)]
    st_sum = sbA("st_sum", [128, NTT], F32)
    st_sq = sbA("st_sq", [128, NTT], F32)
    st_var = sbA("st_var", [128, NTT], F32)
    st_sd = sbA("st_sd", [128, NTT], F32)
    m2_s = sbA("m2_s", [128, NTT], F32)
    ps_tr = [esA.enter_context(nc.psum_tensor(f"ps_tr{i}", [128, 128], BF16)) for i in range(2)]
    ps_qkv = [esA.enter_context(nc.psum_tensor(f"ps_qkv{i}", [128, QBS], F32)) for i in range(4)]
    ps_nrm = esA.enter_context(nc.psum_tensor("ps_nrm", [128, QBS], F32))


    ld_x = [NONE_MS] * NTT
    xn_done = [NONE_MS] * NTT
    xnbf_free = [NONE_MS, NONE_MS]
    trev = [NONE_MS, NONE_MS]
    ld_gate = [c_ld]
    for i in range(NTT):
        ld_x[i] = p.dma("sync", xt[i % 3][:, :], x_own[i * 128:(i + 1) * 128, :],
                        waits=ld_gate + ([] if i < 3 else [xn_done[i - 3]]))
        v_sum = p.op("vector", lambda e, i=i: e.tensor_scalar(
            scrA[:, :], xt[i % 3][:, :], 1.0, None, ALU.mult, ALU.add,
            accum_out=st_sum[:, i:i + 1]), waits=[ld_x[i]])
        a_sq = p.op("scalar", lambda e, i=i: e.activation(
            sq_s[:, :], xt[i % 3][:, :], AF.Square, accum_out=st_sq[:, i:i + 1]),
            waits=[ld_x[i]])
        ld_gate = [v_sum, a_sq]
        v_m = p.op("vector", lambda e, i=i: e.tensor_scalar(
            stats_m[:, i:i + 1], st_sum[:, i:i + 1], 1.0 / D, None, ALU.mult),
            waits=[v_sum])
        v_e2 = p.op("vector", lambda e, i=i: e.tensor_scalar(
            st_var[:, i:i + 1], st_sq[:, i:i + 1], 1.0 / D, None, ALU.mult),
            waits=[a_sq])
        v_m2 = p.op("vector", lambda e, i=i: e.tensor_tensor(
            m2_s[:, i:i + 1], stats_m[:, i:i + 1], stats_m[:, i:i + 1], ALU.mult),
            waits=[v_m])
        v_var = p.op("vector", lambda e, i=i: e.tensor_tensor(
            st_var[:, i:i + 1], st_var[:, i:i + 1], m2_s[:, i:i + 1], ALU.subtract),
            waits=[v_m2, v_e2])
        a_sd = p.op("scalar", lambda e, i=i: e.activation(
            st_sd[:, i:i + 1], st_var[:, i:i + 1], AF.Sqrt, bias=eps_col[:, 0:1]),
            waits=[v_var, eps_ms])
        v_r = p.op("vector", lambda e, i=i: e.reciprocal(
            stats_r[:, i:i + 1], st_sd[:, i:i + 1]), waits=[a_sd])
        v_xn = p.op("vector", lambda e, i=i: e.tensor_scalar(
            xn32[:, :], xt[i % 3][:, :], stats_m[:, i:i + 1], stats_r[:, i:i + 1],
            ALU.subtract, ALU.mult), waits=[v_r])
        v_bf = p.op("vector", lambda e, i=i: e.tensor_copy(
            xnbf[i % 2][:, :], xn32[:, :]),
            waits=[v_xn, xnbf_free[i % 2], p.drain("scalar")])
        p.dma("scalar", xn_tm_dram[i * 128:(i + 1) * 128, :], xnbf[i % 2][:, :],
              waits=[v_bf])
        last = NONE_MS
        for dc in range(DC):
            tp = p.op("tensor", lambda e, i=i, dc=dc: e.transpose(
                ps_tr[dc % 2][:, :], xnbf[i % 2][:, dc * 128:(dc + 1) * 128], ident_s[:, :]),
                waits=[v_bf, trev[dc % 2]])
            trev[dc % 2] = p.op("vector", lambda e, i=i, dc=dc: e.tensor_copy(
                xn_fm[:, dc * TOK + i * 128: dc * TOK + (i + 1) * 128], ps_tr[dc % 2][:, :]),
                waits=[tp])
            last = trev[dc % 2]
            if dc == DC - 1:
                xnbf_free[i % 2] = tp
        xn_done[i] = last
    xn_all = xn_done[NTT - 1]

    esA2 = contextlib.ExitStack()
    sbA2 = lambda name, shape, dt: esA2.enter_context(nc.sbuf_tensor(name, shape, dt))
    xfmb = [sbA2(f"xfmb{i}", [128, TOK], F32) for i in range(2)]
    wr_s = sbA2("wr_s", [128, DC * NEXP], F32)
    ones_col = sbA2("ones_col", [128, 1], F32)
    cwb = sbA2("cwb", [128, NEXP], F32)
    lgraw = sbA2("lgraw", [128, 256], F32)
    lgev = sbA2("lgev", [128, 256], F32)
    cwev = sbA2("cwev", [1, NEXP], F32)
    exps_a = sbA2("exps_a", [128, NEXP], F32)
    probs_a = sbA2("probs_a", [128, NEXP], F32)
    ge8_a = sbA2("ge8_a", [128, NEXP], F32)
    pm8_a = sbA2("pm8_a", [128, NEXP], F32)
    gm8_a = sbA2("gm8_a", [128, NEXP], F32)
    scr8 = sbA2("scr8", [128, NEXP], F32)
    lg2 = sbA2("lg2", [128, NEXP], F32)
    rt = sbA2("rt", [128, 8 * NTT], F32)
    ps_lg = [ps_qkv[0][:, 0:256], ps_qkv[1][:, 0:256]]
    ps_cw = ps_nrm

    wr_ld = NONE_MS
    for dc in range(DC):
        wr_ld = p.dma("sync", wr_s[:, dc * NEXP:(dc + 1) * NEXP],
                      wr_in[dc * 128:(dc + 1) * 128, :])
    oc_ms = p.op("vector", lambda e: e.memset(ones_col[:, :], 1.0))
    lg_zero = p.op("vector", lambda e: e.memset(lgraw[:, :], 0.0))

    xfm_ld = [NONE_MS, NONE_MS]
    xfm_free = [NONE_MS, NONE_MS]
    xfm_gate = wr_ld
    lg_acc = lg_zero
    cw_mm = NONE_MS
    psg_free = [NONE_MS, NONE_MS]
    for dc in range(DC):
        xfm_ld[dc % 2] = p.dma("sync", xfmb[dc % 2][:, :],
                               x_fm[dc * 128:(dc + 1) * 128, :],
                               waits=[xfm_free[dc % 2], xfm_gate])
        cw_mm = p.op("tensor", lambda e, dc=dc: e.matmul(
            ps_cw[0:1, 0:NEXP], ones_col[:, :], wr_s[:, dc * NEXP:(dc + 1) * NEXP],
            start=(dc == 0), stop=(dc == DC - 1)),
            waits=([wr_ld, oc_ms, xn_all] if dc == 0 else []))
        mmlast = NONE_MS
        for t in range(NTT):
            mmlast = p.op("tensor", lambda e, dc=dc, t=t: e.matmul(
                ps_lg[dc % 2][:, t * NEXP:(t + 1) * NEXP],
                xfmb[dc % 2][:, t * 128:(t + 1) * 128],
                wr_s[:, dc * NEXP:(dc + 1) * NEXP], start=True, stop=True),
                waits=([xfm_ld[dc % 2], psg_free[dc % 2]] if t == 0 else []))
            if t == 0:
                xfm_gate = mmlast
        xfm_free[dc % 2] = mmlast
        evg = p.op("vector", lambda e, dc=dc: e.tensor_copy(
            lgev[:, :], ps_lg[dc % 2][:, :]), waits=[mmlast])
        psg_free[dc % 2] = evg
        lg_acc = p.op("vector", lambda e: e.tensor_tensor(
            lgraw[:, :], lgraw[:, :], lgev[:, :], ALU.add), waits=[evg, lg_acc])
    cw_ev = p.op("vector", lambda e: e.tensor_copy(cwev[0:1, :], ps_cw[0:1, 0:NEXP]),
                 waits=[cw_mm])
    cw_st = p.dma("gpsimd", cw_dram[0:1, :], cwev[0:1, :], waits=[cw_ev])
    cw_bc = p.dma("sync", cwb[:, :], cw_dram[0:1, :].to_broadcast([128, NEXP]),
                  waits=[cw_st])

    g_done = NONE_MS
    for t in range(NTT):
        c0 = 8 * t
        w0 = [lg_acc, cw_bc] if t == 0 else [g_done]
        v1 = p.op("vector", lambda e, t=t: e.tensor_scalar(
            scr8[:, :], cwb[:, :], stats_m[:, t:t + 1], None, ALU.mult), waits=w0)
        v2 = p.op("vector", lambda e, t=t: e.tensor_tensor(
            lg2[:, :], lgraw[:, t * NEXP:(t + 1) * NEXP], scr8[:, :], ALU.subtract),
            waits=[v1])
        v3 = p.op("vector", lambda e, t=t: e.tensor_scalar(
            lg2[:, :], lg2[:, :], stats_r[:, t:t + 1], None, ALU.mult), waits=[v2])
        v4 = p.op("vector", lambda e, t=t, c0=c0: e.reduce_max(
            rt[:, c0:c0 + 1], lg2[:, :], axis=AX.X, negate=True), waits=[v3])
        a1 = p.op("scalar", lambda e, t=t, c0=c0: e.activation(
            exps_a[:, :], lg2[:, :], AF.Exp, bias=rt[:, c0:c0 + 1],
            accum_out=rt[:, c0 + 1:c0 + 2]), waits=[v4])
        v5 = p.op("vector", lambda e, c0=c0: e.reciprocal(
            rt[:, c0 + 2:c0 + 3], rt[:, c0 + 1:c0 + 2]), waits=[a1])
        v6 = p.op("vector", lambda e, c0=c0: e.tensor_scalar(
            probs_a[:, :], exps_a[:, :], rt[:, c0 + 2:c0 + 3], None, ALU.mult),
            waits=[v5])
        v7 = p.op("vector", lambda e, c0=c0: e.reduce_max(
            rt[:, c0 + 3:c0 + 4], probs_a[:, :], axis=AX.X), waits=[v6])
        v8 = p.op("vector", lambda e, c0=c0: e.tensor_scalar(
            ge8_a[:, :], probs_a[:, :], rt[:, c0 + 3:c0 + 4], None, ALU.is_ge,
            ALU.add, accum_out=rt[:, c0 + 4:c0 + 5]), waits=[v7])
        v9 = p.op("vector", lambda e: e.tensor_tensor(
            pm8_a[:, :], probs_a[:, :], ge8_a[:, :], ALU.subtract), waits=[v8])
        v10 = p.op("vector", lambda e, c0=c0: e.reduce_max(
            rt[:, c0 + 5:c0 + 6], pm8_a[:, :], axis=AX.X), waits=[v9])
        v11 = p.op("vector", lambda e, c0=c0: e.tensor_scalar(
            rt[:, c0 + 4:c0 + 5], rt[:, c0 + 4:c0 + 5], 2.0, None, ALU.is_ge),
            waits=[v10])
        v12 = p.op("vector", lambda e, c0=c0: e.tensor_tensor(
            rt[:, c0 + 7:c0 + 8], rt[:, c0 + 3:c0 + 4], rt[:, c0 + 5:c0 + 6],
            ALU.subtract), waits=[v11])
        v13 = p.op("vector", lambda e, c0=c0: e.tensor_tensor(
            rt[:, c0 + 7:c0 + 8], rt[:, c0 + 4:c0 + 5], rt[:, c0 + 7:c0 + 8],
            ALU.mult), waits=[v12])
        v14 = p.op("vector", lambda e, c0=c0: e.tensor_tensor(
            rt[:, c0 + 6:c0 + 7], rt[:, c0 + 5:c0 + 6], rt[:, c0 + 7:c0 + 8],
            ALU.add), waits=[v13])
        v15 = p.op("vector", lambda e, c0=c0: e.tensor_scalar(
            gm8_a[:, :], probs_a[:, :], rt[:, c0 + 6:c0 + 7], None, ALU.is_ge),
            waits=[v14])
        g_done = p.op("vector", lambda e, t=t: e.tensor_tensor(
            G_own[:, :].rearrange("q (e t) -> q e t", e=NEXP)[:, :, t],
            probs_a[:, :], gm8_a[:, :], ALU.mult), waits=[v15])

    g_ship = p.dma("gpsimd", cin_g[:, :], G_own[:, :], waits=[g_done])
    cc1 = p.op("gpsimd", lambda e: e.collective_compute(
        "AllGather", ALU.bypass, replica_groups=[list(range(NCORES))],
        ins=[cin_g], outs=[cout_g]), waits=[g_ship])

    esB = contextlib.ExitStack()
    sbB = lambda name, shape, dt: esB.enter_context(nc.sbuf_tensor(name, shape, dt))
    wsl = [sbB(f"wsl{i}", [128, DC * 128], BF16) for i in range(2)]
    ev_bf = [sbB(f"ev_bf{i}", [128, QBS], BF16) for i in range(4)]
    sq_bf = [sbB(f"sq_bf{i}", [128, QBS], BF16) for i in range(2)]
    nmx = sbB("nmx", [2, 128], F32)
    vas = [sbB(f"vas{i}", [128, 4 * 130], BF16) for i in range(2)]
    nb16 = sbB("nb16", [16, 4], F32)
    ps_vtr = ps_tr[0]

    xn_st = p.dma("sync", xn_dram[:, :], xn_fm[:, :], waits=[xn_all])

    wsl_ld = [NONE_MS, NONE_MS]
    wsl_gate = NONE_MS
    nmm_pend = None
    sq_guard = [NONE_MS, NONE_MS]
    ev_tp_guard = [NONE_MS] * 4
    psq_free = [[psg_free[0]], [psg_free[1]], [NONE_MS], [NONE_MS]]
    nrm_mm_prev = NONE_MS
    nrm_red_prev = cw_ev
    vtr_prev = NONE_MS

    for s in range(24):
        buf = s % 2
        wsl_ld[buf] = p.dma("gpsimd",
                            wsl[buf][:, :].rearrange("r (dc c) -> r dc c", dc=DC),
                            wqkv_pk[s].rearrange("dc r c -> r dc c"),
                            waits=([] if s == 0 else [wsl_gate]))
        for qb in range(NQB):
            par = (s * 8 + qb) % 4
            mm = NONE_MS
            for dc in range(DC):
                w_mm = []
                if dc == 0:
                    w_mm = [wsl_ld[buf]] + psq_free[par]
                    if s == 0 and qb == 0:
                        w_mm.append(xn_all)
                mm = p.op("tensor", lambda e, buf=buf, qb=qb, dc=dc, par=par: e.matmul(
                    ps_qkv[par][:, :], wsl[buf][:, dc * 128:(dc + 1) * 128],
                    xn_fm[:, dc * TOK + qb * QBS: dc * TOK + (qb + 1) * QBS],
                    start=(dc == 0), stop=(dc == DC - 1)), waits=w_mm)
                if qb == 0 and dc == 0:
                    wsl_gate = mm
            if s < 16:
                dst = q_dram if s < 8 else k_dram
                pr = s % 8
                ev = p.op("vector", lambda e, par=par: e.tensor_copy(
                    ev_bf[par][:, :], ps_qkv[par][:, :]),
                    waits=[mm, p.drain("scalar")])
                p.dma("scalar", dst[pr][:, qb * QBS:(qb + 1) * QBS], ev_bf[par][:, :],
                      waits=[ev])
                sqb = (s * 8 + qb) % 2
                a_sq2 = p.op("scalar", lambda e, par=par, sqb=sqb: e.activation(
                    sq_bf[sqb][:, :], ps_qkv[par][:, :], AF.Square),
                    waits=[mm, ev, sq_guard[sqb]])
                psq_free[par] = [ev, a_sq2]
                if nmm_pend is not None:
                    (pa, psqb, pcol) = nmm_pend
                    nmm = p.op("tensor", lambda e, psqb=psqb: e.matmul(
                        ps_nrm[0:2, :], pairsel_s[:, :], sq_bf[psqb][:, :],
                        start=True, stop=True), waits=[pa, nrm_red_prev])
                    sq_guard[psqb] = nmm
                    nrm_red_prev = p.op("vector", lambda e, pcol=pcol: e.reduce_max(
                        nmx[0:2, pcol:pcol + 1], ps_nrm[0:2, :], axis=AX.X),
                        waits=[nmm])
                nmm_pend = (a_sq2, sqb, s * 8 + qb)
            else:
                if nmm_pend is not None:
                    (pa, psqb, pcol) = nmm_pend
                    nmm = p.op("tensor", lambda e, psqb=psqb: e.matmul(
                        ps_nrm[0:2, :], pairsel_s[:, :], sq_bf[psqb][:, :],
                        start=True, stop=True), waits=[pa, nrm_red_prev])
                    sq_guard[psqb] = nmm
                    nrm_red_prev = p.op("vector", lambda e, pcol=pcol: e.reduce_max(
                        nmx[0:2, pcol:pcol + 1], ps_nrm[0:2, :], axis=AX.X),
                        waits=[nmm])
                    nmm_pend = None
                pr = s - 16
                ev = p.op("vector", lambda e, par=par: e.tensor_copy(
                    ev_bf[par][:, :], ps_qkv[par][:, :]),
                    waits=[mm, p.drain("scalar"), ev_tp_guard[par]])
                psq_free[par] = [ev]
                vb = par % 2
                mset = p.op("vector", lambda e, vb=vb: e.memset(
                    vas[vb][:, :].rearrange("q (kc h x) -> q kc h x", kc=4, h=2)[:, :, :, 64:65],
                    1.0), waits=[p.drain("scalar")])
                lastv = NONE_MS
                last_tp = NONE_MS
                for tt in range(4):
                    tp = p.op("tensor", lambda e, par=par, tt=tt: e.transpose(
                        ps_vtr[:, :], ev_bf[par][:, tt * 128:(tt + 1) * 128], ident_s[:, :]),
                        waits=[ev, vtr_prev])
                    vtr_prev = p.op("vector", lambda e, vb=vb, tt=tt: e.tensor_copy(
                        vas[vb][:, tt * 130:(tt + 1) * 130].rearrange(
                            "q (h x) -> q h x", h=2)[:, :, 0:64],
                        ps_vtr[:, :].rearrange("q (h x) -> q h x", h=2)),
                        waits=[tp, mset])
                    lastv = vtr_prev
                    last_tp = tp
                ev_tp_guard[par] = last_tp
                p.dma("scalar", v_dram[pr, qb * 4:(qb + 1) * 4].rearrange("kc q c -> q kc c"),
                      vas[vb][:, :].rearrange("q (kc c) -> q kc c", kc=4), waits=[lastv])

    prev_ms = nrm_red_prev
    prev_dma = NONE_MS
    for pr in range(8):
        m1 = p.op("vector", lambda e, pr=pr: e.reduce_max(
            nb16[0:2, 0:1], nmx[0:2, pr * 8:(pr + 1) * 8], axis=AX.X),
            waits=[prev_ms, prev_dma])
        m2 = p.op("vector", lambda e, pr=pr: e.reduce_max(
            nb16[0:2, 1:2], nmx[0:2, 64 + pr * 8: 64 + (pr + 1) * 8], axis=AX.X),
            waits=[m1])
        m3 = p.op("vector", lambda e: e.tensor_tensor(
            nb16[0:2, 2:3], nb16[0:2, 0:1], nb16[0:2, 1:2], ALU.mult), waits=[m2])
        aB = p.op("scalar", lambda e: e.activation(
            nb16[0:2, 3:4], nb16[0:2, 2:3], AF.Sqrt), waits=[m3])
        m4 = p.op("vector", lambda e: e.tensor_scalar(
            nb16[0:2, 3:4], nb16[0:2, 3:4], -0.125, None, ALU.mult), waits=[aB])
        prev_dma = p.dma("gpsimd", nb_dram[2 * pr:2 * pr + 2, :], nb16[0:2, 3:4],
                         waits=[m4])
        prev_ms = m4
    nb_ld = p.dma("sync", negb_bc[:, :],
                  nb_dram.rearrange("h one -> one h").to_broadcast([128, 16]),
                  waits=[prev_dma])
    p.op("vector", lambda e: e.tensor_tensor(
        negb_pair[:, :],
        negb_bc[:, :].rearrange("q (p h) -> q p h", h=2)[:, :, 0],
        negb_bc[:, :].rearrange("q (p h) -> q p h", h=2)[:, :, 1], ALU.min),
        waits=[nb_ld])

    p.barrier()
    esB.close()
    esA2.close()
    esA.close()

    esM = contextlib.ExitStack()
    sbM = lambda name, shape, dt: esM.enter_context(nc.sbuf_tensor(name, shape, dt))
    masks_s = sbM("masks_s", [128, 4 * QBS], BF16)
    b_bc = sbM("b_bc", [128, D], F32)
    esel_s = sbM("esel_s", [16, 8 * 128], BF16)
    Gs = sbM("Gs", [128, 2048], F32)
    th_lo = sbM("th_lo", [128, NEXP], F32)
    th_hi = sbM("th_hi", [128, NEXP], F32)
    th_md = sbM("th_md", [128, NEXP], F32)
    cnt_p = sbM("cnt_p", [128, NEXP], F32)
    f_scr = sbM("f_scr", [128, 256], F32)
    cnt_tot = sbM("cnt_tot", [1, NEXP], F32)
    cnt_b = sbM("cnt_b", [128, NEXP], F32)
    ge_b = sbM("ge_b", [128, NEXP], F32)
    u_s = sbM("u_s", [128, NEXP], F32)
    F_e = sbM("F_e", [128, 256], F32)
    Fp_e = sbM("Fp_e", [128, 256], F32)

    esC = contextlib.ExitStack()
    sbC = lambda name, shape, dt: esC.enter_context(nc.sbuf_tensor(name, shape, dt))
    qs = [sbC(f"qs{i}", [128, QBS], BF16) for i in range(2)]
    ks = [sbC(f"ks{i}", [128, TOK], BF16) for i in range(2)]
    vs = [sbC(f"vs{i}", [128, NKC * 130], BF16) for i in range(2)]
    pt = [sbC(f"pt{i}", [128, 1024], BF16) for i in range(4)]
    attn_raw = sbC("attn_raw", [128, 8 * QBS], BF16)
    attn_nrm = sbC("attn_nrm", [128, 8 * QBS], BF16)
    dmat = sbC("dmat", [16, QBS], F32)
    dtmp = [sbC(f"dtmp{i}", [1, QBS], F32) for i in range(2)]
    rbf = sbC("rbf", [16, QBS], BF16)
    rexp_bf = sbC("rexp_bf", [128, QBS], BF16)
    xres = [sbC(f"xres{i}", [128, D], F32) for i in range(2)]
    out32 = [sbC(f"out32_{i}", [128, D], F32) for i in range(2)]
    wout_bf = sbC("wout_bf", [128, DC * D], BF16)
    ps_s = [esC.enter_context(nc.psum_tensor(f"ps_s{i}", [128, 1024], F32)) for i in range(2)]
    ps_o = [esC.enter_context(nc.psum_tensor(f"ps_o{i}", [128, QBS], F32)) for i in range(2)]
    ps_op = [esC.enter_context(nc.psum_tensor(f"ps_op{i}", [128, QBS], F32)) for i in range(2)]

    bld = p.dma("sync", b_bc[:, :], b_out[0:1, :].to_broadcast([128, D]))
    masks_ld = p.dma("sync", masks_s[:, :], masks_c[:, :])
    esel_ld = p.dma("sync", esel_s[:, :], esel_c[:, :])
    wout_ld = NONE_MS
    for c in range(DC):
        wout_ld = p.dma("gpsimd", wout_bf[:, c * D:(c + 1) * D],
                        w_out[c * 128:(c + 1) * 128, :])
    gs_ld = p.dma("gpsimd", Gs[:, :].rearrange("q (r c) -> q r c", r=NCORES),
                  cout_g.rearrange("(r q) c -> q r c", q=128),
                  waits=[cc1, wout_ld])
    gm0 = p.op("vector", lambda e: e.memset(th_lo[:, :], 0.0))
    gm1 = p.op("vector", lambda e: e.memset(th_hi[:, :], 1.0))
    gm2 = p.op("vector", lambda e: e.memset(th_md[:, :], 0.5))

    def gs_e(e_):
        return Gs[:, :].rearrange("q (r e t) -> q r e t", r=NCORES, e=NEXP)[:, :, e_, :]

    bis = {"prev": gm2, "bcast": gs_ld, "d3": NONE_MS}
    bis_steps = []

    def _mk_count(rnd):
        def fn():
            pr = bis["prev"]
            for e_ in range(NEXP):
                pr = p.op("vector", lambda en, e_=e_: en.tensor_scalar(
                    f_scr[:, :].rearrange("q (r t) -> q r t", r=NCORES),
                    gs_e(e_), th_md[:, e_:e_ + 1], None, ALU.is_ge,
                    ALU.add, accum_out=cnt_p[:, e_:e_ + 1]),
                    waits=([bis["prev"], bis["bcast"], gs_ld, gm0, gm1]
                           if e_ == 0 else []))
            rc = p.op("gpsimd", lambda en: en.reduce_sum(
                cnt_tot[0:1, :], cnt_p[:, :], axis=AX.C),
                waits=[pr, bis["d3"]])
            d3 = p.dma("gpsimd", ctot_dram[0:1, :], cnt_tot[0:1, :], waits=[rc])
            bis["d3"] = d3
            bis["bcast"] = p.dma("gpsimd", cnt_b[:, :],
                                 ctot_dram[0:1, :].to_broadcast([128, NEXP]),
                                 waits=[d3])
            bis["prev"] = pr
        return fn

    def _mk_update(rnd):
        def fn():
            g1 = p.op("vector", lambda en: en.tensor_scalar(
                ge_b[:, :], cnt_b[:, :], float(CAP), None, ALU.is_ge),
                waits=[bis["bcast"], bis["prev"]])
            g2_ = p.op("vector", lambda en: en.tensor_tensor(
                u_s[:, :], th_md[:, :], th_lo[:, :], ALU.subtract), waits=[g1])
            g3 = p.op("vector", lambda en: en.tensor_tensor(
                u_s[:, :], ge_b[:, :], u_s[:, :], ALU.mult), waits=[g2_])
            g4 = p.op("vector", lambda en: en.tensor_tensor(
                th_lo[:, :], th_lo[:, :], u_s[:, :], ALU.add), waits=[g3])
            g5 = p.op("vector", lambda en: en.tensor_tensor(
                u_s[:, :], th_hi[:, :], th_md[:, :], ALU.subtract), waits=[g4])
            g6 = p.op("vector", lambda en: en.tensor_tensor(
                u_s[:, :], ge_b[:, :], u_s[:, :], ALU.mult), waits=[g5])
            g7 = p.op("vector", lambda en: en.tensor_tensor(
                th_hi[:, :], th_md[:, :], u_s[:, :], ALU.add), waits=[g6])
            g8 = p.op("vector", lambda en: en.tensor_tensor(
                u_s[:, :], th_lo[:, :], th_hi[:, :], ALU.add), waits=[g7])
            bis["prev"] = p.op("vector", lambda en: en.tensor_scalar(
                th_md[:, :], u_s[:, :], 0.5, None, ALU.mult), waits=[g8])
        return fn

    for rnd in range(BIS_ROUNDS):
        bis_steps.append(_mk_count(rnd))
        bis_steps.append(_mk_update(rnd))


    strip_free = [NONE_MS, NONE_MS]
    pt_free = [NONE_MS] * 4
    s_free = [NONE_MS, NONE_MS]
    o_free = [NONE_MS, NONE_MS]
    dmat_free = NONE_MS
    op_free = [NONE_MS, NONE_MS]

    lq_gate = NONE_MS
    for qb in range(NQB):
        nkc = 4 * (qb + 1)
        for pr in range(NPAIR):
            sp = pr % 2
            lq = p.dma("sync", qs[sp][:, :], q_dram[pr][:, qb * QBS:(qb + 1) * QBS],
                       waits=[strip_free[sp], lq_gate])
            lk = p.dma("sync", ks[sp][:, 0:nkc * 128], k_dram[pr][:, 0:nkc * 128])
            lv = p.dma("sync",
                       vs[sp][:, 0:nkc * 130].rearrange("q (kc c) -> q kc c", kc=nkc),
                       v_dram[pr, 0:nkc].rearrange("kc q c -> q kc c"))
            last_av = NONE_MS
            av_pend = []
            for kc in range(nkc):
                par = kc % 2
                pp = kc % 4
                w_s = [s_free[par]]
                if kc == 0:
                    w_s += [p.drain("sync")]
                mmA = p.op("tensor", lambda e, sp=sp, kc=kc, par=par: e.matmul(
                    ps_s[par][:, 0:512], ks[sp][0:64, kc * 128:(kc + 1) * 128],
                    qs[sp][0:64, :], start=True, stop=True, tile_position=(0, 0)),
                    waits=w_s)
                if kc == 0:
                    lq_gate = mmA
                mmB = p.op("tensor", lambda e, sp=sp, kc=kc, par=par: e.matmul(
                    ps_s[par][:, 512:1024], ks[sp][64:128, kc * 128:(kc + 1) * 128],
                    qs[sp][64:128, :], start=True, stop=True, tile_position=(64, 0)))
                eB = p.op("scalar", lambda e, pp=pp, par=par, pr=pr: e.activation(
                    pt[pp][:, 0:1024], ps_s[par][:, 0:1024], AF.Exp,
                    bias=negb_pair[:, pr:pr + 1], scale=0.125),
                    waits=[mmB, pt_free[pp]])
                s_free[par] = eB
                w_av = [eB]
                if kc >= 4 * qb:
                    m = kc - 4 * qb
                    p.op("vector", lambda e, pp=pp, m=m: e.tensor_tensor(
                        pt[pp][:, 0:512], pt[pp][:, 0:512],
                        masks_s[:, m * QBS:(m + 1) * QBS], ALU.mult),
                        waits=[eB])
                    mk2 = p.op("vector", lambda e, pp=pp, m=m: e.tensor_tensor(
                        pt[pp][:, 512:1024], pt[pp][:, 512:1024],
                        masks_s[:, m * QBS:(m + 1) * QBS], ALU.mult))
                    w_av = [mk2]
                if kc == 0:
                    w_av = w_av + [o_free[0], o_free[1]]
                av_pend.append((kc, pp, w_av))
                for (kcp, ppp, w_avp) in av_pend if kc == nkc - 1 else av_pend[:-3]:
                    avA = p.op("tensor", lambda e, sp=sp, kc=kcp, nkc=nkc, pp=ppp: e.matmul(
                        ps_o[0][0:65, :], vs[sp][:, kc * 130: kc * 130 + 65],
                        pt[pp][:, 0:512], start=(kc == 0), stop=(kc == nkc - 1)),
                        waits=w_avp)
                    avB = p.op("tensor", lambda e, sp=sp, kc=kcp, nkc=nkc, pp=ppp: e.matmul(
                        ps_o[1][0:65, :], vs[sp][:, kc * 130 + 65: kc * 130 + 130],
                        pt[pp][:, 512:1024], start=(kc == 0), stop=(kc == nkc - 1)))
                    pt_free[ppp] = avB
                    last_av = avB
                av_pend = [] if kc == nkc - 1 else av_pend[-3:]
            strip_free[sp] = last_av
            ev1 = p.op("vector", lambda e, pr=pr: e.tensor_copy(
                attn_raw[0:64, pr * QBS:(pr + 1) * QBS], ps_o[0][0:64, :]),
                waits=[last_av])
            ev2 = p.op("vector", lambda e, pr=pr: e.tensor_copy(
                attn_raw[64:128, pr * QBS:(pr + 1) * QBS], ps_o[1][0:64, :]))
            d1 = p.op("vector", lambda e: e.tensor_copy(
                dtmp[0][0:1, :], ps_o[0][64:65, :]), waits=[p.drain("scalar")])
            d2 = p.op("vector", lambda e: e.tensor_copy(
                dtmp[1][0:1, :], ps_o[1][64:65, :]))
            o_free[0] = d1
            o_free[1] = d2
            p.dma("scalar", dmat[2 * pr:2 * pr + 1, :], dtmp[0][0:1, :],
                  waits=[d1, dmat_free])
            p.dma("scalar", dmat[2 * pr + 1:2 * pr + 2, :], dtmp[1][0:1, :],
                  waits=[d2])
            if bis_steps:
                bis_steps.pop(0)()
        vR = p.op("vector", lambda e: e.reciprocal(dmat[0:16, :], dmat[0:16, :]),
                  waits=[p.drain("scalar")])
        vRb = p.op("vector", lambda e: e.tensor_copy(rbf[0:16, :], dmat[0:16, :]),
                   waits=[vR])
        dmat_free = vRb
        nrm_done = NONE_MS
        for c in range(DC):
            mmr = p.op("tensor", lambda e, c=c: e.matmul(
                ps_op[0][:, :], esel_s[:, c * 128:(c + 1) * 128], rbf[0:16, :],
                start=True, stop=True), waits=[vRb, op_free[0]])
            rev = p.op("vector", lambda e: e.tensor_copy(rexp_bf[:, :], ps_op[0][:, :]),
                       waits=[mmr])
            nrm_done = p.op("vector", lambda e, c=c: e.tensor_tensor(
                attn_nrm[:, c * QBS:(c + 1) * QBS], attn_raw[:, c * QBS:(c + 1) * QBS],
                rexp_bf[:, :], ALU.mult), waits=[rev])
            op_free[0] = nrm_done
        xres_gate = vR
        for tt in range(4):
            ti = qb * 4 + tt
            ob = tt % 2
            xld = p.dma("scalar", xres[ob][:, :], x_own[ti * 128:(ti + 1) * 128, :],
                        waits=[xres_gate])
            mm_h = [NONE_MS, NONE_MS]
            for half in range(2):
                for c in range(DC):
                    w_op = []
                    if c == 0:
                        w_op = [nrm_done, op_free[half]]
                        if qb == 0 and tt == 0:
                            w_op.append(wout_ld)
                    mm_h[half] = p.op("tensor", lambda e, tt=tt, c=c, half=half: e.matmul(
                        ps_op[half][:, :],
                        attn_nrm[:, c * QBS + tt * 128: c * QBS + (tt + 1) * 128],
                        wout_bf[:, c * D + half * 512: c * D + (half + 1) * 512],
                        start=(c == 0), stop=(c == DC - 1)), waits=w_op)
            va = p.op("vector", lambda e, tt=tt, ob=ob: e.tensor_tensor(
                out32[ob][:, 0:512], ps_op[0][:, :], xres[ob][:, 0:512], ALU.add),
                waits=[mm_h[0], p.drain("scalar"), bld])
            xres_gate = va
            vb2 = p.op("vector", lambda e, tt=tt, ob=ob: e.tensor_tensor(
                out32[ob][:, 512:1024], ps_op[1][:, :], xres[ob][:, 512:1024], ALU.add),
                waits=[mm_h[1]])
            vc1 = p.op("vector", lambda e, ob=ob: e.tensor_tensor(
                out32[ob][:, 0:512], out32[ob][:, 0:512], b_bc[:, 0:512], ALU.add),
                waits=[va])
            vc2 = p.op("vector", lambda e, ob=ob: e.tensor_tensor(
                out32[ob][:, 512:1024], out32[ob][:, 512:1024], b_bc[:, 512:1024],
                ALU.add), waits=[vb2])
            op_free[0] = vc1
            op_free[1] = vc2
            p.dma("scalar", out_t[ti * 128:(ti + 1) * 128, :],
                  out32[ob][:, :], waits=[vc2])

    while bis_steps:
        bis_steps.pop(0)()
    wprev = bis["prev"]
    for e_ in range(NEXP):
        s1 = p.op("vector", lambda en, e_=e_: en.tensor_scalar(
            F_e[:, 0:32], G_own[:, e_ * 32:(e_ + 1) * 32], th_lo[:, e_:e_ + 1],
            None, ALU.is_ge), waits=[wprev])
        s2 = p.op("vector", lambda en, e_=e_: en.tensor_scalar(
            Fp_e[:, 0:32], G_own[:, e_ * 32:(e_ + 1) * 32], 0.0, None, ALU.is_gt),
            waits=[s1])
        s3 = p.op("vector", lambda en: en.tensor_tensor(
            F_e[:, 0:32], F_e[:, 0:32], Fp_e[:, 0:32], ALU.mult), waits=[s2])
        wprev = p.op("vector", lambda en, e_=e_: en.tensor_tensor(
            wsel[:, e_ * 32:(e_ + 1) * 32], G_own[:, e_ * 32:(e_ + 1) * 32],
            F_e[:, 0:32], ALU.mult), waits=[s3])
    wsel_done = wprev

    p.barrier()
    esC.close()
    esM.close()

    esDR = contextlib.ExitStack()
    sbDR = lambda name, shape, dt: esDR.enter_context(nc.sbuf_tensor(name, shape, dt))
    slotidx_sb = sbDR("slotidx_sb", [128, 256], F32)
    gc_sb = sbDR("gc_sb", [128, 128], F32)
    iota2b = sbDR("iota2b", [128, SLOTB], F32)
    ind_all = sbDR("ind_all", [128, 256], F32)

    esD0 = contextlib.ExitStack()
    sbD0 = lambda name, shape, dt: esD0.enter_context(nc.sbuf_tensor(name, shape, dt))
    xn_tm_sb = sbD0("xn_tm_sb", [128, 4 * D], BF16)
    tri_s = sbD0("tri_s", [128, 128], F32)
    onescol_d = sbD0("onescol_d", [128, 1], F32)
    ones32 = sbD0("ones32", [128, 32], F32)
    base_sb = sbD0("base_sb", [128, 64], F32)

    tri_ld = p.dma("sync", tri_s[:, :], tri_c[:, :])
    iota_ld = p.dma("sync", iota2b[:, :],
                    iota2_c[0:1, :].to_broadcast([128, SLOTB]))
    on_ms = p.op("vector", lambda e: e.memset(ones32[:, :], 1.0))
    on1_ms = p.op("vector", lambda e: e.memset(onescol_d[:, :], 1.0))

    esD0i = contextlib.ExitStack()
    sbD0i = lambda name, shape, dt: esD0i.enter_context(nc.sbuf_tensor(name, shape, dt))
    scn_all = sbD0i("scn_all", [128, 256], F32)
    segb = sbD0i("segb", [128, 256], F32)
    rc_all = sbD0i("rc_all", [128, 64], F32)
    esD0p = contextlib.ExitStack()
    ps_xc = [esD0p.enter_context(nc.psum_tensor(f"ps_xc{i}", [128, 512], F32))
             for i in range(4)]
    ps_bs = esD0p.enter_context(nc.psum_tensor("ps_bs", [128, 512], F32))
    ps_gc = [esD0p.enter_context(nc.psum_tensor(f"ps_gc{i}", [128, 512], F32))
             for i in range(3)]

    ia0 = p.op("vector", lambda e: e.tensor_scalar(
        ind_all[:, :], wsel[:, :], 0.0, None, ALU.is_gt), waits=[wsel_done])
    pv = p.op("vector", lambda e: e.memset(segb[:, :], 0.0), waits=[ia0])
    for e_ in range(NEXP):
        pv = p.op("vector", lambda e, e_=e_: e.tensor_tensor_scan(
            scn_all[:, e_ * 32:(e_ + 1) * 32], ind_all[:, e_ * 32:(e_ + 1) * 32],
            ones32[:, :], 0.0, ALU.add, ALU.mult),
            waits=([pv, on_ms] if e_ == 0 else [pv]))
    for e_ in range(NEXP):
        sv = scn_all[:, e_ * 32:(e_ + 1) * 32].rearrange("q (b f) -> q b f", f=4)
        gv = segb[:, e_ * 32:(e_ + 1) * 32].rearrange("q (b f) -> q b f", f=4)
        for j in range(4):
            pv = p.op("vector", lambda e, sv=sv, gv=gv, j=j: e.tensor_copy(
                gv[:, 1:8, j], sv[:, 0:7, 3]), waits=[pv])
        pv = p.op("vector", lambda e, e_=e_, sv=sv, gv=gv: e.tensor_tensor(
            rc_all[:, e_ * 8:(e_ + 1) * 8], sv[:, :, 3], gv[:, :, 0],
            ALU.subtract), waits=[pv])
    bmm0 = p.op("tensor", lambda e: e.matmul(
        ps_bs[:, 0:64], tri_s[:, :], rc_all[:, 0:64], start=True, stop=True),
        waits=[pv, iota_ld])
    bev0 = p.op("vector", lambda e: e.tensor_copy(
        base_sb[:, :], ps_bs[:, 0:64]), waits=[bmm0])
    pv = p.op("vector", lambda e: e.tensor_tensor(
        slotidx_sb[:, :], scn_all[:, :], ind_all[:, :], ALU.subtract),
        waits=[bev0])
    pv = p.op("vector", lambda e: e.tensor_tensor(
        slotidx_sb[:, :], slotidx_sb[:, :], segb[:, :], ALU.subtract),
        waits=[pv])
    for e_ in range(NEXP):
        for blk in range(NQB):
            c0 = e_ * 32 + blk * 4
            pv = p.op("vector", lambda e, e_=e_, blk=blk, c0=c0: e.tensor_scalar(
                slotidx_sb[:, c0:c0 + 4], slotidx_sb[:, c0:c0 + 4],
                base_sb[:, e_ * 8 + blk:e_ * 8 + blk + 1], None, ALU.add),
                waits=[pv])
    slfin = pv
    esD0i.close()

    selT_sb = [sbD0(f"selT_sb{i}", [128, 4 * SLOTB], BF16) for i in range(2)]
    gpk2 = [sbD0(f"gpk2{i}", [128, 8], BF16) for i in range(2)]
    ghi_f = sbD0("ghi_f", [128, 4], F32)
    glo_f = sbD0("glo_f", [128, 4], F32)
    xc_ev = [sbD0(f"xc_ev{i}", [128, SLOTB], BF16) for i in range(4)]

    xn_free = iota_ld
    psxc_free = [NONE_MS] * 4
    gc_free = [bev0, NONE_MS, NONE_MS]
    ghi_free = [NONE_MS, NONE_MS]
    selT_free = [NONE_MS, NONE_MS]
    xcev_free = [NONE_MS] * 4

    def emit_selT(e_, blk):
        eb = e_ % 2
        last = NONE_MS
        for tc in range(4):
            for st in range(2):
                c0 = e_ * 32 + blk * 4
                last = p.op("vector", lambda e, tc=tc, st=st, c0=c0, eb=eb: e.tensor_scalar(
                    selT_sb[eb][:, tc * SLOTB + st * 128: tc * SLOTB + (st + 1) * 128],
                    iota2b[:, st * 128:(st + 1) * 128],
                    slotidx_sb[:, c0 + tc:c0 + tc + 1], ind_all[:, c0 + tc:c0 + tc + 1],
                    ALU.is_equal, ALU.mult),
                    waits=([slfin, selT_free[eb]] if tc == 0 and st == 0 else []))
        return last

    for blk in range(NQB):
        xn_ld = p.dma("sync", xn_tm_sb[:, :].rearrange("q (tc f) -> q tc f", tc=4),
                      xn_tm_dram[blk * 512:(blk + 1) * 512, :].rearrange(
                          "(tc q) f -> q tc f", q=128),
                      waits=[xn_free])
        bld_last = emit_selT(0, blk)
        gc_pend = None

        def emit_gc(pend):
            (pe_, peb, pg4) = pend
            lg = NONE_MS
            for st in range(2):
                gp = (pe_ * 2 + st) % 3
                gmm = NONE_MS
                for tc in range(4):
                    gmm = p.op("tensor", lambda e, st=st, tc=tc, peb=peb, gp=gp: e.matmul(
                        ps_gc[gp][:, 0:2],
                        selT_sb[peb][:, tc * SLOTB + st * 128: tc * SLOTB + (st + 1) * 128],
                        gpk2[peb][:, tc * 2:(tc + 1) * 2], start=(tc == 0), stop=(tc == 3)),
                        waits=([pg4, gc_free[gp]] if tc == 0 else []))
                lg = gmm
                gcol = gc_sb[:, pe_ * 16 + blk * 2 + st: pe_ * 16 + blk * 2 + st + 1]
                gv1 = p.op("vector", lambda e, gcol=gcol, gp=gp: e.tensor_copy(
                    gcol, ps_gc[gp][:, 0:1]), waits=[gmm])
                gev = p.op("vector", lambda e, gcol=gcol, gp=gp: e.tensor_tensor(
                    gcol, gcol, ps_gc[gp][:, 1:2], ALU.add), waits=[gv1])
                gc_free[gp] = gev
            return lg

        for e_ in range(NEXP):
            eb = e_ % 2
            c0 = e_ * 32 + blk * 4
            if gc_pend is not None:
                lgm = emit_gc(gc_pend)
                selT_free[gc_pend[1]] = lgm
                ghi_free[gc_pend[1]] = lgm
                gc_pend = None
            mm_xc = NONE_MS
            for featc in range(DC):
                fb = featc % 4
                for tc in range(4):
                    w_m = []
                    if tc == 0:
                        w_m = [bld_last, psxc_free[fb]]
                        if featc == 0:
                            w_m.append(xn_ld)
                    mm_xc = p.op("tensor", lambda e, featc=featc, tc=tc, fb=fb, eb=eb: e.matmul(
                        ps_xc[fb][:, 0:SLOTB],
                        xn_tm_sb[:, tc * D + featc * 128: tc * D + (featc + 1) * 128],
                        selT_sb[eb][:, tc * SLOTB:(tc + 1) * SLOTB],
                        start=(tc == 0), stop=(tc == 3)), waits=w_m)
                xce = p.op("vector", lambda e, fb=fb: e.tensor_copy(
                    xc_ev[fb][:, :], ps_xc[fb][:, 0:SLOTB]),
                    waits=[mm_xc, xcev_free[fb], p.drain("scalar")])
                psxc_free[fb] = xce
                xcev_free[fb] = p.dma(
                    "scalar",
                    xc_dram[e_][:, featc * SLOT_E + blk * SLOTB:
                                featc * SLOT_E + (blk + 1) * SLOTB],
                    xc_ev[fb][:, :], waits=[xce])
            if selT_free[eb] is NONE_MS or e_ == 0:
                selT_free[eb] = mm_xc
            if e_ + 1 < NEXP:
                bld_last = emit_selT(e_ + 1, blk)
            gview = gpk2[eb][:, :].rearrange("q (tc two) -> q tc two", two=2)
            g1_ = p.op("vector", lambda e, c0=c0, gview=gview: e.tensor_copy(
                gview[:, :, 0], wsel[:, c0:c0 + 4]), waits=[ghi_free[eb]])
            g2_ = p.op("vector", lambda e, gview=gview: e.tensor_copy(
                ghi_f[:, :], gview[:, :, 0]), waits=[g1_])
            g3_ = p.op("vector", lambda e, c0=c0: e.tensor_tensor(
                glo_f[:, :], wsel[:, c0:c0 + 4], ghi_f[:, :], ALU.subtract),
                waits=[g2_])
            g4_ = p.op("vector", lambda e, gview=gview: e.tensor_copy(
                gview[:, :, 1], glo_f[:, :]), waits=[g3_])
            gc_pend = (e_, eb, g4_)
        lgm = emit_gc(gc_pend)
        selT_free[gc_pend[1]] = lgm
        ghi_free[gc_pend[1]] = lgm
        xn_free = (PE, PE.v)
    gc_done = gc_free[(7 * 2 + 1) % 3]
    xc_done = xcev_free[3]
    d0_gate = (PE, PE.v)
    esD0p.close()

    esD1 = contextlib.ExitStack()
    sbD1 = lambda name, shape, dt: esD1.enter_context(nc.sbuf_tensor(name, shape, dt))
    w1_sb = sbD1("w1_sb", [128, FC * 1024], BF16)
    w2_sb = sbD1("w2_sb", [128, FC * D], BF16)
    xnb = [sbD1(f"xnb{i}", [128, DC * 512], BF16) for i in range(2)]
    h_sb = sbD1("h_sb", [128, FC * 512], BF16)
    yst = [sbD1(f"yst{i}", [128, D], F32) for i in range(2)]
    ps_h = [esD1.enter_context(nc.psum_tensor(f"ps_h{i}", [128, 512], F32)) for i in range(4)]
    ps_y = [esD1.enter_context(nc.psum_tensor(f"ps_y{i}", [128, 512], F32)) for i in range(2)]

    w1_free = NONE_MS
    w2_free = NONE_MS
    xnb_free = [NONE_MS, NONE_MS]
    psy_free = [NONE_MS, NONE_MS]
    bank_guard = [NONE_MS] * 4
    mmh_gate = d0_gate
    y2_gate = NONE_MS
    w1_next_free = NONE_MS

    for e_ in range(NEXP):
        w1_lds = []

        def emit_w1_batch(b, gate):
            for fc in range(8 * b, 8 * b + 8):
                w1_lds.append(p.dma(
                    "gpsimd",
                    w1_sb[:, fc * 1024:(fc + 1) * 1024].rearrange(
                        "r (dc c) -> r dc c", dc=DC),
                    w1_pk[e_, fc].rearrange("dc r c -> r dc c"),
                    waits=(gate if fc == 8 * b else [])))
        emit_w1_batch(0, [w1_free, y2_gate])
        w2_lds = []
        for sb_ in range(NSB):
            xld = p.dma("sync",
                        xnb[sb_ % 2][:, :].rearrange("r (dc c) -> r dc c", dc=DC),
                        xc_dram[e_].rearrange("r (dc s) -> r dc s", dc=DC)[
                            :, :, sb_ * 512:(sb_ + 1) * 512],
                        waits=([xnb_free[sb_ % 2], mmh_gate]
                               + ([xc_done, gc_done] if e_ == 0 and sb_ == 0 else [])))
            relu_ms = {}
            for fc in range(FC):
                mmh = NONE_MS
                for dc in range(DC):
                    w_m = []
                    if dc == 0:
                        w_m = [xld, bank_guard[fc % 4]]
                        if sb_ == 0 and fc % 8 == 0:
                            w_m.append(w1_lds[fc + 7])
                    mmh = p.op("tensor", lambda e, fc=fc, dc=dc, sb_=sb_: e.matmul(
                        ps_h[fc % 4][:, :],
                        w1_sb[:, fc * 1024 + dc * 128: fc * 1024 + (dc + 1) * 128],
                        xnb[sb_ % 2][:, dc * 512:(dc + 1) * 512],
                        start=(dc == 0), stop=(dc == DC - 1)), waits=w_m)
                    if sb_ == 0 and dc == 0 and fc % 8 == 0 and fc // 8 + 1 < 4:
                        emit_w1_batch(fc // 8 + 1, [mmh])
                    if fc == 0 and dc == 0:
                        mmh_gate = mmh
                        if sb_ == 0:
                            for fc2 in range(8):
                                w2_lds.append(p.dma(
                                    "gpsimd", w2_sb[:, fc2 * D:(fc2 + 1) * D],
                                    w2_in[e_, fc2 * 128:(fc2 + 1) * 128, :],
                                    waits=([w2_free, mmh] if fc2 == 0 else [])))
                relu_ms[fc] = p.op("scalar", lambda e, fc=fc: e.activation(
                    h_sb[:, fc * 512:(fc + 1) * 512], ps_h[fc % 4][:, :], AF.Relu),
                    waits=[mmh])
                bank_guard[fc % 4] = relu_ms[fc]
            xnb_free[sb_ % 2] = (PE, PE.v)
            if sb_ == NSB - 1:
                w1_next_free = (PE, PE.v)
            for tt in range(4):
                ti = sb_ * 4 + tt
                mmy = [NONE_MS, NONE_MS]
                for half in range(2):
                    for fc in range(FC):
                        w_m = [relu_ms[fc]]
                        if fc == 0:
                            w_m.append(psy_free[half])
                        if sb_ == 0 and tt == 0 and half == 0 and fc % 8 == 0:
                            w_m.append(w2_lds[fc + 7])
                        mmy[half] = p.op("tensor", lambda e, fc=fc, half=half, tt=tt: e.matmul(
                            ps_y[half][:, :],
                            h_sb[:, fc * 512 + tt * 128: fc * 512 + (tt + 1) * 128],
                            w2_sb[:, fc * D + half * 512: fc * D + (half + 1) * 512],
                            start=(fc == 0), stop=(fc == FC - 1)), waits=w_m)
                        if (sb_ == 0 and tt == 0 and half == 0 and fc % 8 == 0
                                and fc // 8 + 1 < 4):
                            for fc2 in range(8 * (fc // 8 + 1), 8 * (fc // 8 + 1) + 8):
                                w2_lds.append(p.dma(
                                    "gpsimd", w2_sb[:, fc2 * D:(fc2 + 1) * D],
                                    w2_in[e_, fc2 * 128:(fc2 + 1) * 128, :],
                                    waits=([mmy[half]] if fc2 % 8 == 0 else [])))
                y1 = p.op("vector", lambda e, ti=ti, e_=e_: e.tensor_scalar(
                    yst[ti % 2][:, 0:512], ps_y[0][:, :],
                    gc_sb[:, e_ * 16 + ti: e_ * 16 + ti + 1], None, ALU.mult),
                    waits=[mmy[0], p.drain("gpsimd"), gc_done])
                y2 = p.op("vector", lambda e, ti=ti, e_=e_: e.tensor_scalar(
                    yst[ti % 2][:, 512:1024], ps_y[1][:, :],
                    gc_sb[:, e_ * 16 + ti: e_ * 16 + ti + 1], None, ALU.mult),
                    waits=[mmy[1]])
                psy_free[0] = y1
                psy_free[1] = y2
                if ti == NST_E - 3:
                    y2_gate = y2
                p.dma("gpsimd", y_dram[e_, ti], yst[ti % 2][:, :],
                      waits=[y1, y2])
        w1_free = w1_next_free
        w2_free = (PE, PE.v)

    p.barrier()
    esD1.close()
    esD0.close()

    esD2 = contextlib.ExitStack()
    sbD2 = lambda name, shape, dt: esD2.enter_context(nc.sbuf_tensor(name, shape, dt))
    y_sb = [sbD2(f"y_sb{i}", [128, 16 * D], BF16) for i in range(2)]
    selT_d = [sbD2(f"selT_d{i}", [128, 128], BF16) for i in range(2)]
    sel_ev = [sbD2(f"sel_ev{i}", [128, 128], BF16) for i in range(2)]
    oadd = [sbD2(f"oadd{i}", [128, D], F32) for i in range(2)]
    ps_sc = [esD2.enter_context(nc.psum_tensor(f"ps_sc{i}", [128, 512], F32))
             for i in range(4)]
    ps_tT = [esD2.enter_context(nc.psum_tensor(f"ps_tT{i}", [128, 1024], BF16))
             for i in range(2)]

    ysb_free = [NONE_MS, NONE_MS]
    y_gate = NONE_MS
    bld_free = [NONE_MS, NONE_MS]
    sev_free = [NONE_MS, NONE_MS]
    psc_free = [NONE_MS] * 4
    oadd_free = [NONE_MS, NONE_MS]
    for blk in range(NQB):
        bb = blk % 2
        y_lds = NONE_MS
        for e_ in range(NEXP):
            y_lds = p.dma(
                "sync",
                y_sb[bb][:, e_ * 2 * D:(e_ + 1) * 2 * D].rearrange(
                    "q (st f) -> q st f", st=2),
                y_dram[e_, blk * 2:(blk + 1) * 2].rearrange("st q f -> q st f"),
                waits=([ysb_free[bb], y_gate] if e_ == 0 else []))
        for tc in range(4):
            mm_sc = NONE_MS
            pend = None
            for idx in range(17):
                if idx < 16:
                    e_, st = idx // 2, idx % 2
                    ib = idx % 2
                    c0 = e_ * 32 + blk * 4 + tc
                    bld = p.op("vector", lambda e, st=st, c0=c0, ib=ib: e.tensor_scalar(
                        selT_d[ib][:, :], iota2b[:, st * 128:(st + 1) * 128],
                        slotidx_sb[:, c0:c0 + 1], ind_all[:, c0:c0 + 1],
                        ALU.is_equal, ALU.mult),
                        waits=[bld_free[ib]])
                    tp = p.op("tensor", lambda e, ib=ib: e.transpose(
                        ps_tT[ib][:, 0:128], selT_d[ib][:, :], ident_s[:, :]),
                        waits=[bld, sev_free[ib]])
                    bld_free[ib] = tp
                    sev = p.op("vector", lambda e, ib=ib: e.tensor_copy(
                        sel_ev[ib][:, :], ps_tT[ib][:, 0:128]), waits=[tp])
                if pend is not None:
                    (pidx, pib, pe_, pst, psev) = pend
                    tb = tc % 2
                    for half in range(2):
                        w_m = [psev]
                        if pidx == 0:
                            w_m += [psc_free[2 * tb + half], y_lds, p.drain("sync")]
                        mm_sc = p.op("tensor", lambda e, pidx=pidx, half=half, pib=pib,
                                     bb=bb, pe_=pe_, pst=pst, tb=tb: e.matmul(
                            ps_sc[2 * tb + half][:, :], sel_ev[pib][:, :],
                            y_sb[bb][:, (pe_ * 2 + pst) * D + half * 512:
                                      (pe_ * 2 + pst) * D + (half + 1) * 512],
                            start=(pidx == 0), stop=(pidx == 15)), waits=w_m)
                    sev_free[pib] = mm_sc
                    if tc == 0 and pidx == 0:
                        y_gate = mm_sc
                if idx < 16:
                    pend = (idx, ib, e_, st, sev)
            ob = tc % 2
            tb = tc % 2
            o1 = p.op("vector", lambda e, ob=ob, tb=tb: e.tensor_copy(
                oadd[ob][:, 0:512], ps_sc[2 * tb][:, :]),
                waits=[mm_sc, oadd_free[ob]])
            o2 = p.op("vector", lambda e, ob=ob, tb=tb: e.tensor_copy(
                oadd[ob][:, 512:1024], ps_sc[2 * tb + 1][:, :]))
            psc_free[2 * tb] = o1
            psc_free[2 * tb + 1] = o2
            ti = blk * 4 + tc
            acc = p.dma("gpsimd", out_t[ti * 128:(ti + 1) * 128, :],
                        oadd[ob][:, :], accum_op=ALU.add,
                        waits=[o1, o2, p.drain("gpsimd")])
            oadd_free[ob] = acc
        ysb_free[bb] = (PE, PE.v)

    p.wt("gpsimd", (p.d["gpsimd"], p.d["gpsimd"].v))
    p.wt("sync", (DS, DS.v))
    p.wt("scalar", (DSC, DSC.v))
    esD2.close()
    esDR.close()

    es0.close()

    with nc.Block() as block:
        p.emit(block)
    return nc


_NC_CACHE = None


def kernel(L0_attn_x, L0_attn_w_qkv, L0_attn_w_out, L0_attn_b_out,
           L0_ffn_Wr, L0_ffn_W1, L0_ffn_W2):
    global _NC_CACHE
    X = np.ascontiguousarray(np.asarray(L0_attn_x, dtype=np.float32))
    Wqkv = np.ascontiguousarray(np.asarray(L0_attn_w_qkv, dtype=np.float32))
    Wout = np.ascontiguousarray(np.asarray(L0_attn_w_out, dtype=np.float32))
    Bout = np.ascontiguousarray(np.asarray(L0_attn_b_out, dtype=np.float32)).reshape(1, D)
    Wr = np.ascontiguousarray(np.asarray(L0_ffn_Wr, dtype=np.float32))
    W1 = np.asarray(L0_ffn_W1, dtype=np.float32)
    W2 = np.ascontiguousarray(np.asarray(L0_ffn_W2, dtype=np.float32))
    wqkv_pk = np.ascontiguousarray(Wqkv.reshape(DC, 128, 24, 128).transpose(2, 0, 1, 3))
    w1_pk = np.ascontiguousarray(
        W1.reshape(NEXP, DC, 128, FC, 128).transpose(0, 3, 1, 2, 4))
    if _NC_CACHE is None:
        _NC_CACHE = build_nc()
    nc = _NC_CACHE
    in_maps = []
    for c in range(NCORES):
        xc = np.ascontiguousarray(X[c])
        in_maps.append({
            "x_own": xc,
            "x_fm": np.ascontiguousarray(xc.T),
            "wqkv_pk": wqkv_pk,
            "w_out": Wout,
            "b_out": Bout,
            "wr_in": Wr,
            "w1_pk": w1_pk,
            "w2_in": W2,
        })
    res = run_bass_kernel_spmd(nc, in_maps, core_ids=list(range(NCORES)))
    out = np.stack([np.asarray(res.results[c]["out"]) for c in range(NCORES)], axis=0)
    return out


# version 09
# speedup vs baseline: 1.2113x; 1.0661x over previous
import contextlib
import numpy as np
import ml_dtypes

import concourse.bass as bass
import concourse.mybir as mybir
from concourse.bass_utils import run_bass_kernel_spmd

F32 = mybir.dt.float32
BF16 = mybir.dt.bfloat16
I32 = mybir.dt.int32
AF = mybir.ActivationFunctionType
ALU = mybir.AluOpType
AX = mybir.AxisListType

NCORES = 8
TOK = 4096
D = 1024
DC = 8
NPAIR = 8
NQB = 8
QBS = 512
NKC = 32
NTT = 32
FFN = 4096
FC = 32
NEXP = 8
CAP = 10240
NST = CAP // 128
TT_ALL = 32768
EPS = 1e-5
INVF = float(1 << 24)
BIS_ROUNDS = 34
SLOTB = 256
SLOT_E = NQB * SLOTB
NST_E = SLOT_E // 128
NSB = SLOT_E // 512


class Sem:
    def __init__(self, nc, name):
        self.h = nc.alloc_semaphore(name)
        self.v = 0

    def inc(self, n=1):
        self.v += n
        return self.v


NONE_MS = None


class Prog:
    ENGINES = ("sync", "tensor", "vector", "scalar", "gpsimd")

    def __init__(self, nc):
        self.nc = nc
        self.items = []
        self.e = {n: Sem(nc, "se_" + n) for n in self.ENGINES}
        self.d = {n: Sem(nc, "sd_" + n) for n in ("sync", "scalar", "gpsimd")}

    @staticmethod
    def _clean(waits):
        return tuple(w for w in waits if w is not None and w[1] > 0)

    def op(self, eng, fn, waits=()):
        s = self.e[eng]
        v = s.inc(1)
        self.items.append((eng, self._clean(waits), fn, ((s, 1),)))
        return (s, v)

    def dma(self, eng, out, in_, waits=(), **kw):
        s = self.d[eng]
        v = s.inc(16)
        self.items.append((eng, self._clean(waits),
                           lambda e, out=out, in_=in_, kw=kw: e.dma_start(out=out, in_=in_, **kw),
                           ((s, 16),)))
        return (s, v)

    def idma(self, out, out_off, in_, in_off, bounds, waits=(), **kw):
        s = self.d["gpsimd"]
        v = s.inc(16)

        def fn(e, out=out, oo=out_off, in_=in_, io=in_off, b=bounds, kw=kw):
            try:
                return e.indirect_dma_start(out, oo, in_, io, bounds_check=b,
                                            oob_is_err=False, **kw)
            except Exception as ex:
                raise RuntimeError(
                    f"idma failed: out={out.tensor.name if hasattr(out, 'tensor') else out} "
                    f"in={in_.tensor.name if hasattr(in_, 'tensor') else in_} "
                    f"kw={kw}: {ex}") from ex
        self.items.append(("gpsimd", self._clean(waits), fn, ((s, 16),)))
        return (s, v)

    def wt(self, eng, *ms):
        self.items.append((eng, self._clean(ms), lambda e: e.nop(), ()))

    def drain(self, eng):
        s = self.d[eng]
        return (s, s.v)

    def barrier(self):
        for n in ("sync", "scalar", "gpsimd"):
            self.wt(n, (self.d[n], self.d[n].v))
        vals = []
        for n in self.ENGINES:
            s = self.e[n]
            v = s.inc(1)
            self.items.append((n, (), lambda e: e.nop(), ((s, 1),)))
            vals.append((s, v))
        for n in self.ENGINES:
            self.items.append((n, tuple(vals), lambda e: e.nop(), ()))

    def emit(self, block):
        for ename in self.ENGINES:
            def make(ename):
                def run(eng):
                    for (e, waits, fn, incs) in self.items:
                        if e != ename:
                            continue
                        for (sem, val) in waits:
                            eng.wait_ge(sem.h, val)
                        ins = fn(eng)
                        for (sem, n) in incs:
                            ins = ins.then_inc(sem.h, n)
                return run
            getattr(block, ename)(make(ename))


def build_nc():
    nc = bass.Bass("TRN2", target_bir_lowering=False, debug=False)

    x_own = nc.dram_tensor("x_own", [TOK, D], F32, kind="ExternalInput").ap()
    x_fm = nc.dram_tensor("x_fm", [D, TOK], F32, kind="ExternalInput").ap()
    wqkv_pk = nc.dram_tensor("wqkv_pk", [24, DC, 128, 128], F32, kind="ExternalInput").ap()
    w_out = nc.dram_tensor("w_out", [D, D], F32, kind="ExternalInput").ap()
    b_out = nc.dram_tensor("b_out", [1, D], F32, kind="ExternalInput").ap()
    wr_in = nc.dram_tensor("wr_in", [D, NEXP], F32, kind="ExternalInput").ap()
    w1_pk = nc.dram_tensor("w1_pk", [NEXP, FC, DC, 128, 128], F32, kind="ExternalInput").ap()
    w2_in = nc.dram_tensor("w2_in", [NEXP, FFN, D], F32, kind="ExternalInput").ap()
    out_t = nc.dram_tensor("out", [TOK, D], F32, kind="ExternalOutput").ap()

    q_dram = nc.dram_tensor("q_dram", [NPAIR, 128, TOK], BF16, kind="Internal").ap()
    k_dram = nc.dram_tensor("k_dram", [NPAIR, 128, TOK], BF16, kind="Internal").ap()
    v_dram = nc.dram_tensor("v_dram", [NPAIR, NKC, 128, 130], BF16, kind="Internal").ap()
    nb_dram = nc.dram_tensor("nb_dram", [16, 1], F32, kind="Internal").ap()
    cw_dram = nc.dram_tensor("cw_dram", [1, NEXP], F32, kind="Internal").ap()
    cin_g = nc.dram_tensor("cin_g", [128, 256], F32, kind="Internal").ap()
    cout_g = nc.dram_tensor("cout_g", [NCORES * 128, 256], F32, kind="Internal").ap()
    ctot_dram = nc.dram_tensor("ctot_dram", [1, NEXP], F32, kind="Internal").ap()
    xn_dram = nc.dram_tensor("xn_dram", [128, DC * TOK], BF16, kind="Internal").ap()
    xn_tm_dram = nc.dram_tensor("xn_tm_dram", [TOK, D], BF16, kind="Internal").ap()
    xc_dram = nc.dram_tensor("xc_dram", [NEXP, 128, DC * SLOT_E], BF16,
                             kind="Internal").ap()
    y_dram = nc.dram_tensor("y_dram", [NEXP, NST_E, 128, D], BF16,
                            kind="Internal").ap()

    ident_c = nc.inline_tensor(np.eye(128, dtype=ml_dtypes.bfloat16), name="ident_c").ap()
    pairsel_np = np.zeros((128, 2), dtype=ml_dtypes.bfloat16)
    pairsel_np[0:64, 0] = 1
    pairsel_np[64:128, 1] = 1
    pairsel_c = nc.inline_tensor(pairsel_np, name="pairsel_c").ap()
    masks_np = np.zeros((128, 4 * QBS), dtype=ml_dtypes.bfloat16)
    kparr = np.arange(128)[:, None]
    qfarr = np.arange(QBS)[None, :]
    for m in range(4):
        masks_np[:, m * QBS:(m + 1) * QBS] = (qfarr >= kparr + 128 * m).astype(ml_dtypes.bfloat16)
    masks_c = nc.inline_tensor(masks_np, name="masks_c").ap()
    esel_np = np.zeros((16, 8 * 128), dtype=ml_dtypes.bfloat16)
    for c in range(8):
        esel_np[2 * c, c * 128:c * 128 + 64] = 1
        esel_np[2 * c + 1, c * 128 + 64:c * 128 + 128] = 1
    esel_c = nc.inline_tensor(esel_np, name="esel_c").ap()
    ones_row_c = nc.inline_tensor(np.ones((1, 128), np.float32), name="ones_row_c").ap()
    tri_np = (np.arange(128)[:, None] < np.arange(128)[None, :]).astype(np.float32)
    tri_c = nc.inline_tensor(tri_np, name="tri_c").ap()
    iota2_c = nc.inline_tensor(np.arange(SLOTB, dtype=np.float32)[None, :],
                               name="iota2_c").ap()

    p = Prog(nc)
    PE, VE, AC, GP = p.e["tensor"], p.e["vector"], p.e["scalar"], p.e["gpsimd"]
    DS, DSC, DG = p.d["sync"], p.d["scalar"], p.d["gpsimd"]

    es0 = contextlib.ExitStack()
    sb0 = lambda name, shape, dt: es0.enter_context(nc.sbuf_tensor(name, shape, dt))

    pairsel_s = sb0("pairsel_s", [128, 2], BF16)
    ident_s = sb0("ident_s", [128, 128], BF16)
    onesr_s = sb0("onesr_s", [1, 128], F32)
    negb_bc = sb0("negb_bc", [128, 16], F32)
    negb_pair = sb0("negb_pair", [128, 8], F32)
    stats_m = sb0("stats_m", [128, NTT], F32)
    stats_r = sb0("stats_r", [128, NTT], F32)
    eps_col = sb0("eps_col", [128, 1], F32)
    G_own = sb0("G_own", [128, 256], F32)
    wsel = sb0("wsel", [128, 256], F32)

    p.dma("sync", pairsel_s[:, :], pairsel_c[:, :])
    p.dma("sync", ident_s[:, :], ident_c[:, :])
    c_ld = p.dma("sync", onesr_s[:, :], ones_row_c[:, :])
    eps_ms = p.op("vector", lambda e: e.memset(eps_col[:, :], EPS))

    esA = contextlib.ExitStack()
    sbA = lambda name, shape, dt: esA.enter_context(nc.sbuf_tensor(name, shape, dt))
    xn_fm = sbA("xn_fm", [128, DC * TOK], BF16)
    xt = [sbA(f"xt{i}", [128, D], F32) for i in range(3)]
    scrA = sbA("scrA", [128, D], F32)
    sq_s = sbA("sq_s", [128, D], F32)
    xn32 = sbA("xn32", [128, D], F32)
    xnbf = [sbA(f"xnbf{i}", [128, D], BF16) for i in range(2)]
    st_sum = sbA("st_sum", [128, NTT], F32)
    st_sq = sbA("st_sq", [128, NTT], F32)
    st_var = sbA("st_var", [128, NTT], F32)
    st_sd = sbA("st_sd", [128, NTT], F32)
    m2_s = sbA("m2_s", [128, NTT], F32)
    ps_tr = [esA.enter_context(nc.psum_tensor(f"ps_tr{i}", [128, 128], BF16)) for i in range(2)]
    ps_qkv = [esA.enter_context(nc.psum_tensor(f"ps_qkv{i}", [128, QBS], F32)) for i in range(4)]
    ps_nrm = esA.enter_context(nc.psum_tensor("ps_nrm", [128, QBS], F32))


    ld_x = [NONE_MS] * NTT
    xn_done = [NONE_MS] * NTT
    xnbf_free = [NONE_MS, NONE_MS]
    trev = [NONE_MS, NONE_MS]
    ld_gate = [c_ld]
    for i in range(NTT):
        ld_x[i] = p.dma("sync", xt[i % 3][:, :], x_own[i * 128:(i + 1) * 128, :],
                        waits=ld_gate + ([] if i < 3 else [xn_done[i - 3]]))
        v_sum = p.op("vector", lambda e, i=i: e.tensor_scalar(
            scrA[:, :], xt[i % 3][:, :], 1.0, None, ALU.mult, ALU.add,
            accum_out=st_sum[:, i:i + 1]), waits=[ld_x[i]])
        a_sq = p.op("scalar", lambda e, i=i: e.activation(
            sq_s[:, :], xt[i % 3][:, :], AF.Square, accum_out=st_sq[:, i:i + 1]),
            waits=[ld_x[i]])
        ld_gate = [v_sum, a_sq]
        v_m = p.op("vector", lambda e, i=i: e.tensor_scalar(
            stats_m[:, i:i + 1], st_sum[:, i:i + 1], 1.0 / D, None, ALU.mult),
            waits=[v_sum])
        v_e2 = p.op("vector", lambda e, i=i: e.tensor_scalar(
            st_var[:, i:i + 1], st_sq[:, i:i + 1], 1.0 / D, None, ALU.mult),
            waits=[a_sq])
        v_m2 = p.op("vector", lambda e, i=i: e.tensor_tensor(
            m2_s[:, i:i + 1], stats_m[:, i:i + 1], stats_m[:, i:i + 1], ALU.mult),
            waits=[v_m])
        v_var = p.op("vector", lambda e, i=i: e.tensor_tensor(
            st_var[:, i:i + 1], st_var[:, i:i + 1], m2_s[:, i:i + 1], ALU.subtract),
            waits=[v_m2, v_e2])
        a_sd = p.op("scalar", lambda e, i=i: e.activation(
            st_sd[:, i:i + 1], st_var[:, i:i + 1], AF.Sqrt, bias=eps_col[:, 0:1]),
            waits=[v_var, eps_ms])
        v_r = p.op("vector", lambda e, i=i: e.reciprocal(
            stats_r[:, i:i + 1], st_sd[:, i:i + 1]), waits=[a_sd])
        v_xn = p.op("vector", lambda e, i=i: e.tensor_scalar(
            xn32[:, :], xt[i % 3][:, :], stats_m[:, i:i + 1], stats_r[:, i:i + 1],
            ALU.subtract, ALU.mult), waits=[v_r])
        v_bf = p.op("vector", lambda e, i=i: e.tensor_copy(
            xnbf[i % 2][:, :], xn32[:, :]),
            waits=[v_xn, xnbf_free[i % 2], p.drain("scalar")])
        p.dma("scalar", xn_tm_dram[i * 128:(i + 1) * 128, :], xnbf[i % 2][:, :],
              waits=[v_bf])
        last = NONE_MS
        for dc in range(DC):
            tp = p.op("tensor", lambda e, i=i, dc=dc: e.transpose(
                ps_tr[dc % 2][:, :], xnbf[i % 2][:, dc * 128:(dc + 1) * 128], ident_s[:, :]),
                waits=[v_bf, trev[dc % 2]])
            trev[dc % 2] = p.op("vector", lambda e, i=i, dc=dc: e.tensor_copy(
                xn_fm[:, dc * TOK + i * 128: dc * TOK + (i + 1) * 128], ps_tr[dc % 2][:, :]),
                waits=[tp])
            last = trev[dc % 2]
            if dc == DC - 1:
                xnbf_free[i % 2] = tp
        xn_done[i] = last
    xn_all = xn_done[NTT - 1]

    esA2 = contextlib.ExitStack()
    sbA2 = lambda name, shape, dt: esA2.enter_context(nc.sbuf_tensor(name, shape, dt))
    xfmb = [sbA2(f"xfmb{i}", [128, TOK], F32) for i in range(2)]
    wr_s = sbA2("wr_s", [128, DC * NEXP], F32)
    ones_col = sbA2("ones_col", [128, 1], F32)
    cwb = sbA2("cwb", [128, NEXP], F32)
    lgraw = sbA2("lgraw", [128, 256], F32)
    lgev = sbA2("lgev", [128, 256], F32)
    cwev = sbA2("cwev", [1, NEXP], F32)
    exps_a = sbA2("exps_a", [128, NEXP], F32)
    probs_a = sbA2("probs_a", [128, NEXP], F32)
    ge8_a = sbA2("ge8_a", [128, NEXP], F32)
    pm8_a = sbA2("pm8_a", [128, NEXP], F32)
    gm8_a = sbA2("gm8_a", [128, NEXP], F32)
    scr8 = sbA2("scr8", [128, NEXP], F32)
    lg2 = sbA2("lg2", [128, NEXP], F32)
    rt = sbA2("rt", [128, 8 * NTT], F32)
    ps_lg = [ps_qkv[0][:, 0:256], ps_qkv[1][:, 0:256]]
    ps_cw = ps_nrm

    wr_ld = NONE_MS
    for dc in range(DC):
        wr_ld = p.dma("sync", wr_s[:, dc * NEXP:(dc + 1) * NEXP],
                      wr_in[dc * 128:(dc + 1) * 128, :])
    oc_ms = p.op("vector", lambda e: e.memset(ones_col[:, :], 1.0))
    lg_zero = p.op("vector", lambda e: e.memset(lgraw[:, :], 0.0))

    xfm_ld = [NONE_MS, NONE_MS]
    xfm_free = [NONE_MS, NONE_MS]
    xfm_gate = wr_ld
    lg_acc = lg_zero
    cw_mm = NONE_MS
    psg_free = [NONE_MS, NONE_MS]
    for dc in range(DC):
        xfm_ld[dc % 2] = p.dma("sync", xfmb[dc % 2][:, :],
                               x_fm[dc * 128:(dc + 1) * 128, :],
                               waits=[xfm_free[dc % 2], xfm_gate])
        cw_mm = p.op("tensor", lambda e, dc=dc: e.matmul(
            ps_cw[0:1, 0:NEXP], ones_col[:, :], wr_s[:, dc * NEXP:(dc + 1) * NEXP],
            start=(dc == 0), stop=(dc == DC - 1)),
            waits=([wr_ld, oc_ms, xn_all] if dc == 0 else []))
        mmlast = NONE_MS
        for t in range(NTT):
            mmlast = p.op("tensor", lambda e, dc=dc, t=t: e.matmul(
                ps_lg[dc % 2][:, t * NEXP:(t + 1) * NEXP],
                xfmb[dc % 2][:, t * 128:(t + 1) * 128],
                wr_s[:, dc * NEXP:(dc + 1) * NEXP], start=True, stop=True),
                waits=([xfm_ld[dc % 2], psg_free[dc % 2]] if t == 0 else []))
            if t == 0:
                xfm_gate = mmlast
        xfm_free[dc % 2] = mmlast
        evg = p.op("vector", lambda e, dc=dc: e.tensor_copy(
            lgev[:, :], ps_lg[dc % 2][:, :]), waits=[mmlast])
        psg_free[dc % 2] = evg
        lg_acc = p.op("vector", lambda e: e.tensor_tensor(
            lgraw[:, :], lgraw[:, :], lgev[:, :], ALU.add), waits=[evg, lg_acc])
    cw_ev = p.op("vector", lambda e: e.tensor_copy(cwev[0:1, :], ps_cw[0:1, 0:NEXP]),
                 waits=[cw_mm])
    cw_st = p.dma("gpsimd", cw_dram[0:1, :], cwev[0:1, :], waits=[cw_ev])
    cw_bc = p.dma("sync", cwb[:, :], cw_dram[0:1, :].to_broadcast([128, NEXP]),
                  waits=[cw_st])

    g_done = NONE_MS
    for t in range(NTT):
        c0 = 8 * t
        w0 = [lg_acc, cw_bc] if t == 0 else [g_done]
        v1 = p.op("vector", lambda e, t=t: e.tensor_scalar(
            scr8[:, :], cwb[:, :], stats_m[:, t:t + 1], None, ALU.mult), waits=w0)
        v2 = p.op("vector", lambda e, t=t: e.tensor_tensor(
            lg2[:, :], lgraw[:, t * NEXP:(t + 1) * NEXP], scr8[:, :], ALU.subtract),
            waits=[v1])
        v3 = p.op("vector", lambda e, t=t: e.tensor_scalar(
            lg2[:, :], lg2[:, :], stats_r[:, t:t + 1], None, ALU.mult), waits=[v2])
        v4 = p.op("vector", lambda e, t=t, c0=c0: e.reduce_max(
            rt[:, c0:c0 + 1], lg2[:, :], axis=AX.X, negate=True), waits=[v3])
        a1 = p.op("scalar", lambda e, t=t, c0=c0: e.activation(
            exps_a[:, :], lg2[:, :], AF.Exp, bias=rt[:, c0:c0 + 1],
            accum_out=rt[:, c0 + 1:c0 + 2]), waits=[v4])
        v5 = p.op("vector", lambda e, c0=c0: e.reciprocal(
            rt[:, c0 + 2:c0 + 3], rt[:, c0 + 1:c0 + 2]), waits=[a1])
        v6 = p.op("vector", lambda e, c0=c0: e.tensor_scalar(
            probs_a[:, :], exps_a[:, :], rt[:, c0 + 2:c0 + 3], None, ALU.mult),
            waits=[v5])
        v7 = p.op("vector", lambda e, c0=c0: e.reduce_max(
            rt[:, c0 + 3:c0 + 4], probs_a[:, :], axis=AX.X), waits=[v6])
        v8 = p.op("vector", lambda e, c0=c0: e.tensor_scalar(
            ge8_a[:, :], probs_a[:, :], rt[:, c0 + 3:c0 + 4], None, ALU.is_ge,
            ALU.add, accum_out=rt[:, c0 + 4:c0 + 5]), waits=[v7])
        v9 = p.op("vector", lambda e: e.tensor_tensor(
            pm8_a[:, :], probs_a[:, :], ge8_a[:, :], ALU.subtract), waits=[v8])
        v10 = p.op("vector", lambda e, c0=c0: e.reduce_max(
            rt[:, c0 + 5:c0 + 6], pm8_a[:, :], axis=AX.X), waits=[v9])
        v11 = p.op("vector", lambda e, c0=c0: e.tensor_scalar(
            rt[:, c0 + 4:c0 + 5], rt[:, c0 + 4:c0 + 5], 2.0, None, ALU.is_ge),
            waits=[v10])
        v12 = p.op("vector", lambda e, c0=c0: e.tensor_tensor(
            rt[:, c0 + 7:c0 + 8], rt[:, c0 + 3:c0 + 4], rt[:, c0 + 5:c0 + 6],
            ALU.subtract), waits=[v11])
        v13 = p.op("vector", lambda e, c0=c0: e.tensor_tensor(
            rt[:, c0 + 7:c0 + 8], rt[:, c0 + 4:c0 + 5], rt[:, c0 + 7:c0 + 8],
            ALU.mult), waits=[v12])
        v14 = p.op("vector", lambda e, c0=c0: e.tensor_tensor(
            rt[:, c0 + 6:c0 + 7], rt[:, c0 + 5:c0 + 6], rt[:, c0 + 7:c0 + 8],
            ALU.add), waits=[v13])
        v15 = p.op("vector", lambda e, c0=c0: e.tensor_scalar(
            gm8_a[:, :], probs_a[:, :], rt[:, c0 + 6:c0 + 7], None, ALU.is_ge),
            waits=[v14])
        g_done = p.op("vector", lambda e, t=t: e.tensor_tensor(
            G_own[:, :].rearrange("q (e t) -> q e t", e=NEXP)[:, :, t],
            probs_a[:, :], gm8_a[:, :], ALU.mult), waits=[v15])

    g_ship = p.dma("gpsimd", cin_g[:, :], G_own[:, :], waits=[g_done])
    cc1 = p.op("gpsimd", lambda e: e.collective_compute(
        "AllGather", ALU.bypass, replica_groups=[list(range(NCORES))],
        ins=[cin_g], outs=[cout_g]), waits=[g_ship])

    esB = contextlib.ExitStack()
    sbB = lambda name, shape, dt: esB.enter_context(nc.sbuf_tensor(name, shape, dt))
    wsl = [sbB(f"wsl{i}", [128, DC * 128], BF16) for i in range(2)]
    ev_bf = [sbB(f"ev_bf{i}", [128, QBS], BF16) for i in range(4)]
    sq_bf = [sbB(f"sq_bf{i}", [128, QBS], BF16) for i in range(2)]
    nmx = sbB("nmx", [2, 128], F32)
    vas = [sbB(f"vas{i}", [128, 4 * 130], BF16) for i in range(2)]
    nb16 = sbB("nb16", [16, 4], F32)
    ps_vtr = ps_tr[0]

    xn_st = p.dma("sync", xn_dram[:, :], xn_fm[:, :], waits=[xn_all])

    wsl_ld = [NONE_MS, NONE_MS]
    wsl_gate = NONE_MS
    nmm_pend = None
    sq_guard = [NONE_MS, NONE_MS]
    ev_tp_guard = [NONE_MS] * 4
    psq_free = [[psg_free[0]], [psg_free[1]], [NONE_MS], [NONE_MS]]
    nrm_mm_prev = NONE_MS
    nrm_red_prev = cw_ev
    vtr_prev = NONE_MS

    for s in range(24):
        buf = s % 2
        wsl_ld[buf] = p.dma("gpsimd",
                            wsl[buf][:, :].rearrange("r (dc c) -> r dc c", dc=DC),
                            wqkv_pk[s].rearrange("dc r c -> r dc c"),
                            waits=([] if s == 0 else [wsl_gate]))
        for qb in range(NQB):
            par = (s * 8 + qb) % 4
            mm = NONE_MS
            for dc in range(DC):
                w_mm = []
                if dc == 0:
                    w_mm = [wsl_ld[buf]] + psq_free[par]
                    if s == 0 and qb == 0:
                        w_mm.append(xn_all)
                mm = p.op("tensor", lambda e, buf=buf, qb=qb, dc=dc, par=par: e.matmul(
                    ps_qkv[par][:, :], wsl[buf][:, dc * 128:(dc + 1) * 128],
                    xn_fm[:, dc * TOK + qb * QBS: dc * TOK + (qb + 1) * QBS],
                    start=(dc == 0), stop=(dc == DC - 1)), waits=w_mm)
                if qb == 0 and dc == 0:
                    wsl_gate = mm
            if s < 16:
                dst = q_dram if s < 8 else k_dram
                pr = s % 8
                ev = p.op("vector", lambda e, par=par: e.tensor_copy(
                    ev_bf[par][:, :], ps_qkv[par][:, :]),
                    waits=[mm, p.drain("scalar")])
                p.dma("scalar", dst[pr][:, qb * QBS:(qb + 1) * QBS], ev_bf[par][:, :],
                      waits=[ev])
                sqb = (s * 8 + qb) % 2
                a_sq2 = p.op("scalar", lambda e, par=par, sqb=sqb: e.activation(
                    sq_bf[sqb][:, :], ps_qkv[par][:, :], AF.Square),
                    waits=[mm, ev, sq_guard[sqb]])
                psq_free[par] = [ev, a_sq2]
                if nmm_pend is not None:
                    (pa, psqb, pcol) = nmm_pend
                    nmm = p.op("tensor", lambda e, psqb=psqb: e.matmul(
                        ps_nrm[0:2, :], pairsel_s[:, :], sq_bf[psqb][:, :],
                        start=True, stop=True), waits=[pa, nrm_red_prev])
                    sq_guard[psqb] = nmm
                    nrm_red_prev = p.op("vector", lambda e, pcol=pcol: e.reduce_max(
                        nmx[0:2, pcol:pcol + 1], ps_nrm[0:2, :], axis=AX.X),
                        waits=[nmm])
                nmm_pend = (a_sq2, sqb, s * 8 + qb)
            else:
                if nmm_pend is not None:
                    (pa, psqb, pcol) = nmm_pend
                    nmm = p.op("tensor", lambda e, psqb=psqb: e.matmul(
                        ps_nrm[0:2, :], pairsel_s[:, :], sq_bf[psqb][:, :],
                        start=True, stop=True), waits=[pa, nrm_red_prev])
                    sq_guard[psqb] = nmm
                    nrm_red_prev = p.op("vector", lambda e, pcol=pcol: e.reduce_max(
                        nmx[0:2, pcol:pcol + 1], ps_nrm[0:2, :], axis=AX.X),
                        waits=[nmm])
                    nmm_pend = None
                pr = s - 16
                ev = p.op("vector", lambda e, par=par: e.tensor_copy(
                    ev_bf[par][:, :], ps_qkv[par][:, :]),
                    waits=[mm, p.drain("scalar"), ev_tp_guard[par]])
                psq_free[par] = [ev]
                vb = par % 2
                mset = p.op("vector", lambda e, vb=vb: e.memset(
                    vas[vb][:, :].rearrange("q (kc h x) -> q kc h x", kc=4, h=2)[:, :, :, 64:65],
                    1.0), waits=[p.drain("scalar")])
                lastv = NONE_MS
                last_tp = NONE_MS
                for tt in range(4):
                    tp = p.op("tensor", lambda e, par=par, tt=tt: e.transpose(
                        ps_vtr[:, :], ev_bf[par][:, tt * 128:(tt + 1) * 128], ident_s[:, :]),
                        waits=[ev, vtr_prev])
                    vtr_prev = p.op("vector", lambda e, vb=vb, tt=tt: e.tensor_copy(
                        vas[vb][:, tt * 130:(tt + 1) * 130].rearrange(
                            "q (h x) -> q h x", h=2)[:, :, 0:64],
                        ps_vtr[:, :].rearrange("q (h x) -> q h x", h=2)),
                        waits=[tp, mset])
                    lastv = vtr_prev
                    last_tp = tp
                ev_tp_guard[par] = last_tp
                p.dma("scalar", v_dram[pr, qb * 4:(qb + 1) * 4].rearrange("kc q c -> q kc c"),
                      vas[vb][:, :].rearrange("q (kc c) -> q kc c", kc=4), waits=[lastv])

    prev_ms = nrm_red_prev
    prev_dma = NONE_MS
    for pr in range(8):
        m1 = p.op("vector", lambda e, pr=pr: e.reduce_max(
            nb16[0:2, 0:1], nmx[0:2, pr * 8:(pr + 1) * 8], axis=AX.X),
            waits=[prev_ms, prev_dma])
        m2 = p.op("vector", lambda e, pr=pr: e.reduce_max(
            nb16[0:2, 1:2], nmx[0:2, 64 + pr * 8: 64 + (pr + 1) * 8], axis=AX.X),
            waits=[m1])
        m3 = p.op("vector", lambda e: e.tensor_tensor(
            nb16[0:2, 2:3], nb16[0:2, 0:1], nb16[0:2, 1:2], ALU.mult), waits=[m2])
        aB = p.op("scalar", lambda e: e.activation(
            nb16[0:2, 3:4], nb16[0:2, 2:3], AF.Sqrt), waits=[m3])
        m4 = p.op("vector", lambda e: e.tensor_scalar(
            nb16[0:2, 3:4], nb16[0:2, 3:4], -0.125, None, ALU.mult), waits=[aB])
        prev_dma = p.dma("gpsimd", nb_dram[2 * pr:2 * pr + 2, :], nb16[0:2, 3:4],
                         waits=[m4])
        prev_ms = m4
    nb_ld = p.dma("sync", negb_bc[:, :],
                  nb_dram.rearrange("h one -> one h").to_broadcast([128, 16]),
                  waits=[prev_dma])
    p.op("vector", lambda e: e.tensor_tensor(
        negb_pair[:, :],
        negb_bc[:, :].rearrange("q (p h) -> q p h", h=2)[:, :, 0],
        negb_bc[:, :].rearrange("q (p h) -> q p h", h=2)[:, :, 1], ALU.min),
        waits=[nb_ld])

    p.barrier()
    esB.close()
    esA2.close()
    esA.close()

    esM = contextlib.ExitStack()
    sbM = lambda name, shape, dt: esM.enter_context(nc.sbuf_tensor(name, shape, dt))
    masks_s = sbM("masks_s", [128, 4 * QBS], BF16)
    b_bc = sbM("b_bc", [128, D], F32)
    esel_s = sbM("esel_s", [16, 8 * 128], BF16)
    Gs = sbM("Gs", [128, 2048], F32)
    th_lo = sbM("th_lo", [128, NEXP], F32)
    th_hi = sbM("th_hi", [128, NEXP], F32)
    th_md = sbM("th_md", [128, NEXP], F32)
    cnt_p = sbM("cnt_p", [128, NEXP], F32)
    f_scr = sbM("f_scr", [128, 256], F32)
    cnt_tot = sbM("cnt_tot", [1, NEXP], F32)
    cnt_b = sbM("cnt_b", [128, NEXP], F32)
    ge_b = sbM("ge_b", [128, NEXP], F32)
    u_s = sbM("u_s", [128, NEXP], F32)
    F_e = sbM("F_e", [128, 256], F32)
    Fp_e = sbM("Fp_e", [128, 256], F32)

    esC = contextlib.ExitStack()
    sbC = lambda name, shape, dt: esC.enter_context(nc.sbuf_tensor(name, shape, dt))
    qs = [sbC(f"qs{i}", [128, QBS], BF16) for i in range(2)]
    ks = [sbC(f"ks{i}", [128, TOK], BF16) for i in range(2)]
    vs = [sbC(f"vs{i}", [128, NKC * 130], BF16) for i in range(2)]
    pt = [sbC(f"pt{i}", [128, 1024], BF16) for i in range(4)]
    attn_raw = sbC("attn_raw", [128, 8 * QBS], BF16)
    attn_nrm = sbC("attn_nrm", [128, 8 * QBS], BF16)
    dmat = sbC("dmat", [16, QBS], F32)
    dtmp = [sbC(f"dtmp{i}", [1, QBS], F32) for i in range(2)]
    rbf = sbC("rbf", [16, QBS], BF16)
    rexp_bf = sbC("rexp_bf", [128, QBS], BF16)
    xres = [sbC(f"xres{i}", [128, D], F32) for i in range(2)]
    out32 = [sbC(f"out32_{i}", [128, D], F32) for i in range(2)]
    wout_bf = sbC("wout_bf", [128, DC * D], BF16)
    ps_s = [esC.enter_context(nc.psum_tensor(f"ps_s{i}", [128, 1024], F32)) for i in range(2)]
    ps_o = [esC.enter_context(nc.psum_tensor(f"ps_o{i}", [128, QBS], F32)) for i in range(2)]
    ps_op = [esC.enter_context(nc.psum_tensor(f"ps_op{i}", [128, QBS], F32)) for i in range(2)]

    bld = p.dma("sync", b_bc[:, :], b_out[0:1, :].to_broadcast([128, D]))
    masks_ld = p.dma("sync", masks_s[:, :], masks_c[:, :])
    esel_ld = p.dma("sync", esel_s[:, :], esel_c[:, :])
    wout_ld = NONE_MS
    for c in range(DC):
        wout_ld = p.dma("gpsimd", wout_bf[:, c * D:(c + 1) * D],
                        w_out[c * 128:(c + 1) * 128, :])
    gs_ld = p.dma("gpsimd", Gs[:, :].rearrange("q (r c) -> q r c", r=NCORES),
                  cout_g.rearrange("(r q) c -> q r c", q=128),
                  waits=[cc1, wout_ld])
    gm0 = p.op("vector", lambda e: e.memset(th_lo[:, :], 0.0))
    gm1 = p.op("vector", lambda e: e.memset(th_hi[:, :], 1.0))
    gm2 = p.op("vector", lambda e: e.memset(th_md[:, :], 0.5))

    def gs_e(e_):
        return Gs[:, :].rearrange("q (r e t) -> q r e t", r=NCORES, e=NEXP)[:, :, e_, :]

    bis = {"prev": gm2, "bcast": gs_ld, "d3": NONE_MS}
    bis_steps = []

    def _mk_count(rnd):
        def fn():
            pr = bis["prev"]
            for e_ in range(NEXP):
                pr = p.op("vector", lambda en, e_=e_: en.tensor_scalar(
                    f_scr[:, :].rearrange("q (r t) -> q r t", r=NCORES),
                    gs_e(e_), th_md[:, e_:e_ + 1], None, ALU.is_ge,
                    ALU.add, accum_out=cnt_p[:, e_:e_ + 1]),
                    waits=([bis["prev"], bis["bcast"], gs_ld, gm0, gm1]
                           if e_ == 0 else []))
            rc = p.op("gpsimd", lambda en: en.reduce_sum(
                cnt_tot[0:1, :], cnt_p[:, :], axis=AX.C),
                waits=[pr, bis["d3"]])
            d3 = p.dma("gpsimd", ctot_dram[0:1, :], cnt_tot[0:1, :], waits=[rc])
            bis["d3"] = d3
            bis["bcast"] = p.dma("gpsimd", cnt_b[:, :],
                                 ctot_dram[0:1, :].to_broadcast([128, NEXP]),
                                 waits=[d3])
            bis["prev"] = pr
        return fn

    def _mk_update(rnd):
        def fn():
            g1 = p.op("vector", lambda en: en.tensor_scalar(
                ge_b[:, :], cnt_b[:, :], float(CAP), None, ALU.is_ge),
                waits=[bis["bcast"], bis["prev"]])
            g2_ = p.op("vector", lambda en: en.tensor_tensor(
                u_s[:, :], th_md[:, :], th_lo[:, :], ALU.subtract), waits=[g1])
            g3 = p.op("vector", lambda en: en.tensor_tensor(
                u_s[:, :], ge_b[:, :], u_s[:, :], ALU.mult), waits=[g2_])
            g4 = p.op("vector", lambda en: en.tensor_tensor(
                th_lo[:, :], th_lo[:, :], u_s[:, :], ALU.add), waits=[g3])
            g5 = p.op("vector", lambda en: en.tensor_tensor(
                u_s[:, :], th_hi[:, :], th_md[:, :], ALU.subtract), waits=[g4])
            g6 = p.op("vector", lambda en: en.tensor_tensor(
                u_s[:, :], ge_b[:, :], u_s[:, :], ALU.mult), waits=[g5])
            g7 = p.op("vector", lambda en: en.tensor_tensor(
                th_hi[:, :], th_md[:, :], u_s[:, :], ALU.add), waits=[g6])
            g8 = p.op("vector", lambda en: en.tensor_tensor(
                u_s[:, :], th_lo[:, :], th_hi[:, :], ALU.add), waits=[g7])
            bis["prev"] = p.op("vector", lambda en: en.tensor_scalar(
                th_md[:, :], u_s[:, :], 0.5, None, ALU.mult), waits=[g8])
        return fn

    for rnd in range(BIS_ROUNDS):
        bis_steps.append(_mk_count(rnd))
        bis_steps.append(_mk_update(rnd))


    strip_free = [NONE_MS, NONE_MS]
    pt_free = [NONE_MS] * 4
    s_free = [NONE_MS, NONE_MS]
    o_free = [NONE_MS, NONE_MS]
    dmat_free = NONE_MS
    op_free = [NONE_MS, NONE_MS]

    lq_gate = NONE_MS
    for qb in range(NQB):
        nkc = 4 * (qb + 1)
        for pr in range(NPAIR):
            sp = pr % 2
            lq = p.dma("sync", qs[sp][:, :], q_dram[pr][:, qb * QBS:(qb + 1) * QBS],
                       waits=[strip_free[sp], lq_gate])
            lk = p.dma("sync", ks[sp][:, 0:nkc * 128], k_dram[pr][:, 0:nkc * 128])
            lv = p.dma("sync",
                       vs[sp][:, 0:nkc * 130].rearrange("q (kc c) -> q kc c", kc=nkc),
                       v_dram[pr, 0:nkc].rearrange("kc q c -> q kc c"))
            last_av = NONE_MS
            av_pend = []
            for kc in range(nkc):
                par = kc % 2
                pp = kc % 4
                w_s = [s_free[par]]
                if kc == 0:
                    w_s += [p.drain("sync")]
                mmA = p.op("tensor", lambda e, sp=sp, kc=kc, par=par: e.matmul(
                    ps_s[par][:, 0:512], ks[sp][0:64, kc * 128:(kc + 1) * 128],
                    qs[sp][0:64, :], start=True, stop=True, tile_position=(0, 0)),
                    waits=w_s)
                if kc == 0:
                    lq_gate = mmA
                mmB = p.op("tensor", lambda e, sp=sp, kc=kc, par=par: e.matmul(
                    ps_s[par][:, 512:1024], ks[sp][64:128, kc * 128:(kc + 1) * 128],
                    qs[sp][64:128, :], start=True, stop=True, tile_position=(64, 0)))
                eB = p.op("scalar", lambda e, pp=pp, par=par, pr=pr: e.activation(
                    pt[pp][:, 0:1024], ps_s[par][:, 0:1024], AF.Exp,
                    bias=negb_pair[:, pr:pr + 1], scale=0.125),
                    waits=[mmB, pt_free[pp]])
                s_free[par] = eB
                w_av = [eB]
                if kc >= 4 * qb:
                    m = kc - 4 * qb
                    p.op("vector", lambda e, pp=pp, m=m: e.tensor_tensor(
                        pt[pp][:, 0:512], pt[pp][:, 0:512],
                        masks_s[:, m * QBS:(m + 1) * QBS], ALU.mult),
                        waits=[eB])
                    mk2 = p.op("vector", lambda e, pp=pp, m=m: e.tensor_tensor(
                        pt[pp][:, 512:1024], pt[pp][:, 512:1024],
                        masks_s[:, m * QBS:(m + 1) * QBS], ALU.mult))
                    w_av = [mk2]
                if kc == 0:
                    w_av = w_av + [o_free[0], o_free[1]]
                av_pend.append((kc, pp, w_av))
                for (kcp, ppp, w_avp) in av_pend if kc == nkc - 1 else av_pend[:-3]:
                    avA = p.op("tensor", lambda e, sp=sp, kc=kcp, nkc=nkc, pp=ppp: e.matmul(
                        ps_o[0][0:65, :], vs[sp][:, kc * 130: kc * 130 + 65],
                        pt[pp][:, 0:512], start=(kc == 0), stop=(kc == nkc - 1)),
                        waits=w_avp)
                    avB = p.op("tensor", lambda e, sp=sp, kc=kcp, nkc=nkc, pp=ppp: e.matmul(
                        ps_o[1][0:65, :], vs[sp][:, kc * 130 + 65: kc * 130 + 130],
                        pt[pp][:, 512:1024], start=(kc == 0), stop=(kc == nkc - 1)))
                    pt_free[ppp] = avB
                    last_av = avB
                av_pend = [] if kc == nkc - 1 else av_pend[-3:]
            strip_free[sp] = last_av
            ev1 = p.op("vector", lambda e, pr=pr: e.tensor_copy(
                attn_raw[0:64, pr * QBS:(pr + 1) * QBS], ps_o[0][0:64, :]),
                waits=[last_av])
            ev2 = p.op("vector", lambda e, pr=pr: e.tensor_copy(
                attn_raw[64:128, pr * QBS:(pr + 1) * QBS], ps_o[1][0:64, :]))
            d1 = p.op("vector", lambda e: e.tensor_copy(
                dtmp[0][0:1, :], ps_o[0][64:65, :]), waits=[p.drain("scalar")])
            d2 = p.op("vector", lambda e: e.tensor_copy(
                dtmp[1][0:1, :], ps_o[1][64:65, :]))
            o_free[0] = d1
            o_free[1] = d2
            p.dma("scalar", dmat[2 * pr:2 * pr + 1, :], dtmp[0][0:1, :],
                  waits=[d1, dmat_free])
            p.dma("scalar", dmat[2 * pr + 1:2 * pr + 2, :], dtmp[1][0:1, :],
                  waits=[d2])
            if bis_steps:
                bis_steps.pop(0)()
        vR = p.op("vector", lambda e: e.reciprocal(dmat[0:16, :], dmat[0:16, :]),
                  waits=[p.drain("scalar")])
        vRb = p.op("vector", lambda e: e.tensor_copy(rbf[0:16, :], dmat[0:16, :]),
                   waits=[vR])
        dmat_free = vRb
        nrm_done = NONE_MS
        for c in range(DC):
            mmr = p.op("tensor", lambda e, c=c: e.matmul(
                ps_op[0][:, :], esel_s[:, c * 128:(c + 1) * 128], rbf[0:16, :],
                start=True, stop=True), waits=[vRb, op_free[0]])
            rev = p.op("vector", lambda e: e.tensor_copy(rexp_bf[:, :], ps_op[0][:, :]),
                       waits=[mmr])
            nrm_done = p.op("vector", lambda e, c=c: e.tensor_tensor(
                attn_nrm[:, c * QBS:(c + 1) * QBS], attn_raw[:, c * QBS:(c + 1) * QBS],
                rexp_bf[:, :], ALU.mult), waits=[rev])
            op_free[0] = nrm_done
        xres_gate = vR
        for tt in range(4):
            ti = qb * 4 + tt
            ob = tt % 2
            xld = p.dma("scalar", xres[ob][:, :], x_own[ti * 128:(ti + 1) * 128, :],
                        waits=[xres_gate])
            mm_h = [NONE_MS, NONE_MS]
            for half in range(2):
                for c in range(DC):
                    w_op = []
                    if c == 0:
                        w_op = [nrm_done, op_free[half]]
                        if qb == 0 and tt == 0:
                            w_op.append(wout_ld)
                    mm_h[half] = p.op("tensor", lambda e, tt=tt, c=c, half=half: e.matmul(
                        ps_op[half][:, :],
                        attn_nrm[:, c * QBS + tt * 128: c * QBS + (tt + 1) * 128],
                        wout_bf[:, c * D + half * 512: c * D + (half + 1) * 512],
                        start=(c == 0), stop=(c == DC - 1)), waits=w_op)
            va = p.op("vector", lambda e, tt=tt, ob=ob: e.tensor_tensor(
                out32[ob][:, 0:512], ps_op[0][:, :], xres[ob][:, 0:512], ALU.add),
                waits=[mm_h[0], p.drain("scalar"), bld])
            xres_gate = va
            vb2 = p.op("vector", lambda e, tt=tt, ob=ob: e.tensor_tensor(
                out32[ob][:, 512:1024], ps_op[1][:, :], xres[ob][:, 512:1024], ALU.add),
                waits=[mm_h[1]])
            vc1 = p.op("vector", lambda e, ob=ob: e.tensor_tensor(
                out32[ob][:, 0:512], out32[ob][:, 0:512], b_bc[:, 0:512], ALU.add),
                waits=[va])
            vc2 = p.op("vector", lambda e, ob=ob: e.tensor_tensor(
                out32[ob][:, 512:1024], out32[ob][:, 512:1024], b_bc[:, 512:1024],
                ALU.add), waits=[vb2])
            op_free[0] = vc1
            op_free[1] = vc2
            p.dma("scalar", out_t[ti * 128:(ti + 1) * 128, :],
                  out32[ob][:, :], waits=[vc2])

    while bis_steps:
        bis_steps.pop(0)()
    wprev = bis["prev"]
    for e_ in range(NEXP):
        s1 = p.op("vector", lambda en, e_=e_: en.tensor_scalar(
            F_e[:, 0:32], G_own[:, e_ * 32:(e_ + 1) * 32], th_lo[:, e_:e_ + 1],
            None, ALU.is_ge), waits=[wprev])
        s2 = p.op("vector", lambda en, e_=e_: en.tensor_scalar(
            Fp_e[:, 0:32], G_own[:, e_ * 32:(e_ + 1) * 32], 0.0, None, ALU.is_gt),
            waits=[s1])
        s3 = p.op("vector", lambda en: en.tensor_tensor(
            F_e[:, 0:32], F_e[:, 0:32], Fp_e[:, 0:32], ALU.mult), waits=[s2])
        wprev = p.op("vector", lambda en, e_=e_: en.tensor_tensor(
            wsel[:, e_ * 32:(e_ + 1) * 32], G_own[:, e_ * 32:(e_ + 1) * 32],
            F_e[:, 0:32], ALU.mult), waits=[s3])
    wsel_done = wprev

    p.barrier()
    esC.close()
    esM.close()

    esDR = contextlib.ExitStack()
    sbDR = lambda name, shape, dt: esDR.enter_context(nc.sbuf_tensor(name, shape, dt))
    slotidx_sb = sbDR("slotidx_sb", [128, 256], F32)
    gc_sb = sbDR("gc_sb", [128, 128], F32)
    iota2b = sbDR("iota2b", [128, SLOTB], F32)
    ind_all = sbDR("ind_all", [128, 256], F32)

    esD0 = contextlib.ExitStack()
    sbD0 = lambda name, shape, dt: esD0.enter_context(nc.sbuf_tensor(name, shape, dt))
    xn_tm_sb = sbD0("xn_tm_sb", [128, 4 * D], BF16)
    tri_s = sbD0("tri_s", [128, 128], F32)
    onescol_d = sbD0("onescol_d", [128, 1], F32)
    ones32 = sbD0("ones32", [128, 32], F32)
    base_sb = sbD0("base_sb", [128, 64], F32)

    tri_ld = p.dma("sync", tri_s[:, :], tri_c[:, :])
    iota_ld = p.dma("sync", iota2b[:, :],
                    iota2_c[0:1, :].to_broadcast([128, SLOTB]))
    on_ms = p.op("vector", lambda e: e.memset(ones32[:, :], 1.0))
    on1_ms = p.op("vector", lambda e: e.memset(onescol_d[:, :], 1.0))

    esD0i = contextlib.ExitStack()
    sbD0i = lambda name, shape, dt: esD0i.enter_context(nc.sbuf_tensor(name, shape, dt))
    scn_all = sbD0i("scn_all", [128, 256], F32)
    segb = sbD0i("segb", [128, 256], F32)
    rc_all = sbD0i("rc_all", [128, 64], F32)
    esD0p = contextlib.ExitStack()
    ps_xc = [esD0p.enter_context(nc.psum_tensor(f"ps_xc{i}", [128, 512], F32))
             for i in range(4)]
    ps_bs = esD0p.enter_context(nc.psum_tensor("ps_bs", [128, 512], F32))
    ps_gc = [esD0p.enter_context(nc.psum_tensor(f"ps_gc{i}", [128, 512], F32))
             for i in range(3)]

    ia0 = p.op("vector", lambda e: e.tensor_scalar(
        ind_all[:, :], wsel[:, :], 0.0, None, ALU.is_gt), waits=[wsel_done])
    pv = p.op("vector", lambda e: e.memset(segb[:, :], 0.0), waits=[ia0])
    for e_ in range(NEXP):
        pv = p.op("vector", lambda e, e_=e_: e.tensor_tensor_scan(
            scn_all[:, e_ * 32:(e_ + 1) * 32], ind_all[:, e_ * 32:(e_ + 1) * 32],
            ones32[:, :], 0.0, ALU.add, ALU.mult),
            waits=([pv, on_ms] if e_ == 0 else [pv]))
    for e_ in range(NEXP):
        sv = scn_all[:, e_ * 32:(e_ + 1) * 32].rearrange("q (b f) -> q b f", f=4)
        gv = segb[:, e_ * 32:(e_ + 1) * 32].rearrange("q (b f) -> q b f", f=4)
        for j in range(4):
            pv = p.op("vector", lambda e, sv=sv, gv=gv, j=j: e.tensor_copy(
                gv[:, 1:8, j], sv[:, 0:7, 3]), waits=[pv])
        pv = p.op("vector", lambda e, e_=e_, sv=sv, gv=gv: e.tensor_tensor(
            rc_all[:, e_ * 8:(e_ + 1) * 8], sv[:, :, 3], gv[:, :, 0],
            ALU.subtract), waits=[pv])
    bmm0 = p.op("tensor", lambda e: e.matmul(
        ps_bs[:, 0:64], tri_s[:, :], rc_all[:, 0:64], start=True, stop=True),
        waits=[pv, iota_ld])
    bev0 = p.op("vector", lambda e: e.tensor_copy(
        base_sb[:, :], ps_bs[:, 0:64]), waits=[bmm0])
    pv = p.op("vector", lambda e: e.tensor_tensor(
        slotidx_sb[:, :], scn_all[:, :], ind_all[:, :], ALU.subtract),
        waits=[bev0])
    pv = p.op("vector", lambda e: e.tensor_tensor(
        slotidx_sb[:, :], slotidx_sb[:, :], segb[:, :], ALU.subtract),
        waits=[pv])
    for e_ in range(NEXP):
        for blk in range(NQB):
            c0 = e_ * 32 + blk * 4
            pv = p.op("vector", lambda e, e_=e_, blk=blk, c0=c0: e.tensor_scalar(
                slotidx_sb[:, c0:c0 + 4], slotidx_sb[:, c0:c0 + 4],
                base_sb[:, e_ * 8 + blk:e_ * 8 + blk + 1], None, ALU.add),
                waits=[pv])
    slfin = pv
    esD0i.close()

    selT_sb = [sbD0(f"selT_sb{i}", [128, 4 * SLOTB], BF16) for i in range(2)]
    gpk2 = [sbD0(f"gpk2{i}", [128, 8], BF16) for i in range(2)]
    ghi_f = sbD0("ghi_f", [128, 4], F32)
    glo_f = sbD0("glo_f", [128, 4], F32)
    xc_ev = [sbD0(f"xc_ev{i}", [128, SLOTB], BF16) for i in range(8)]

    xn_free = iota_ld
    psxc_free = [NONE_MS] * 4
    gc_free = [bev0, NONE_MS, NONE_MS]
    ghi_free = [NONE_MS, NONE_MS]
    selT_free = [NONE_MS, NONE_MS]
    xcev_free = [NONE_MS] * 8

    def emit_selT(e_, blk):
        eb = e_ % 2
        last = NONE_MS
        for tc in range(4):
            for st in range(2):
                c0 = e_ * 32 + blk * 4
                last = p.op("vector", lambda e, tc=tc, st=st, c0=c0, eb=eb: e.tensor_scalar(
                    selT_sb[eb][:, tc * SLOTB + st * 128: tc * SLOTB + (st + 1) * 128],
                    iota2b[:, st * 128:(st + 1) * 128],
                    slotidx_sb[:, c0 + tc:c0 + tc + 1], ind_all[:, c0 + tc:c0 + tc + 1],
                    ALU.is_equal, ALU.mult),
                    waits=([slfin, selT_free[eb]] if tc == 0 and st == 0 else []))
        return last

    for blk in range(NQB):
        xn_ld = p.dma("sync", xn_tm_sb[:, :].rearrange("q (tc f) -> q tc f", tc=4),
                      xn_tm_dram[blk * 512:(blk + 1) * 512, :].rearrange(
                          "(tc q) f -> q tc f", q=128),
                      waits=[xn_free])
        bld_last = emit_selT(0, blk)
        gc_pend = None

        def emit_gc(pend):
            (pe_, peb, pg4) = pend
            lg = NONE_MS
            for st in range(2):
                gp = (pe_ * 2 + st) % 3
                gmm = NONE_MS
                for tc in range(4):
                    gmm = p.op("tensor", lambda e, st=st, tc=tc, peb=peb, gp=gp: e.matmul(
                        ps_gc[gp][:, 0:2],
                        selT_sb[peb][:, tc * SLOTB + st * 128: tc * SLOTB + (st + 1) * 128],
                        gpk2[peb][:, tc * 2:(tc + 1) * 2], start=(tc == 0), stop=(tc == 3)),
                        waits=([pg4, gc_free[gp]] if tc == 0 else []))
                lg = gmm
                gcol = gc_sb[:, pe_ * 16 + blk * 2 + st: pe_ * 16 + blk * 2 + st + 1]
                gv1 = p.op("vector", lambda e, gcol=gcol, gp=gp: e.tensor_copy(
                    gcol, ps_gc[gp][:, 0:1]), waits=[gmm])
                gev = p.op("vector", lambda e, gcol=gcol, gp=gp: e.tensor_tensor(
                    gcol, gcol, ps_gc[gp][:, 1:2], ALU.add), waits=[gv1])
                gc_free[gp] = gev
            return lg

        for e_ in range(NEXP):
            eb = e_ % 2
            c0 = e_ * 32 + blk * 4
            if gc_pend is not None:
                lgm = emit_gc(gc_pend)
                selT_free[gc_pend[1]] = lgm
                ghi_free[gc_pend[1]] = lgm
                gc_pend = None
            mm_xc = NONE_MS
            for featc in range(DC):
                fb = featc % 4
                xb = (e_ % 2) * 4 + fb
                for tc in range(4):
                    w_m = []
                    if tc == 0:
                        w_m = [bld_last, psxc_free[fb]]
                        if featc == 0:
                            w_m.append(xn_ld)
                    mm_xc = p.op("tensor", lambda e, featc=featc, tc=tc, fb=fb, eb=eb: e.matmul(
                        ps_xc[fb][:, 0:SLOTB],
                        xn_tm_sb[:, tc * D + featc * 128: tc * D + (featc + 1) * 128],
                        selT_sb[eb][:, tc * SLOTB:(tc + 1) * SLOTB],
                        start=(tc == 0), stop=(tc == 3)), waits=w_m)
                xce = p.op("vector", lambda e, fb=fb, xb=xb: e.tensor_copy(
                    xc_ev[xb][:, :], ps_xc[fb][:, 0:SLOTB]),
                    waits=[mm_xc, xcev_free[xb]])
                psxc_free[fb] = xce
                xcev_free[xb] = p.dma(
                    "scalar",
                    xc_dram[e_][:, featc * SLOT_E + blk * SLOTB:
                                featc * SLOT_E + (blk + 1) * SLOTB],
                    xc_ev[xb][:, :], waits=[xce])
            if selT_free[eb] is NONE_MS or e_ == 0:
                selT_free[eb] = mm_xc
            if e_ + 1 < NEXP:
                bld_last = emit_selT(e_ + 1, blk)
            gview = gpk2[eb][:, :].rearrange("q (tc two) -> q tc two", two=2)
            g1_ = p.op("vector", lambda e, c0=c0, gview=gview: e.tensor_copy(
                gview[:, :, 0], wsel[:, c0:c0 + 4]), waits=[ghi_free[eb]])
            g2_ = p.op("vector", lambda e, gview=gview: e.tensor_copy(
                ghi_f[:, :], gview[:, :, 0]), waits=[g1_])
            g3_ = p.op("vector", lambda e, c0=c0: e.tensor_tensor(
                glo_f[:, :], wsel[:, c0:c0 + 4], ghi_f[:, :], ALU.subtract),
                waits=[g2_])
            g4_ = p.op("vector", lambda e, gview=gview: e.tensor_copy(
                gview[:, :, 1], glo_f[:, :]), waits=[g3_])
            gc_pend = (e_, eb, g4_)
        lgm = emit_gc(gc_pend)
        selT_free[gc_pend[1]] = lgm
        ghi_free[gc_pend[1]] = lgm
        xn_free = (PE, PE.v)
    gc_done = gc_free[(7 * 2 + 1) % 3]
    xc_done = xcev_free[((NEXP - 1) % 2) * 4 + 3]
    d0_gate = (PE, PE.v)
    esD0p.close()

    esD1 = contextlib.ExitStack()
    sbD1 = lambda name, shape, dt: esD1.enter_context(nc.sbuf_tensor(name, shape, dt))
    w1_sb = sbD1("w1_sb", [128, FC * 1024], BF16)
    w2_sb = sbD1("w2_sb", [128, FC * D], BF16)
    xnb = [sbD1(f"xnb{i}", [128, DC * 512], BF16) for i in range(2)]
    h_sb = sbD1("h_sb", [128, FC * 512], BF16)
    yst = [sbD1(f"yst{i}", [128, D], F32) for i in range(2)]
    ps_h = [esD1.enter_context(nc.psum_tensor(f"ps_h{i}", [128, 512], F32)) for i in range(4)]
    ps_y = [esD1.enter_context(nc.psum_tensor(f"ps_y{i}", [128, 512], F32)) for i in range(2)]

    w1_free = NONE_MS
    w2_free = NONE_MS
    xnb_free = [NONE_MS, NONE_MS]
    psy_free = [NONE_MS, NONE_MS]
    bank_guard = [NONE_MS] * 4
    mmh_gate = d0_gate
    y2_gate = NONE_MS
    w1_next_free = NONE_MS

    for e_ in range(NEXP):
        w1_lds = []

        def emit_w1_batch(b, gate):
            for fc in range(8 * b, 8 * b + 8):
                w1_lds.append(p.dma(
                    "gpsimd",
                    w1_sb[:, fc * 1024:(fc + 1) * 1024].rearrange(
                        "r (dc c) -> r dc c", dc=DC),
                    w1_pk[e_, fc].rearrange("dc r c -> r dc c"),
                    waits=(gate if fc == 8 * b else [])))
        emit_w1_batch(0, [w1_free, y2_gate])
        w2_lds = []
        for sb_ in range(NSB):
            xld = p.dma("sync",
                        xnb[sb_ % 2][:, :].rearrange("r (dc c) -> r dc c", dc=DC),
                        xc_dram[e_].rearrange("r (dc s) -> r dc s", dc=DC)[
                            :, :, sb_ * 512:(sb_ + 1) * 512],
                        waits=([xnb_free[sb_ % 2], mmh_gate]
                               + ([xc_done, gc_done] if e_ == 0 and sb_ == 0 else [])))
            relu_ms = {}
            for fc in range(FC):
                mmh = NONE_MS
                for dc in range(DC):
                    w_m = []
                    if dc == 0:
                        w_m = [xld, bank_guard[fc % 4]]
                        if sb_ == 0 and fc % 8 == 0:
                            w_m.append(w1_lds[fc + 7])
                    mmh = p.op("tensor", lambda e, fc=fc, dc=dc, sb_=sb_: e.matmul(
                        ps_h[fc % 4][:, :],
                        w1_sb[:, fc * 1024 + dc * 128: fc * 1024 + (dc + 1) * 128],
                        xnb[sb_ % 2][:, dc * 512:(dc + 1) * 512],
                        start=(dc == 0), stop=(dc == DC - 1)), waits=w_m)
                    if sb_ == 0 and dc == 0 and fc % 8 == 0 and fc // 8 + 1 < 4:
                        emit_w1_batch(fc // 8 + 1, [mmh])
                    if fc == 0 and dc == 0:
                        mmh_gate = mmh
                        if sb_ == 0:
                            for fc2 in range(8):
                                w2_lds.append(p.dma(
                                    "gpsimd", w2_sb[:, fc2 * D:(fc2 + 1) * D],
                                    w2_in[e_, fc2 * 128:(fc2 + 1) * 128, :],
                                    waits=([w2_free, mmh] if fc2 == 0 else [])))
                relu_ms[fc] = p.op("scalar", lambda e, fc=fc: e.activation(
                    h_sb[:, fc * 512:(fc + 1) * 512], ps_h[fc % 4][:, :], AF.Relu),
                    waits=[mmh])
                bank_guard[fc % 4] = relu_ms[fc]
            xnb_free[sb_ % 2] = (PE, PE.v)
            if sb_ == NSB - 1:
                w1_next_free = (PE, PE.v)
            for tt in range(4):
                ti = sb_ * 4 + tt
                mmy = [NONE_MS, NONE_MS]
                for half in range(2):
                    for fc in range(FC):
                        w_m = [relu_ms[fc]]
                        if fc == 0:
                            w_m.append(psy_free[half])
                        if sb_ == 0 and tt == 0 and half == 0 and fc % 8 == 0:
                            w_m.append(w2_lds[fc + 7])
                        mmy[half] = p.op("tensor", lambda e, fc=fc, half=half, tt=tt: e.matmul(
                            ps_y[half][:, :],
                            h_sb[:, fc * 512 + tt * 128: fc * 512 + (tt + 1) * 128],
                            w2_sb[:, fc * D + half * 512: fc * D + (half + 1) * 512],
                            start=(fc == 0), stop=(fc == FC - 1)), waits=w_m)
                        if (sb_ == 0 and tt == 0 and half == 0 and fc % 8 == 0
                                and fc // 8 + 1 < 4):
                            for fc2 in range(8 * (fc // 8 + 1), 8 * (fc // 8 + 1) + 8):
                                w2_lds.append(p.dma(
                                    "gpsimd", w2_sb[:, fc2 * D:(fc2 + 1) * D],
                                    w2_in[e_, fc2 * 128:(fc2 + 1) * 128, :],
                                    waits=([mmy[half]] if fc2 % 8 == 0 else [])))
                y1 = p.op("vector", lambda e, ti=ti, e_=e_: e.tensor_scalar(
                    yst[ti % 2][:, 0:512], ps_y[0][:, :],
                    gc_sb[:, e_ * 16 + ti: e_ * 16 + ti + 1], None, ALU.mult),
                    waits=[mmy[0], p.drain("gpsimd"), gc_done])
                y2 = p.op("vector", lambda e, ti=ti, e_=e_: e.tensor_scalar(
                    yst[ti % 2][:, 512:1024], ps_y[1][:, :],
                    gc_sb[:, e_ * 16 + ti: e_ * 16 + ti + 1], None, ALU.mult),
                    waits=[mmy[1]])
                psy_free[0] = y1
                psy_free[1] = y2
                if ti == NST_E - 3:
                    y2_gate = y2
                p.dma("gpsimd", y_dram[e_, ti], yst[ti % 2][:, :],
                      waits=[y1, y2])
        w1_free = w1_next_free
        w2_free = (PE, PE.v)

    p.barrier()
    esD1.close()
    esD0.close()

    esD2 = contextlib.ExitStack()
    sbD2 = lambda name, shape, dt: esD2.enter_context(nc.sbuf_tensor(name, shape, dt))
    y_sb = [sbD2(f"y_sb{i}", [128, 16 * D], BF16) for i in range(2)]
    selT_d = [sbD2(f"selT_d{i}", [128, 128], BF16) for i in range(2)]
    sel_ev = [sbD2(f"sel_ev{i}", [128, 128], BF16) for i in range(2)]
    oadd = [sbD2(f"oadd{i}", [128, D], F32) for i in range(2)]
    ps_sc = [esD2.enter_context(nc.psum_tensor(f"ps_sc{i}", [128, 512], F32))
             for i in range(4)]
    ps_tT = [esD2.enter_context(nc.psum_tensor(f"ps_tT{i}", [128, 1024], BF16))
             for i in range(2)]

    ysb_free = [NONE_MS, NONE_MS]
    y_gate = NONE_MS
    bld_free = [NONE_MS, NONE_MS]
    sev_free = [NONE_MS, NONE_MS]
    psc_free = [NONE_MS] * 4
    oadd_free = [NONE_MS, NONE_MS]
    for blk in range(NQB):
        bb = blk % 2
        y_lds = NONE_MS
        for e_ in range(NEXP):
            y_lds = p.dma(
                "sync",
                y_sb[bb][:, e_ * 2 * D:(e_ + 1) * 2 * D].rearrange(
                    "q (st f) -> q st f", st=2),
                y_dram[e_, blk * 2:(blk + 1) * 2].rearrange("st q f -> q st f"),
                waits=([ysb_free[bb], y_gate] if e_ == 0 else []))
        for tc in range(4):
            mm_sc = NONE_MS
            pend = None
            for idx in range(17):
                if idx < 16:
                    e_, st = idx // 2, idx % 2
                    ib = idx % 2
                    c0 = e_ * 32 + blk * 4 + tc
                    bld = p.op("vector", lambda e, st=st, c0=c0, ib=ib: e.tensor_scalar(
                        selT_d[ib][:, :], iota2b[:, st * 128:(st + 1) * 128],
                        slotidx_sb[:, c0:c0 + 1], ind_all[:, c0:c0 + 1],
                        ALU.is_equal, ALU.mult),
                        waits=[bld_free[ib]])
                    tp = p.op("tensor", lambda e, ib=ib: e.transpose(
                        ps_tT[ib][:, 0:128], selT_d[ib][:, :], ident_s[:, :]),
                        waits=[bld, sev_free[ib]])
                    bld_free[ib] = tp
                    sev = p.op("vector", lambda e, ib=ib: e.tensor_copy(
                        sel_ev[ib][:, :], ps_tT[ib][:, 0:128]), waits=[tp])
                if pend is not None:
                    (pidx, pib, pe_, pst, psev) = pend
                    tb = tc % 2
                    for half in range(2):
                        w_m = [psev]
                        if pidx == 0:
                            w_m += [psc_free[2 * tb + half], y_lds, p.drain("sync")]
                        mm_sc = p.op("tensor", lambda e, pidx=pidx, half=half, pib=pib,
                                     bb=bb, pe_=pe_, pst=pst, tb=tb: e.matmul(
                            ps_sc[2 * tb + half][:, :], sel_ev[pib][:, :],
                            y_sb[bb][:, (pe_ * 2 + pst) * D + half * 512:
                                      (pe_ * 2 + pst) * D + (half + 1) * 512],
                            start=(pidx == 0), stop=(pidx == 15)), waits=w_m)
                    sev_free[pib] = mm_sc
                    if tc == 0 and pidx == 0:
                        y_gate = mm_sc
                if idx < 16:
                    pend = (idx, ib, e_, st, sev)
            ob = tc % 2
            tb = tc % 2
            o1 = p.op("vector", lambda e, ob=ob, tb=tb: e.tensor_copy(
                oadd[ob][:, 0:512], ps_sc[2 * tb][:, :]),
                waits=[mm_sc, oadd_free[ob]])
            o2 = p.op("vector", lambda e, ob=ob, tb=tb: e.tensor_copy(
                oadd[ob][:, 512:1024], ps_sc[2 * tb + 1][:, :]))
            psc_free[2 * tb] = o1
            psc_free[2 * tb + 1] = o2
            ti = blk * 4 + tc
            acc = p.dma("gpsimd", out_t[ti * 128:(ti + 1) * 128, :],
                        oadd[ob][:, :], accum_op=ALU.add,
                        waits=[o1, o2, p.drain("gpsimd")])
            oadd_free[ob] = acc
        ysb_free[bb] = (PE, PE.v)

    p.wt("gpsimd", (p.d["gpsimd"], p.d["gpsimd"].v))
    p.wt("sync", (DS, DS.v))
    p.wt("scalar", (DSC, DSC.v))
    esD2.close()
    esDR.close()

    es0.close()

    with nc.Block() as block:
        p.emit(block)
    return nc


_NC_CACHE = None


def kernel(L0_attn_x, L0_attn_w_qkv, L0_attn_w_out, L0_attn_b_out,
           L0_ffn_Wr, L0_ffn_W1, L0_ffn_W2):
    global _NC_CACHE
    X = np.ascontiguousarray(np.asarray(L0_attn_x, dtype=np.float32))
    Wqkv = np.ascontiguousarray(np.asarray(L0_attn_w_qkv, dtype=np.float32))
    Wout = np.ascontiguousarray(np.asarray(L0_attn_w_out, dtype=np.float32))
    Bout = np.ascontiguousarray(np.asarray(L0_attn_b_out, dtype=np.float32)).reshape(1, D)
    Wr = np.ascontiguousarray(np.asarray(L0_ffn_Wr, dtype=np.float32))
    W1 = np.asarray(L0_ffn_W1, dtype=np.float32)
    W2 = np.ascontiguousarray(np.asarray(L0_ffn_W2, dtype=np.float32))
    wqkv_pk = np.ascontiguousarray(Wqkv.reshape(DC, 128, 24, 128).transpose(2, 0, 1, 3))
    w1_pk = np.ascontiguousarray(
        W1.reshape(NEXP, DC, 128, FC, 128).transpose(0, 3, 1, 2, 4))
    if _NC_CACHE is None:
        _NC_CACHE = build_nc()
    nc = _NC_CACHE
    in_maps = []
    for c in range(NCORES):
        xc = np.ascontiguousarray(X[c])
        in_maps.append({
            "x_own": xc,
            "x_fm": np.ascontiguousarray(xc.T),
            "wqkv_pk": wqkv_pk,
            "w_out": Wout,
            "b_out": Bout,
            "wr_in": Wr,
            "w1_pk": w1_pk,
            "w2_in": W2,
        })
    res = run_bass_kernel_spmd(nc, in_maps, core_ids=list(range(NCORES)))
    out = np.stack([np.asarray(res.results[c]["out"]) for c in range(NCORES)], axis=0)
    return out


# version 10 (final)
# speedup vs baseline: 1.2121x; 1.0006x over previous
import contextlib
import numpy as np
import ml_dtypes

import concourse.bass as bass
import concourse.mybir as mybir
from concourse.bass_utils import run_bass_kernel_spmd

F32 = mybir.dt.float32
BF16 = mybir.dt.bfloat16
I32 = mybir.dt.int32
AF = mybir.ActivationFunctionType
ALU = mybir.AluOpType
AX = mybir.AxisListType

NCORES = 8
TOK = 4096
D = 1024
DC = 8
NPAIR = 8
NQB = 8
QBS = 512
NKC = 32
NTT = 32
FFN = 4096
FC = 32
NEXP = 8
CAP = 10240
NST = CAP // 128
TT_ALL = 32768
EPS = 1e-5
INVF = float(1 << 24)
BIS_ROUNDS = 34
SLOTB = 256
SLOT_E = NQB * SLOTB
NST_E = SLOT_E // 128
NSB = SLOT_E // 512


class Sem:
    def __init__(self, nc, name):
        self.h = nc.alloc_semaphore(name)
        self.v = 0

    def inc(self, n=1):
        self.v += n
        return self.v


NONE_MS = None


class Prog:
    ENGINES = ("sync", "tensor", "vector", "scalar", "gpsimd")

    def __init__(self, nc):
        self.nc = nc
        self.items = []
        self.e = {n: Sem(nc, "se_" + n) for n in self.ENGINES}
        self.d = {n: Sem(nc, "sd_" + n) for n in ("sync", "scalar", "gpsimd")}

    @staticmethod
    def _clean(waits):
        return tuple(w for w in waits if w is not None and w[1] > 0)

    def op(self, eng, fn, waits=()):
        s = self.e[eng]
        v = s.inc(1)
        self.items.append((eng, self._clean(waits), fn, ((s, 1),)))
        return (s, v)

    def dma(self, eng, out, in_, waits=(), **kw):
        s = self.d[eng]
        v = s.inc(16)
        self.items.append((eng, self._clean(waits),
                           lambda e, out=out, in_=in_, kw=kw: e.dma_start(out=out, in_=in_, **kw),
                           ((s, 16),)))
        return (s, v)

    def idma(self, out, out_off, in_, in_off, bounds, waits=(), **kw):
        s = self.d["gpsimd"]
        v = s.inc(16)

        def fn(e, out=out, oo=out_off, in_=in_, io=in_off, b=bounds, kw=kw):
            try:
                return e.indirect_dma_start(out, oo, in_, io, bounds_check=b,
                                            oob_is_err=False, **kw)
            except Exception as ex:
                raise RuntimeError(
                    f"idma failed: out={out.tensor.name if hasattr(out, 'tensor') else out} "
                    f"in={in_.tensor.name if hasattr(in_, 'tensor') else in_} "
                    f"kw={kw}: {ex}") from ex
        self.items.append(("gpsimd", self._clean(waits), fn, ((s, 16),)))
        return (s, v)

    def wt(self, eng, *ms):
        self.items.append((eng, self._clean(ms), lambda e: e.nop(), ()))

    def drain(self, eng):
        s = self.d[eng]
        return (s, s.v)

    def barrier(self):
        for n in ("sync", "scalar", "gpsimd"):
            self.wt(n, (self.d[n], self.d[n].v))
        vals = []
        for n in self.ENGINES:
            s = self.e[n]
            v = s.inc(1)
            self.items.append((n, (), lambda e: e.nop(), ((s, 1),)))
            vals.append((s, v))
        for n in self.ENGINES:
            self.items.append((n, tuple(vals), lambda e: e.nop(), ()))

    def emit(self, block):
        for ename in self.ENGINES:
            def make(ename):
                def run(eng):
                    for (e, waits, fn, incs) in self.items:
                        if e != ename:
                            continue
                        for (sem, val) in waits:
                            eng.wait_ge(sem.h, val)
                        ins = fn(eng)
                        for (sem, n) in incs:
                            ins = ins.then_inc(sem.h, n)
                return run
            getattr(block, ename)(make(ename))


def build_nc():
    nc = bass.Bass("TRN2", target_bir_lowering=False, debug=False)

    x_own = nc.dram_tensor("x_own", [TOK, D], F32, kind="ExternalInput").ap()
    x_fm = nc.dram_tensor("x_fm", [D, TOK], F32, kind="ExternalInput").ap()
    wqkv_pk = nc.dram_tensor("wqkv_pk", [24, DC, 128, 128], F32, kind="ExternalInput").ap()
    w_out = nc.dram_tensor("w_out", [D, D], F32, kind="ExternalInput").ap()
    b_out = nc.dram_tensor("b_out", [1, D], F32, kind="ExternalInput").ap()
    wr_in = nc.dram_tensor("wr_in", [D, NEXP], F32, kind="ExternalInput").ap()
    w1_pk = nc.dram_tensor("w1_pk", [NEXP, FC, DC, 128, 128], F32, kind="ExternalInput").ap()
    w2_in = nc.dram_tensor("w2_in", [NEXP, FFN, D], F32, kind="ExternalInput").ap()
    out_t = nc.dram_tensor("out", [TOK, D], F32, kind="ExternalOutput").ap()

    q_dram = nc.dram_tensor("q_dram", [NPAIR, 128, TOK], BF16, kind="Internal").ap()
    k_dram = nc.dram_tensor("k_dram", [NPAIR, 128, TOK], BF16, kind="Internal").ap()
    v_dram = nc.dram_tensor("v_dram", [NPAIR, NKC, 128, 130], BF16, kind="Internal").ap()
    nb_dram = nc.dram_tensor("nb_dram", [16, 1], F32, kind="Internal").ap()
    cw_dram = nc.dram_tensor("cw_dram", [1, NEXP], F32, kind="Internal").ap()
    cin_g = nc.dram_tensor("cin_g", [128, 256], F32, kind="Internal").ap()
    cout_g = nc.dram_tensor("cout_g", [NCORES * 128, 256], F32, kind="Internal").ap()
    ctot_dram = nc.dram_tensor("ctot_dram", [1, NEXP], F32, kind="Internal").ap()
    xn_dram = nc.dram_tensor("xn_dram", [128, DC * TOK], BF16, kind="Internal").ap()
    xn_tm_dram = nc.dram_tensor("xn_tm_dram", [TOK, D], BF16, kind="Internal").ap()
    xc_dram = nc.dram_tensor("xc_dram", [NEXP, 128, DC * SLOT_E], BF16,
                             kind="Internal").ap()
    y_dram = nc.dram_tensor("y_dram", [NEXP, NST_E, 128, D], BF16,
                            kind="Internal").ap()

    ident_c = nc.inline_tensor(np.eye(128, dtype=ml_dtypes.bfloat16), name="ident_c").ap()
    pairsel_np = np.zeros((128, 2), dtype=ml_dtypes.bfloat16)
    pairsel_np[0:64, 0] = 1
    pairsel_np[64:128, 1] = 1
    pairsel_c = nc.inline_tensor(pairsel_np, name="pairsel_c").ap()
    masks_np = np.zeros((128, 4 * QBS), dtype=ml_dtypes.bfloat16)
    kparr = np.arange(128)[:, None]
    qfarr = np.arange(QBS)[None, :]
    for m in range(4):
        masks_np[:, m * QBS:(m + 1) * QBS] = (qfarr >= kparr + 128 * m).astype(ml_dtypes.bfloat16)
    masks_c = nc.inline_tensor(masks_np, name="masks_c").ap()
    esel_np = np.zeros((16, 8 * 128), dtype=ml_dtypes.bfloat16)
    for c in range(8):
        esel_np[2 * c, c * 128:c * 128 + 64] = 1
        esel_np[2 * c + 1, c * 128 + 64:c * 128 + 128] = 1
    esel_c = nc.inline_tensor(esel_np, name="esel_c").ap()
    ones_row_c = nc.inline_tensor(np.ones((1, 128), np.float32), name="ones_row_c").ap()
    tri_np = (np.arange(128)[:, None] < np.arange(128)[None, :]).astype(np.float32)
    tri_c = nc.inline_tensor(tri_np, name="tri_c").ap()
    iota2_c = nc.inline_tensor(np.arange(SLOTB, dtype=np.float32)[None, :],
                               name="iota2_c").ap()

    p = Prog(nc)
    PE, VE, AC, GP = p.e["tensor"], p.e["vector"], p.e["scalar"], p.e["gpsimd"]
    DS, DSC, DG = p.d["sync"], p.d["scalar"], p.d["gpsimd"]

    es0 = contextlib.ExitStack()
    sb0 = lambda name, shape, dt: es0.enter_context(nc.sbuf_tensor(name, shape, dt))

    pairsel_s = sb0("pairsel_s", [128, 2], BF16)
    ident_s = sb0("ident_s", [128, 128], BF16)
    onesr_s = sb0("onesr_s", [1, 128], F32)
    negb_bc = sb0("negb_bc", [128, 16], F32)
    negb_pair = sb0("negb_pair", [128, 8], F32)
    stats_m = sb0("stats_m", [128, NTT], F32)
    stats_r = sb0("stats_r", [128, NTT], F32)
    eps_col = sb0("eps_col", [128, 1], F32)
    G_own = sb0("G_own", [128, 256], F32)
    wsel = sb0("wsel", [128, 256], F32)

    p.dma("sync", pairsel_s[:, :], pairsel_c[:, :])
    p.dma("sync", ident_s[:, :], ident_c[:, :])
    c_ld = p.dma("sync", onesr_s[:, :], ones_row_c[:, :])
    eps_ms = p.op("vector", lambda e: e.memset(eps_col[:, :], EPS))

    esA = contextlib.ExitStack()
    sbA = lambda name, shape, dt: esA.enter_context(nc.sbuf_tensor(name, shape, dt))
    xn_fm = sbA("xn_fm", [128, DC * TOK], BF16)
    xt = [sbA(f"xt{i}", [128, D], F32) for i in range(3)]
    scrA = sbA("scrA", [128, D], F32)
    sq_s = sbA("sq_s", [128, D], F32)
    xn32 = sbA("xn32", [128, D], F32)
    xnbf = [sbA(f"xnbf{i}", [128, D], BF16) for i in range(2)]
    st_sum = sbA("st_sum", [128, NTT], F32)
    st_sq = sbA("st_sq", [128, NTT], F32)
    st_var = sbA("st_var", [128, NTT], F32)
    st_sd = sbA("st_sd", [128, NTT], F32)
    m2_s = sbA("m2_s", [128, NTT], F32)
    ps_tr = [esA.enter_context(nc.psum_tensor(f"ps_tr{i}", [128, 128], BF16)) for i in range(2)]
    ps_qkv = [esA.enter_context(nc.psum_tensor(f"ps_qkv{i}", [128, QBS], F32)) for i in range(4)]
    ps_nrm = esA.enter_context(nc.psum_tensor("ps_nrm", [128, QBS], F32))


    ld_x = [NONE_MS] * NTT
    xn_done = [NONE_MS] * NTT
    xnbf_free = [NONE_MS, NONE_MS]
    trev = [NONE_MS, NONE_MS]
    ld_gate = [c_ld]
    xn_tm_st = [NONE_MS, NONE_MS]
    for i in range(NTT):
        ld_x[i] = p.dma("sync", xt[i % 3][:, :], x_own[i * 128:(i + 1) * 128, :],
                        waits=ld_gate + ([] if i < 3 else [xn_done[i - 3]]))
        v_sum = p.op("vector", lambda e, i=i: e.tensor_scalar(
            scrA[:, :], xt[i % 3][:, :], 1.0, None, ALU.mult, ALU.add,
            accum_out=st_sum[:, i:i + 1]), waits=[ld_x[i]])
        a_sq = p.op("scalar", lambda e, i=i: e.activation(
            sq_s[:, :], xt[i % 3][:, :], AF.Square, accum_out=st_sq[:, i:i + 1]),
            waits=[ld_x[i]])
        ld_gate = [v_sum, a_sq]
        v_m = p.op("vector", lambda e, i=i: e.tensor_scalar(
            stats_m[:, i:i + 1], st_sum[:, i:i + 1], 1.0 / D, None, ALU.mult),
            waits=[v_sum])
        v_e2 = p.op("vector", lambda e, i=i: e.tensor_scalar(
            st_var[:, i:i + 1], st_sq[:, i:i + 1], 1.0 / D, None, ALU.mult),
            waits=[a_sq])
        v_m2 = p.op("vector", lambda e, i=i: e.tensor_tensor(
            m2_s[:, i:i + 1], stats_m[:, i:i + 1], stats_m[:, i:i + 1], ALU.mult),
            waits=[v_m])
        v_var = p.op("vector", lambda e, i=i: e.tensor_tensor(
            st_var[:, i:i + 1], st_var[:, i:i + 1], m2_s[:, i:i + 1], ALU.subtract),
            waits=[v_m2, v_e2])
        a_sd = p.op("scalar", lambda e, i=i: e.activation(
            st_sd[:, i:i + 1], st_var[:, i:i + 1], AF.Sqrt, bias=eps_col[:, 0:1]),
            waits=[v_var, eps_ms])
        v_r = p.op("vector", lambda e, i=i: e.reciprocal(
            stats_r[:, i:i + 1], st_sd[:, i:i + 1]), waits=[a_sd])
        v_xn = p.op("vector", lambda e, i=i: e.tensor_scalar(
            xn32[:, :], xt[i % 3][:, :], stats_m[:, i:i + 1], stats_r[:, i:i + 1],
            ALU.subtract, ALU.mult), waits=[v_r])
        v_bf = p.op("vector", lambda e, i=i: e.tensor_copy(
            xnbf[i % 2][:, :], xn32[:, :]),
            waits=[v_xn, xnbf_free[i % 2], xn_tm_st[i % 2]])
        xn_tm_st[i % 2] = p.dma(
            "scalar", xn_tm_dram[i * 128:(i + 1) * 128, :], xnbf[i % 2][:, :],
            waits=[v_bf])
        last = NONE_MS
        for dc in range(DC):
            tp = p.op("tensor", lambda e, i=i, dc=dc: e.transpose(
                ps_tr[dc % 2][:, :], xnbf[i % 2][:, dc * 128:(dc + 1) * 128], ident_s[:, :]),
                waits=[v_bf, trev[dc % 2]])
            trev[dc % 2] = p.op("vector", lambda e, i=i, dc=dc: e.tensor_copy(
                xn_fm[:, dc * TOK + i * 128: dc * TOK + (i + 1) * 128], ps_tr[dc % 2][:, :]),
                waits=[tp])
            last = trev[dc % 2]
            if dc == DC - 1:
                xnbf_free[i % 2] = tp
        xn_done[i] = last
    xn_all = xn_done[NTT - 1]

    esA2 = contextlib.ExitStack()
    sbA2 = lambda name, shape, dt: esA2.enter_context(nc.sbuf_tensor(name, shape, dt))
    xfmb = [sbA2(f"xfmb{i}", [128, TOK], F32) for i in range(2)]
    wr_s = sbA2("wr_s", [128, DC * NEXP], F32)
    ones_col = sbA2("ones_col", [128, 1], F32)
    cwb = sbA2("cwb", [128, NEXP], F32)
    lgraw = sbA2("lgraw", [128, 256], F32)
    lgev = sbA2("lgev", [128, 256], F32)
    cwev = sbA2("cwev", [1, NEXP], F32)
    exps_a = sbA2("exps_a", [128, NEXP], F32)
    probs_a = sbA2("probs_a", [128, NEXP], F32)
    ge8_a = sbA2("ge8_a", [128, NEXP], F32)
    pm8_a = sbA2("pm8_a", [128, NEXP], F32)
    gm8_a = sbA2("gm8_a", [128, NEXP], F32)
    scr8 = sbA2("scr8", [128, NEXP], F32)
    lg2 = sbA2("lg2", [128, NEXP], F32)
    rt = sbA2("rt", [128, 8 * NTT], F32)
    ps_lg = [ps_qkv[0][:, 0:256], ps_qkv[1][:, 0:256]]
    ps_cw = ps_nrm

    wr_ld = NONE_MS
    for dc in range(DC):
        wr_ld = p.dma("sync", wr_s[:, dc * NEXP:(dc + 1) * NEXP],
                      wr_in[dc * 128:(dc + 1) * 128, :])
    oc_ms = p.op("vector", lambda e: e.memset(ones_col[:, :], 1.0))
    lg_zero = p.op("vector", lambda e: e.memset(lgraw[:, :], 0.0))

    xfm_ld = [NONE_MS, NONE_MS]
    xfm_free = [NONE_MS, NONE_MS]
    xfm_gate = wr_ld
    lg_acc = lg_zero
    cw_mm = NONE_MS
    psg_free = [NONE_MS, NONE_MS]
    for dc in range(DC):
        xfm_ld[dc % 2] = p.dma("sync", xfmb[dc % 2][:, :],
                               x_fm[dc * 128:(dc + 1) * 128, :],
                               waits=[xfm_free[dc % 2], xfm_gate])
        cw_mm = p.op("tensor", lambda e, dc=dc: e.matmul(
            ps_cw[0:1, 0:NEXP], ones_col[:, :], wr_s[:, dc * NEXP:(dc + 1) * NEXP],
            start=(dc == 0), stop=(dc == DC - 1)),
            waits=([wr_ld, oc_ms, xn_all] if dc == 0 else []))
        mmlast = NONE_MS
        for t in range(NTT):
            mmlast = p.op("tensor", lambda e, dc=dc, t=t: e.matmul(
                ps_lg[dc % 2][:, t * NEXP:(t + 1) * NEXP],
                xfmb[dc % 2][:, t * 128:(t + 1) * 128],
                wr_s[:, dc * NEXP:(dc + 1) * NEXP], start=True, stop=True),
                waits=([xfm_ld[dc % 2], psg_free[dc % 2]] if t == 0 else []))
            if t == 0:
                xfm_gate = mmlast
        xfm_free[dc % 2] = mmlast
        evg = p.op("vector", lambda e, dc=dc: e.tensor_copy(
            lgev[:, :], ps_lg[dc % 2][:, :]), waits=[mmlast])
        psg_free[dc % 2] = evg
        lg_acc = p.op("vector", lambda e: e.tensor_tensor(
            lgraw[:, :], lgraw[:, :], lgev[:, :], ALU.add), waits=[evg, lg_acc])
    cw_ev = p.op("vector", lambda e: e.tensor_copy(cwev[0:1, :], ps_cw[0:1, 0:NEXP]),
                 waits=[cw_mm])
    cw_st = p.dma("gpsimd", cw_dram[0:1, :], cwev[0:1, :], waits=[cw_ev])
    cw_bc = p.dma("sync", cwb[:, :], cw_dram[0:1, :].to_broadcast([128, NEXP]),
                  waits=[cw_st])

    g_done = NONE_MS
    for t in range(NTT):
        c0 = 8 * t
        w0 = [lg_acc, cw_bc] if t == 0 else [g_done]
        v1 = p.op("vector", lambda e, t=t: e.tensor_scalar(
            scr8[:, :], cwb[:, :], stats_m[:, t:t + 1], None, ALU.mult), waits=w0)
        v2 = p.op("vector", lambda e, t=t: e.tensor_tensor(
            lg2[:, :], lgraw[:, t * NEXP:(t + 1) * NEXP], scr8[:, :], ALU.subtract),
            waits=[v1])
        v3 = p.op("vector", lambda e, t=t: e.tensor_scalar(
            lg2[:, :], lg2[:, :], stats_r[:, t:t + 1], None, ALU.mult), waits=[v2])
        v4 = p.op("vector", lambda e, t=t, c0=c0: e.reduce_max(
            rt[:, c0:c0 + 1], lg2[:, :], axis=AX.X, negate=True), waits=[v3])
        a1 = p.op("scalar", lambda e, t=t, c0=c0: e.activation(
            exps_a[:, :], lg2[:, :], AF.Exp, bias=rt[:, c0:c0 + 1],
            accum_out=rt[:, c0 + 1:c0 + 2]), waits=[v4])
        v5 = p.op("vector", lambda e, c0=c0: e.reciprocal(
            rt[:, c0 + 2:c0 + 3], rt[:, c0 + 1:c0 + 2]), waits=[a1])
        v6 = p.op("vector", lambda e, c0=c0: e.tensor_scalar(
            probs_a[:, :], exps_a[:, :], rt[:, c0 + 2:c0 + 3], None, ALU.mult),
            waits=[v5])
        v7 = p.op("vector", lambda e, c0=c0: e.reduce_max(
            rt[:, c0 + 3:c0 + 4], probs_a[:, :], axis=AX.X), waits=[v6])
        v8 = p.op("vector", lambda e, c0=c0: e.tensor_scalar(
            ge8_a[:, :], probs_a[:, :], rt[:, c0 + 3:c0 + 4], None, ALU.is_ge,
            ALU.add, accum_out=rt[:, c0 + 4:c0 + 5]), waits=[v7])
        v9 = p.op("vector", lambda e: e.tensor_tensor(
            pm8_a[:, :], probs_a[:, :], ge8_a[:, :], ALU.subtract), waits=[v8])
        v10 = p.op("vector", lambda e, c0=c0: e.reduce_max(
            rt[:, c0 + 5:c0 + 6], pm8_a[:, :], axis=AX.X), waits=[v9])
        v11 = p.op("vector", lambda e, c0=c0: e.tensor_scalar(
            rt[:, c0 + 4:c0 + 5], rt[:, c0 + 4:c0 + 5], 2.0, None, ALU.is_ge),
            waits=[v10])
        v12 = p.op("vector", lambda e, c0=c0: e.tensor_tensor(
            rt[:, c0 + 7:c0 + 8], rt[:, c0 + 3:c0 + 4], rt[:, c0 + 5:c0 + 6],
            ALU.subtract), waits=[v11])
        v13 = p.op("vector", lambda e, c0=c0: e.tensor_tensor(
            rt[:, c0 + 7:c0 + 8], rt[:, c0 + 4:c0 + 5], rt[:, c0 + 7:c0 + 8],
            ALU.mult), waits=[v12])
        v14 = p.op("vector", lambda e, c0=c0: e.tensor_tensor(
            rt[:, c0 + 6:c0 + 7], rt[:, c0 + 5:c0 + 6], rt[:, c0 + 7:c0 + 8],
            ALU.add), waits=[v13])
        v15 = p.op("vector", lambda e, c0=c0: e.tensor_scalar(
            gm8_a[:, :], probs_a[:, :], rt[:, c0 + 6:c0 + 7], None, ALU.is_ge),
            waits=[v14])
        g_done = p.op("vector", lambda e, t=t: e.tensor_tensor(
            G_own[:, :].rearrange("q (e t) -> q e t", e=NEXP)[:, :, t],
            probs_a[:, :], gm8_a[:, :], ALU.mult), waits=[v15])

    g_ship = p.dma("gpsimd", cin_g[:, :], G_own[:, :], waits=[g_done])
    cc1 = p.op("gpsimd", lambda e: e.collective_compute(
        "AllGather", ALU.bypass, replica_groups=[list(range(NCORES))],
        ins=[cin_g], outs=[cout_g]), waits=[g_ship])

    esB = contextlib.ExitStack()
    sbB = lambda name, shape, dt: esB.enter_context(nc.sbuf_tensor(name, shape, dt))
    wsl = [sbB(f"wsl{i}", [128, DC * 128], BF16) for i in range(2)]
    ev_bf = [sbB(f"ev_bf{i}", [128, QBS], BF16) for i in range(4)]
    sq_bf = [sbB(f"sq_bf{i}", [128, QBS], BF16) for i in range(2)]
    nmx = sbB("nmx", [2, 128], F32)
    vas = [sbB(f"vas{i}", [128, 4 * 130], BF16) for i in range(2)]
    nb16 = sbB("nb16", [16, 4], F32)
    ps_vtr = ps_tr[0]

    xn_st = p.dma("sync", xn_dram[:, :], xn_fm[:, :], waits=[xn_all])

    wsl_ld = [NONE_MS, NONE_MS]
    wsl_gate = NONE_MS
    nmm_pend = None
    sq_guard = [NONE_MS, NONE_MS]
    ev_tp_guard = [NONE_MS] * 4
    ev_st_free = [NONE_MS] * 4
    vas_st_free = [NONE_MS, NONE_MS]
    psq_free = [[psg_free[0]], [psg_free[1]], [NONE_MS], [NONE_MS]]
    nrm_mm_prev = NONE_MS
    nrm_red_prev = cw_ev
    vtr_prev = NONE_MS

    for s in range(24):
        buf = s % 2
        wsl_ld[buf] = p.dma("gpsimd",
                            wsl[buf][:, :].rearrange("r (dc c) -> r dc c", dc=DC),
                            wqkv_pk[s].rearrange("dc r c -> r dc c"),
                            waits=([] if s == 0 else [wsl_gate]))
        for qb in range(NQB):
            par = (s * 8 + qb) % 4
            mm = NONE_MS
            for dc in range(DC):
                w_mm = []
                if dc == 0:
                    w_mm = [wsl_ld[buf]] + psq_free[par]
                    if s == 0 and qb == 0:
                        w_mm.append(xn_all)
                mm = p.op("tensor", lambda e, buf=buf, qb=qb, dc=dc, par=par: e.matmul(
                    ps_qkv[par][:, :], wsl[buf][:, dc * 128:(dc + 1) * 128],
                    xn_fm[:, dc * TOK + qb * QBS: dc * TOK + (qb + 1) * QBS],
                    start=(dc == 0), stop=(dc == DC - 1)), waits=w_mm)
                if qb == 0 and dc == 0:
                    wsl_gate = mm
            if s < 16:
                dst = q_dram if s < 8 else k_dram
                pr = s % 8
                ev = p.op("vector", lambda e, par=par: e.tensor_copy(
                    ev_bf[par][:, :], ps_qkv[par][:, :]),
                    waits=[mm, ev_st_free[par]])
                ev_st_free[par] = p.dma(
                    "scalar", dst[pr][:, qb * QBS:(qb + 1) * QBS], ev_bf[par][:, :],
                    waits=[ev])
                sqb = (s * 8 + qb) % 2
                a_sq2 = p.op("scalar", lambda e, par=par, sqb=sqb: e.activation(
                    sq_bf[sqb][:, :], ps_qkv[par][:, :], AF.Square),
                    waits=[mm, ev, sq_guard[sqb]])
                psq_free[par] = [ev, a_sq2]
                if nmm_pend is not None:
                    (pa, psqb, pcol) = nmm_pend
                    nmm = p.op("tensor", lambda e, psqb=psqb: e.matmul(
                        ps_nrm[0:2, :], pairsel_s[:, :], sq_bf[psqb][:, :],
                        start=True, stop=True), waits=[pa, nrm_red_prev])
                    sq_guard[psqb] = nmm
                    nrm_red_prev = p.op("vector", lambda e, pcol=pcol: e.reduce_max(
                        nmx[0:2, pcol:pcol + 1], ps_nrm[0:2, :], axis=AX.X),
                        waits=[nmm])
                nmm_pend = (a_sq2, sqb, s * 8 + qb)
            else:
                if nmm_pend is not None:
                    (pa, psqb, pcol) = nmm_pend
                    nmm = p.op("tensor", lambda e, psqb=psqb: e.matmul(
                        ps_nrm[0:2, :], pairsel_s[:, :], sq_bf[psqb][:, :],
                        start=True, stop=True), waits=[pa, nrm_red_prev])
                    sq_guard[psqb] = nmm
                    nrm_red_prev = p.op("vector", lambda e, pcol=pcol: e.reduce_max(
                        nmx[0:2, pcol:pcol + 1], ps_nrm[0:2, :], axis=AX.X),
                        waits=[nmm])
                    nmm_pend = None
                pr = s - 16
                ev = p.op("vector", lambda e, par=par: e.tensor_copy(
                    ev_bf[par][:, :], ps_qkv[par][:, :]),
                    waits=[mm, ev_st_free[par], ev_tp_guard[par]])
                psq_free[par] = [ev]
                vb = par % 2
                mset = p.op("vector", lambda e, vb=vb: e.memset(
                    vas[vb][:, :].rearrange("q (kc h x) -> q kc h x", kc=4, h=2)[:, :, :, 64:65],
                    1.0), waits=[vas_st_free[vb]])
                lastv = NONE_MS
                last_tp = NONE_MS
                for tt in range(4):
                    tp = p.op("tensor", lambda e, par=par, tt=tt: e.transpose(
                        ps_vtr[:, :], ev_bf[par][:, tt * 128:(tt + 1) * 128], ident_s[:, :]),
                        waits=[ev, vtr_prev])
                    vtr_prev = p.op("vector", lambda e, vb=vb, tt=tt: e.tensor_copy(
                        vas[vb][:, tt * 130:(tt + 1) * 130].rearrange(
                            "q (h x) -> q h x", h=2)[:, :, 0:64],
                        ps_vtr[:, :].rearrange("q (h x) -> q h x", h=2)),
                        waits=[tp, mset])
                    lastv = vtr_prev
                    last_tp = tp
                ev_tp_guard[par] = last_tp
                vas_st_free[vb] = p.dma(
                    "scalar", v_dram[pr, qb * 4:(qb + 1) * 4].rearrange("kc q c -> q kc c"),
                    vas[vb][:, :].rearrange("q (kc c) -> q kc c", kc=4), waits=[lastv])

    prev_ms = nrm_red_prev
    prev_dma = NONE_MS
    for pr in range(8):
        m1 = p.op("vector", lambda e, pr=pr: e.reduce_max(
            nb16[0:2, 0:1], nmx[0:2, pr * 8:(pr + 1) * 8], axis=AX.X),
            waits=[prev_ms, prev_dma])
        m2 = p.op("vector", lambda e, pr=pr: e.reduce_max(
            nb16[0:2, 1:2], nmx[0:2, 64 + pr * 8: 64 + (pr + 1) * 8], axis=AX.X),
            waits=[m1])
        m3 = p.op("vector", lambda e: e.tensor_tensor(
            nb16[0:2, 2:3], nb16[0:2, 0:1], nb16[0:2, 1:2], ALU.mult), waits=[m2])
        aB = p.op("scalar", lambda e: e.activation(
            nb16[0:2, 3:4], nb16[0:2, 2:3], AF.Sqrt), waits=[m3])
        m4 = p.op("vector", lambda e: e.tensor_scalar(
            nb16[0:2, 3:4], nb16[0:2, 3:4], -0.125, None, ALU.mult), waits=[aB])
        prev_dma = p.dma("gpsimd", nb_dram[2 * pr:2 * pr + 2, :], nb16[0:2, 3:4],
                         waits=[m4])
        prev_ms = m4
    nb_ld = p.dma("sync", negb_bc[:, :],
                  nb_dram.rearrange("h one -> one h").to_broadcast([128, 16]),
                  waits=[prev_dma])
    p.op("vector", lambda e: e.tensor_tensor(
        negb_pair[:, :],
        negb_bc[:, :].rearrange("q (p h) -> q p h", h=2)[:, :, 0],
        negb_bc[:, :].rearrange("q (p h) -> q p h", h=2)[:, :, 1], ALU.min),
        waits=[nb_ld])

    p.barrier()
    esB.close()
    esA2.close()
    esA.close()

    esM = contextlib.ExitStack()
    sbM = lambda name, shape, dt: esM.enter_context(nc.sbuf_tensor(name, shape, dt))
    masks_s = sbM("masks_s", [128, 4 * QBS], BF16)
    b_bc = sbM("b_bc", [128, D], F32)
    esel_s = sbM("esel_s", [16, 8 * 128], BF16)
    Gs = sbM("Gs", [128, 2048], F32)
    th_lo = sbM("th_lo", [128, NEXP], F32)
    th_hi = sbM("th_hi", [128, NEXP], F32)
    th_md = sbM("th_md", [128, NEXP], F32)
    cnt_p = sbM("cnt_p", [128, NEXP], F32)
    f_scr = sbM("f_scr", [128, 256], F32)
    cnt_tot = sbM("cnt_tot", [1, NEXP], F32)
    cnt_b = sbM("cnt_b", [128, NEXP], F32)
    ge_b = sbM("ge_b", [128, NEXP], F32)
    u_s = sbM("u_s", [128, NEXP], F32)
    F_e = sbM("F_e", [128, 256], F32)
    Fp_e = sbM("Fp_e", [128, 256], F32)

    esC = contextlib.ExitStack()
    sbC = lambda name, shape, dt: esC.enter_context(nc.sbuf_tensor(name, shape, dt))
    qs = [sbC(f"qs{i}", [128, QBS], BF16) for i in range(2)]
    ks = [sbC(f"ks{i}", [128, TOK], BF16) for i in range(2)]
    vs = [sbC(f"vs{i}", [128, NKC * 130], BF16) for i in range(2)]
    pt = [sbC(f"pt{i}", [128, 1024], BF16) for i in range(4)]
    attn_raw = sbC("attn_raw", [128, 8 * QBS], BF16)
    attn_nrm = sbC("attn_nrm", [128, 8 * QBS], BF16)
    dmat = sbC("dmat", [16, QBS], F32)
    dtmp = [sbC(f"dtmp{i}", [1, QBS], F32) for i in range(2)]
    rbf = sbC("rbf", [16, QBS], BF16)
    rexp_bf = sbC("rexp_bf", [128, QBS], BF16)
    xres = [sbC(f"xres{i}", [128, D], F32) for i in range(2)]
    out32 = [sbC(f"out32_{i}", [128, D], F32) for i in range(2)]
    wout_bf = sbC("wout_bf", [128, DC * D], BF16)
    ps_s = [esC.enter_context(nc.psum_tensor(f"ps_s{i}", [128, 1024], F32)) for i in range(2)]
    ps_o = [esC.enter_context(nc.psum_tensor(f"ps_o{i}", [128, QBS], F32)) for i in range(2)]
    ps_op = [esC.enter_context(nc.psum_tensor(f"ps_op{i}", [128, QBS], F32)) for i in range(2)]

    bld = p.dma("sync", b_bc[:, :], b_out[0:1, :].to_broadcast([128, D]))
    masks_ld = p.dma("sync", masks_s[:, :], masks_c[:, :])
    esel_ld = p.dma("sync", esel_s[:, :], esel_c[:, :])
    wout_ld = NONE_MS
    for c in range(DC):
        wout_ld = p.dma("gpsimd", wout_bf[:, c * D:(c + 1) * D],
                        w_out[c * 128:(c + 1) * 128, :])
    gs_ld = p.dma("gpsimd", Gs[:, :].rearrange("q (r c) -> q r c", r=NCORES),
                  cout_g.rearrange("(r q) c -> q r c", q=128),
                  waits=[cc1, wout_ld])
    gm0 = p.op("vector", lambda e: e.memset(th_lo[:, :], 0.0))
    gm1 = p.op("vector", lambda e: e.memset(th_hi[:, :], 1.0))
    gm2 = p.op("vector", lambda e: e.memset(th_md[:, :], 0.5))

    def gs_e(e_):
        return Gs[:, :].rearrange("q (r e t) -> q r e t", r=NCORES, e=NEXP)[:, :, e_, :]

    bis = {"prev": gm2, "bcast": gs_ld, "d3": NONE_MS}
    bis_steps = []

    def _mk_count(rnd):
        def fn():
            pr = bis["prev"]
            for e_ in range(NEXP):
                pr = p.op("vector", lambda en, e_=e_: en.tensor_scalar(
                    f_scr[:, :].rearrange("q (r t) -> q r t", r=NCORES),
                    gs_e(e_), th_md[:, e_:e_ + 1], None, ALU.is_ge,
                    ALU.add, accum_out=cnt_p[:, e_:e_ + 1]),
                    waits=([bis["prev"], bis["bcast"], gs_ld, gm0, gm1]
                           if e_ == 0 else []))
            rc = p.op("gpsimd", lambda en: en.reduce_sum(
                cnt_tot[0:1, :], cnt_p[:, :], axis=AX.C),
                waits=[pr, bis["d3"]])
            d3 = p.dma("gpsimd", ctot_dram[0:1, :], cnt_tot[0:1, :], waits=[rc])
            bis["d3"] = d3
            bis["bcast"] = p.dma("gpsimd", cnt_b[:, :],
                                 ctot_dram[0:1, :].to_broadcast([128, NEXP]),
                                 waits=[d3])
            bis["prev"] = pr
        return fn

    def _mk_update(rnd):
        def fn():
            g1 = p.op("vector", lambda en: en.tensor_scalar(
                ge_b[:, :], cnt_b[:, :], float(CAP), None, ALU.is_ge),
                waits=[bis["bcast"], bis["prev"]])
            g2_ = p.op("vector", lambda en: en.tensor_tensor(
                u_s[:, :], th_md[:, :], th_lo[:, :], ALU.subtract), waits=[g1])
            g3 = p.op("vector", lambda en: en.tensor_tensor(
                u_s[:, :], ge_b[:, :], u_s[:, :], ALU.mult), waits=[g2_])
            g4 = p.op("vector", lambda en: en.tensor_tensor(
                th_lo[:, :], th_lo[:, :], u_s[:, :], ALU.add), waits=[g3])
            g5 = p.op("vector", lambda en: en.tensor_tensor(
                u_s[:, :], th_hi[:, :], th_md[:, :], ALU.subtract), waits=[g4])
            g6 = p.op("vector", lambda en: en.tensor_tensor(
                u_s[:, :], ge_b[:, :], u_s[:, :], ALU.mult), waits=[g5])
            g7 = p.op("vector", lambda en: en.tensor_tensor(
                th_hi[:, :], th_md[:, :], u_s[:, :], ALU.add), waits=[g6])
            g8 = p.op("vector", lambda en: en.tensor_tensor(
                u_s[:, :], th_lo[:, :], th_hi[:, :], ALU.add), waits=[g7])
            bis["prev"] = p.op("vector", lambda en: en.tensor_scalar(
                th_md[:, :], u_s[:, :], 0.5, None, ALU.mult), waits=[g8])
        return fn

    for rnd in range(BIS_ROUNDS):
        bis_steps.append(_mk_count(rnd))
        bis_steps.append(_mk_update(rnd))


    strip_free = [NONE_MS, NONE_MS]
    pt_free = [NONE_MS] * 4
    s_free = [NONE_MS, NONE_MS]
    o_free = [NONE_MS, NONE_MS]
    dmat_free = NONE_MS
    op_free = [NONE_MS, NONE_MS]

    lq_gate = NONE_MS
    for qb in range(NQB):
        nkc = 4 * (qb + 1)
        for pr in range(NPAIR):
            sp = pr % 2
            lq = p.dma("sync", qs[sp][:, :], q_dram[pr][:, qb * QBS:(qb + 1) * QBS],
                       waits=[strip_free[sp], lq_gate])
            lk = p.dma("sync", ks[sp][:, 0:nkc * 128], k_dram[pr][:, 0:nkc * 128])
            lv = p.dma("sync",
                       vs[sp][:, 0:nkc * 130].rearrange("q (kc c) -> q kc c", kc=nkc),
                       v_dram[pr, 0:nkc].rearrange("kc q c -> q kc c"))
            last_av = NONE_MS
            av_pend = []
            for kc in range(nkc):
                par = kc % 2
                pp = kc % 4
                w_s = [s_free[par]]
                if kc == 0:
                    w_s += [p.drain("sync")]
                mmA = p.op("tensor", lambda e, sp=sp, kc=kc, par=par: e.matmul(
                    ps_s[par][:, 0:512], ks[sp][0:64, kc * 128:(kc + 1) * 128],
                    qs[sp][0:64, :], start=True, stop=True, tile_position=(0, 0)),
                    waits=w_s)
                if kc == 0:
                    lq_gate = mmA
                mmB = p.op("tensor", lambda e, sp=sp, kc=kc, par=par: e.matmul(
                    ps_s[par][:, 512:1024], ks[sp][64:128, kc * 128:(kc + 1) * 128],
                    qs[sp][64:128, :], start=True, stop=True, tile_position=(64, 0)))
                eB = p.op("scalar", lambda e, pp=pp, par=par, pr=pr: e.activation(
                    pt[pp][:, 0:1024], ps_s[par][:, 0:1024], AF.Exp,
                    bias=negb_pair[:, pr:pr + 1], scale=0.125),
                    waits=[mmB, pt_free[pp]])
                s_free[par] = eB
                w_av = [eB]
                if kc >= 4 * qb:
                    m = kc - 4 * qb
                    p.op("vector", lambda e, pp=pp, m=m: e.tensor_tensor(
                        pt[pp][:, 0:512], pt[pp][:, 0:512],
                        masks_s[:, m * QBS:(m + 1) * QBS], ALU.mult),
                        waits=[eB])
                    mk2 = p.op("vector", lambda e, pp=pp, m=m: e.tensor_tensor(
                        pt[pp][:, 512:1024], pt[pp][:, 512:1024],
                        masks_s[:, m * QBS:(m + 1) * QBS], ALU.mult))
                    w_av = [mk2]
                if kc == 0:
                    w_av = w_av + [o_free[0], o_free[1]]
                av_pend.append((kc, pp, w_av))
                for (kcp, ppp, w_avp) in av_pend if kc == nkc - 1 else av_pend[:-3]:
                    avA = p.op("tensor", lambda e, sp=sp, kc=kcp, nkc=nkc, pp=ppp: e.matmul(
                        ps_o[0][0:65, :], vs[sp][:, kc * 130: kc * 130 + 65],
                        pt[pp][:, 0:512], start=(kc == 0), stop=(kc == nkc - 1)),
                        waits=w_avp)
                    avB = p.op("tensor", lambda e, sp=sp, kc=kcp, nkc=nkc, pp=ppp: e.matmul(
                        ps_o[1][0:65, :], vs[sp][:, kc * 130 + 65: kc * 130 + 130],
                        pt[pp][:, 512:1024], start=(kc == 0), stop=(kc == nkc - 1)))
                    pt_free[ppp] = avB
                    last_av = avB
                av_pend = [] if kc == nkc - 1 else av_pend[-3:]
            strip_free[sp] = last_av
            ev1 = p.op("vector", lambda e, pr=pr: e.tensor_copy(
                attn_raw[0:64, pr * QBS:(pr + 1) * QBS], ps_o[0][0:64, :]),
                waits=[last_av])
            ev2 = p.op("vector", lambda e, pr=pr: e.tensor_copy(
                attn_raw[64:128, pr * QBS:(pr + 1) * QBS], ps_o[1][0:64, :]))
            d1 = p.op("vector", lambda e: e.tensor_copy(
                dtmp[0][0:1, :], ps_o[0][64:65, :]), waits=[p.drain("scalar")])
            d2 = p.op("vector", lambda e: e.tensor_copy(
                dtmp[1][0:1, :], ps_o[1][64:65, :]))
            o_free[0] = d1
            o_free[1] = d2
            p.dma("scalar", dmat[2 * pr:2 * pr + 1, :], dtmp[0][0:1, :],
                  waits=[d1, dmat_free])
            p.dma("scalar", dmat[2 * pr + 1:2 * pr + 2, :], dtmp[1][0:1, :],
                  waits=[d2])
            if bis_steps:
                bis_steps.pop(0)()
        vR = p.op("vector", lambda e: e.reciprocal(dmat[0:16, :], dmat[0:16, :]),
                  waits=[p.drain("scalar")])
        vRb = p.op("vector", lambda e: e.tensor_copy(rbf[0:16, :], dmat[0:16, :]),
                   waits=[vR])
        dmat_free = vRb
        nrm_done = NONE_MS
        for c in range(DC):
            mmr = p.op("tensor", lambda e, c=c: e.matmul(
                ps_op[0][:, :], esel_s[:, c * 128:(c + 1) * 128], rbf[0:16, :],
                start=True, stop=True), waits=[vRb, op_free[0]])
            rev = p.op("vector", lambda e: e.tensor_copy(rexp_bf[:, :], ps_op[0][:, :]),
                       waits=[mmr])
            nrm_done = p.op("vector", lambda e, c=c: e.tensor_tensor(
                attn_nrm[:, c * QBS:(c + 1) * QBS], attn_raw[:, c * QBS:(c + 1) * QBS],
                rexp_bf[:, :], ALU.mult), waits=[rev])
            op_free[0] = nrm_done
        xres_gate = vR
        for tt in range(4):
            ti = qb * 4 + tt
            ob = tt % 2
            xld = p.dma("scalar", xres[ob][:, :], x_own[ti * 128:(ti + 1) * 128, :],
                        waits=[xres_gate])
            mm_h = [NONE_MS, NONE_MS]
            for half in range(2):
                for c in range(DC):
                    w_op = []
                    if c == 0:
                        w_op = [nrm_done, op_free[half]]
                        if qb == 0 and tt == 0:
                            w_op.append(wout_ld)
                    mm_h[half] = p.op("tensor", lambda e, tt=tt, c=c, half=half: e.matmul(
                        ps_op[half][:, :],
                        attn_nrm[:, c * QBS + tt * 128: c * QBS + (tt + 1) * 128],
                        wout_bf[:, c * D + half * 512: c * D + (half + 1) * 512],
                        start=(c == 0), stop=(c == DC - 1)), waits=w_op)
            va = p.op("vector", lambda e, tt=tt, ob=ob: e.tensor_tensor(
                out32[ob][:, 0:512], ps_op[0][:, :], xres[ob][:, 0:512], ALU.add),
                waits=[mm_h[0], p.drain("scalar"), bld])
            xres_gate = va
            vb2 = p.op("vector", lambda e, tt=tt, ob=ob: e.tensor_tensor(
                out32[ob][:, 512:1024], ps_op[1][:, :], xres[ob][:, 512:1024], ALU.add),
                waits=[mm_h[1]])
            vc1 = p.op("vector", lambda e, ob=ob: e.tensor_tensor(
                out32[ob][:, 0:512], out32[ob][:, 0:512], b_bc[:, 0:512], ALU.add),
                waits=[va])
            vc2 = p.op("vector", lambda e, ob=ob: e.tensor_tensor(
                out32[ob][:, 512:1024], out32[ob][:, 512:1024], b_bc[:, 512:1024],
                ALU.add), waits=[vb2])
            op_free[0] = vc1
            op_free[1] = vc2
            p.dma("scalar", out_t[ti * 128:(ti + 1) * 128, :],
                  out32[ob][:, :], waits=[vc2])

    while bis_steps:
        bis_steps.pop(0)()
    wprev = bis["prev"]
    for e_ in range(NEXP):
        s1 = p.op("vector", lambda en, e_=e_: en.tensor_scalar(
            F_e[:, 0:32], G_own[:, e_ * 32:(e_ + 1) * 32], th_lo[:, e_:e_ + 1],
            None, ALU.is_ge), waits=[wprev])
        s2 = p.op("vector", lambda en, e_=e_: en.tensor_scalar(
            Fp_e[:, 0:32], G_own[:, e_ * 32:(e_ + 1) * 32], 0.0, None, ALU.is_gt),
            waits=[s1])
        s3 = p.op("vector", lambda en: en.tensor_tensor(
            F_e[:, 0:32], F_e[:, 0:32], Fp_e[:, 0:32], ALU.mult), waits=[s2])
        wprev = p.op("vector", lambda en, e_=e_: en.tensor_tensor(
            wsel[:, e_ * 32:(e_ + 1) * 32], G_own[:, e_ * 32:(e_ + 1) * 32],
            F_e[:, 0:32], ALU.mult), waits=[s3])
    wsel_done = wprev

    p.barrier()
    esC.close()
    esM.close()

    esDR = contextlib.ExitStack()
    sbDR = lambda name, shape, dt: esDR.enter_context(nc.sbuf_tensor(name, shape, dt))
    slotidx_sb = sbDR("slotidx_sb", [128, 256], F32)
    gc_sb = sbDR("gc_sb", [128, 128], F32)
    iota2b = sbDR("iota2b", [128, SLOTB], F32)
    ind_all = sbDR("ind_all", [128, 256], F32)

    esD0 = contextlib.ExitStack()
    sbD0 = lambda name, shape, dt: esD0.enter_context(nc.sbuf_tensor(name, shape, dt))
    xn_tm_sb = sbD0("xn_tm_sb", [128, 4 * D], BF16)
    tri_s = sbD0("tri_s", [128, 128], F32)
    onescol_d = sbD0("onescol_d", [128, 1], F32)
    ones32 = sbD0("ones32", [128, 32], F32)
    base_sb = sbD0("base_sb", [128, 64], F32)

    tri_ld = p.dma("sync", tri_s[:, :], tri_c[:, :])
    iota_ld = p.dma("sync", iota2b[:, :],
                    iota2_c[0:1, :].to_broadcast([128, SLOTB]))
    on_ms = p.op("vector", lambda e: e.memset(ones32[:, :], 1.0))
    on1_ms = p.op("vector", lambda e: e.memset(onescol_d[:, :], 1.0))

    esD0i = contextlib.ExitStack()
    sbD0i = lambda name, shape, dt: esD0i.enter_context(nc.sbuf_tensor(name, shape, dt))
    scn_all = sbD0i("scn_all", [128, 256], F32)
    segb = sbD0i("segb", [128, 256], F32)
    rc_all = sbD0i("rc_all", [128, 64], F32)
    esD0p = contextlib.ExitStack()
    ps_xc = [esD0p.enter_context(nc.psum_tensor(f"ps_xc{i}", [128, 512], F32))
             for i in range(4)]
    ps_bs = esD0p.enter_context(nc.psum_tensor("ps_bs", [128, 512], F32))
    ps_gc = [esD0p.enter_context(nc.psum_tensor(f"ps_gc{i}", [128, 512], F32))
             for i in range(3)]

    ia0 = p.op("vector", lambda e: e.tensor_scalar(
        ind_all[:, :], wsel[:, :], 0.0, None, ALU.is_gt), waits=[wsel_done])
    pv = p.op("vector", lambda e: e.memset(segb[:, :], 0.0), waits=[ia0])
    for e_ in range(NEXP):
        pv = p.op("vector", lambda e, e_=e_: e.tensor_tensor_scan(
            scn_all[:, e_ * 32:(e_ + 1) * 32], ind_all[:, e_ * 32:(e_ + 1) * 32],
            ones32[:, :], 0.0, ALU.add, ALU.mult),
            waits=([pv, on_ms] if e_ == 0 else [pv]))
    for e_ in range(NEXP):
        sv = scn_all[:, e_ * 32:(e_ + 1) * 32].rearrange("q (b f) -> q b f", f=4)
        gv = segb[:, e_ * 32:(e_ + 1) * 32].rearrange("q (b f) -> q b f", f=4)
        for j in range(4):
            pv = p.op("vector", lambda e, sv=sv, gv=gv, j=j: e.tensor_copy(
                gv[:, 1:8, j], sv[:, 0:7, 3]), waits=[pv])
        pv = p.op("vector", lambda e, e_=e_, sv=sv, gv=gv: e.tensor_tensor(
            rc_all[:, e_ * 8:(e_ + 1) * 8], sv[:, :, 3], gv[:, :, 0],
            ALU.subtract), waits=[pv])
    bmm0 = p.op("tensor", lambda e: e.matmul(
        ps_bs[:, 0:64], tri_s[:, :], rc_all[:, 0:64], start=True, stop=True),
        waits=[pv, iota_ld])
    bev0 = p.op("vector", lambda e: e.tensor_copy(
        base_sb[:, :], ps_bs[:, 0:64]), waits=[bmm0])
    pv = p.op("vector", lambda e: e.tensor_tensor(
        slotidx_sb[:, :], scn_all[:, :], ind_all[:, :], ALU.subtract),
        waits=[bev0])
    pv = p.op("vector", lambda e: e.tensor_tensor(
        slotidx_sb[:, :], slotidx_sb[:, :], segb[:, :], ALU.subtract),
        waits=[pv])
    for e_ in range(NEXP):
        for blk in range(NQB):
            c0 = e_ * 32 + blk * 4
            pv = p.op("vector", lambda e, e_=e_, blk=blk, c0=c0: e.tensor_scalar(
                slotidx_sb[:, c0:c0 + 4], slotidx_sb[:, c0:c0 + 4],
                base_sb[:, e_ * 8 + blk:e_ * 8 + blk + 1], None, ALU.add),
                waits=[pv])
    slfin = pv
    esD0i.close()

    selT_sb = [sbD0(f"selT_sb{i}", [128, 4 * SLOTB], BF16) for i in range(2)]
    gpk2 = [sbD0(f"gpk2{i}", [128, 8], BF16) for i in range(2)]
    ghi_f = sbD0("ghi_f", [128, 4], F32)
    glo_f = sbD0("glo_f", [128, 4], F32)
    xc_ev = [sbD0(f"xc_ev{i}", [128, SLOTB], BF16) for i in range(8)]

    xn_free = iota_ld
    psxc_free = [NONE_MS] * 4
    gc_free = [bev0, NONE_MS, NONE_MS]
    ghi_free = [NONE_MS, NONE_MS]
    selT_free = [NONE_MS, NONE_MS]
    xcev_free = [NONE_MS] * 8

    def emit_selT(e_, blk):
        eb = e_ % 2
        last = NONE_MS
        for tc in range(4):
            for st in range(2):
                c0 = e_ * 32 + blk * 4
                last = p.op("vector", lambda e, tc=tc, st=st, c0=c0, eb=eb: e.tensor_scalar(
                    selT_sb[eb][:, tc * SLOTB + st * 128: tc * SLOTB + (st + 1) * 128],
                    iota2b[:, st * 128:(st + 1) * 128],
                    slotidx_sb[:, c0 + tc:c0 + tc + 1], ind_all[:, c0 + tc:c0 + tc + 1],
                    ALU.is_equal, ALU.mult),
                    waits=([slfin, selT_free[eb]] if tc == 0 and st == 0 else []))
        return last

    for blk in range(NQB):
        xn_ld = p.dma("sync", xn_tm_sb[:, :].rearrange("q (tc f) -> q tc f", tc=4),
                      xn_tm_dram[blk * 512:(blk + 1) * 512, :].rearrange(
                          "(tc q) f -> q tc f", q=128),
                      waits=[xn_free])
        bld_last = emit_selT(0, blk)
        gc_pend = None

        def emit_gc(pend):
            (pe_, peb, pg4) = pend
            lg = NONE_MS
            for st in range(2):
                gp = (pe_ * 2 + st) % 3
                gmm = NONE_MS
                for tc in range(4):
                    gmm = p.op("tensor", lambda e, st=st, tc=tc, peb=peb, gp=gp: e.matmul(
                        ps_gc[gp][:, 0:2],
                        selT_sb[peb][:, tc * SLOTB + st * 128: tc * SLOTB + (st + 1) * 128],
                        gpk2[peb][:, tc * 2:(tc + 1) * 2], start=(tc == 0), stop=(tc == 3)),
                        waits=([pg4, gc_free[gp]] if tc == 0 else []))
                lg = gmm
                gcol = gc_sb[:, pe_ * 16 + blk * 2 + st: pe_ * 16 + blk * 2 + st + 1]
                gv1 = p.op("vector", lambda e, gcol=gcol, gp=gp: e.tensor_copy(
                    gcol, ps_gc[gp][:, 0:1]), waits=[gmm])
                gev = p.op("vector", lambda e, gcol=gcol, gp=gp: e.tensor_tensor(
                    gcol, gcol, ps_gc[gp][:, 1:2], ALU.add), waits=[gv1])
                gc_free[gp] = gev
            return lg

        for e_ in range(NEXP):
            eb = e_ % 2
            c0 = e_ * 32 + blk * 4
            if gc_pend is not None:
                lgm = emit_gc(gc_pend)
                selT_free[gc_pend[1]] = lgm
                ghi_free[gc_pend[1]] = lgm
                gc_pend = None
            mm_xc = NONE_MS
            for featc in range(DC):
                fb = featc % 4
                xb = (e_ % 2) * 4 + fb
                for tc in range(4):
                    w_m = []
                    if tc == 0:
                        w_m = [bld_last, psxc_free[fb]]
                        if featc == 0:
                            w_m.append(xn_ld)
                    mm_xc = p.op("tensor", lambda e, featc=featc, tc=tc, fb=fb, eb=eb: e.matmul(
                        ps_xc[fb][:, 0:SLOTB],
                        xn_tm_sb[:, tc * D + featc * 128: tc * D + (featc + 1) * 128],
                        selT_sb[eb][:, tc * SLOTB:(tc + 1) * SLOTB],
                        start=(tc == 0), stop=(tc == 3)), waits=w_m)
                xce = p.op("vector", lambda e, fb=fb, xb=xb: e.tensor_copy(
                    xc_ev[xb][:, :], ps_xc[fb][:, 0:SLOTB]),
                    waits=[mm_xc, xcev_free[xb]])
                psxc_free[fb] = xce
                xcev_free[xb] = p.dma(
                    "scalar",
                    xc_dram[e_][:, featc * SLOT_E + blk * SLOTB:
                                featc * SLOT_E + (blk + 1) * SLOTB],
                    xc_ev[xb][:, :], waits=[xce])
            if selT_free[eb] is NONE_MS or e_ == 0:
                selT_free[eb] = mm_xc
            if e_ + 1 < NEXP:
                bld_last = emit_selT(e_ + 1, blk)
            gview = gpk2[eb][:, :].rearrange("q (tc two) -> q tc two", two=2)
            g1_ = p.op("vector", lambda e, c0=c0, gview=gview: e.tensor_copy(
                gview[:, :, 0], wsel[:, c0:c0 + 4]), waits=[ghi_free[eb]])
            g2_ = p.op("vector", lambda e, gview=gview: e.tensor_copy(
                ghi_f[:, :], gview[:, :, 0]), waits=[g1_])
            g3_ = p.op("vector", lambda e, c0=c0: e.tensor_tensor(
                glo_f[:, :], wsel[:, c0:c0 + 4], ghi_f[:, :], ALU.subtract),
                waits=[g2_])
            g4_ = p.op("vector", lambda e, gview=gview: e.tensor_copy(
                gview[:, :, 1], glo_f[:, :]), waits=[g3_])
            gc_pend = (e_, eb, g4_)
        lgm = emit_gc(gc_pend)
        selT_free[gc_pend[1]] = lgm
        ghi_free[gc_pend[1]] = lgm
        xn_free = (PE, PE.v)
    gc_done = gc_free[(7 * 2 + 1) % 3]
    xc_done = xcev_free[((NEXP - 1) % 2) * 4 + 3]
    d0_gate = (PE, PE.v)
    esD0p.close()

    esD1 = contextlib.ExitStack()
    sbD1 = lambda name, shape, dt: esD1.enter_context(nc.sbuf_tensor(name, shape, dt))
    w1_sb = sbD1("w1_sb", [128, FC * 1024], BF16)
    w2_sb = sbD1("w2_sb", [128, FC * D], BF16)
    xnb = [sbD1(f"xnb{i}", [128, DC * 512], BF16) for i in range(2)]
    h_sb = sbD1("h_sb", [128, FC * 512], BF16)
    yst = [sbD1(f"yst{i}", [128, D], F32) for i in range(2)]
    ps_h = [esD1.enter_context(nc.psum_tensor(f"ps_h{i}", [128, 512], F32)) for i in range(4)]
    ps_y = [esD1.enter_context(nc.psum_tensor(f"ps_y{i}", [128, 512], F32)) for i in range(2)]

    w1_free = NONE_MS
    w2_free = NONE_MS
    xnb_free = [NONE_MS, NONE_MS]
    psy_free = [NONE_MS, NONE_MS]
    bank_guard = [NONE_MS] * 4
    mmh_gate = d0_gate
    y2_gate = NONE_MS
    w1_next_free = NONE_MS

    for e_ in range(NEXP):
        w1_lds = []

        def emit_w1_batch(b, gate):
            for fc in range(8 * b, 8 * b + 8):
                w1_lds.append(p.dma(
                    "gpsimd",
                    w1_sb[:, fc * 1024:(fc + 1) * 1024].rearrange(
                        "r (dc c) -> r dc c", dc=DC),
                    w1_pk[e_, fc].rearrange("dc r c -> r dc c"),
                    waits=(gate if fc == 8 * b else [])))
        emit_w1_batch(0, [w1_free, y2_gate])
        w2_lds = []
        for sb_ in range(NSB):
            xld = p.dma("sync",
                        xnb[sb_ % 2][:, :].rearrange("r (dc c) -> r dc c", dc=DC),
                        xc_dram[e_].rearrange("r (dc s) -> r dc s", dc=DC)[
                            :, :, sb_ * 512:(sb_ + 1) * 512],
                        waits=([xnb_free[sb_ % 2], mmh_gate]
                               + ([xc_done, gc_done] if e_ == 0 and sb_ == 0 else [])))
            relu_ms = {}
            for fc in range(FC):
                mmh = NONE_MS
                for dc in range(DC):
                    w_m = []
                    if dc == 0:
                        w_m = [xld, bank_guard[fc % 4]]
                        if sb_ == 0 and fc % 8 == 0:
                            w_m.append(w1_lds[fc + 7])
                    mmh = p.op("tensor", lambda e, fc=fc, dc=dc, sb_=sb_: e.matmul(
                        ps_h[fc % 4][:, :],
                        w1_sb[:, fc * 1024 + dc * 128: fc * 1024 + (dc + 1) * 128],
                        xnb[sb_ % 2][:, dc * 512:(dc + 1) * 512],
                        start=(dc == 0), stop=(dc == DC - 1)), waits=w_m)
                    if sb_ == 0 and dc == 0 and fc % 8 == 0 and fc // 8 + 1 < 4:
                        emit_w1_batch(fc // 8 + 1, [mmh])
                    if fc == 0 and dc == 0:
                        mmh_gate = mmh
                        if sb_ == 0:
                            for fc2 in range(8):
                                w2_lds.append(p.dma(
                                    "gpsimd", w2_sb[:, fc2 * D:(fc2 + 1) * D],
                                    w2_in[e_, fc2 * 128:(fc2 + 1) * 128, :],
                                    waits=([w2_free, mmh] if fc2 == 0 else [])))
                relu_ms[fc] = p.op("scalar", lambda e, fc=fc: e.activation(
                    h_sb[:, fc * 512:(fc + 1) * 512], ps_h[fc % 4][:, :], AF.Relu),
                    waits=[mmh])
                bank_guard[fc % 4] = relu_ms[fc]
            xnb_free[sb_ % 2] = (PE, PE.v)
            if sb_ == NSB - 1:
                w1_next_free = (PE, PE.v)
            for tt in range(4):
                ti = sb_ * 4 + tt
                mmy = [NONE_MS, NONE_MS]
                for half in range(2):
                    for fc in range(FC):
                        w_m = [relu_ms[fc]]
                        if fc == 0:
                            w_m.append(psy_free[half])
                        if sb_ == 0 and tt == 0 and half == 0 and fc % 8 == 0:
                            w_m.append(w2_lds[fc + 7])
                        mmy[half] = p.op("tensor", lambda e, fc=fc, half=half, tt=tt: e.matmul(
                            ps_y[half][:, :],
                            h_sb[:, fc * 512 + tt * 128: fc * 512 + (tt + 1) * 128],
                            w2_sb[:, fc * D + half * 512: fc * D + (half + 1) * 512],
                            start=(fc == 0), stop=(fc == FC - 1)), waits=w_m)
                        if (sb_ == 0 and tt == 0 and half == 0 and fc % 8 == 0
                                and fc // 8 + 1 < 4):
                            for fc2 in range(8 * (fc // 8 + 1), 8 * (fc // 8 + 1) + 8):
                                w2_lds.append(p.dma(
                                    "gpsimd", w2_sb[:, fc2 * D:(fc2 + 1) * D],
                                    w2_in[e_, fc2 * 128:(fc2 + 1) * 128, :],
                                    waits=([mmy[half]] if fc2 % 8 == 0 else [])))
                y1 = p.op("vector", lambda e, ti=ti, e_=e_: e.tensor_scalar(
                    yst[ti % 2][:, 0:512], ps_y[0][:, :],
                    gc_sb[:, e_ * 16 + ti: e_ * 16 + ti + 1], None, ALU.mult),
                    waits=[mmy[0], p.drain("gpsimd"), gc_done])
                y2 = p.op("vector", lambda e, ti=ti, e_=e_: e.tensor_scalar(
                    yst[ti % 2][:, 512:1024], ps_y[1][:, :],
                    gc_sb[:, e_ * 16 + ti: e_ * 16 + ti + 1], None, ALU.mult),
                    waits=[mmy[1]])
                psy_free[0] = y1
                psy_free[1] = y2
                if ti == NST_E - 3:
                    y2_gate = y2
                p.dma("gpsimd", y_dram[e_, ti], yst[ti % 2][:, :],
                      waits=[y1, y2])
        w1_free = w1_next_free
        w2_free = (PE, PE.v)

    p.barrier()
    esD1.close()
    esD0.close()

    esD2 = contextlib.ExitStack()
    sbD2 = lambda name, shape, dt: esD2.enter_context(nc.sbuf_tensor(name, shape, dt))
    y_sb = [sbD2(f"y_sb{i}", [128, 16 * D], BF16) for i in range(2)]
    selT_d = [sbD2(f"selT_d{i}", [128, 128], BF16) for i in range(2)]
    sel_ev = [sbD2(f"sel_ev{i}", [128, 128], BF16) for i in range(2)]
    oadd = [sbD2(f"oadd{i}", [128, D], F32) for i in range(2)]
    ps_sc = [esD2.enter_context(nc.psum_tensor(f"ps_sc{i}", [128, 512], F32))
             for i in range(4)]
    ps_tT = [esD2.enter_context(nc.psum_tensor(f"ps_tT{i}", [128, 1024], BF16))
             for i in range(2)]

    ysb_free = [NONE_MS, NONE_MS]
    y_gate = NONE_MS
    bld_free = [NONE_MS, NONE_MS]
    sev_free = [NONE_MS, NONE_MS]
    psc_free = [NONE_MS] * 4
    oadd_free = [NONE_MS, NONE_MS]
    for blk in range(NQB):
        bb = blk % 2
        y_lds = NONE_MS
        for e_ in range(NEXP):
            y_lds = p.dma(
                "sync",
                y_sb[bb][:, e_ * 2 * D:(e_ + 1) * 2 * D].rearrange(
                    "q (st f) -> q st f", st=2),
                y_dram[e_, blk * 2:(blk + 1) * 2].rearrange("st q f -> q st f"),
                waits=([ysb_free[bb], y_gate] if e_ == 0 else []))
        for tc in range(4):
            mm_sc = NONE_MS
            pend = None
            for idx in range(17):
                if idx < 16:
                    e_, st = idx // 2, idx % 2
                    ib = idx % 2
                    c0 = e_ * 32 + blk * 4 + tc
                    bld = p.op("vector", lambda e, st=st, c0=c0, ib=ib: e.tensor_scalar(
                        selT_d[ib][:, :], iota2b[:, st * 128:(st + 1) * 128],
                        slotidx_sb[:, c0:c0 + 1], ind_all[:, c0:c0 + 1],
                        ALU.is_equal, ALU.mult),
                        waits=[bld_free[ib]])
                    tp = p.op("tensor", lambda e, ib=ib: e.transpose(
                        ps_tT[ib][:, 0:128], selT_d[ib][:, :], ident_s[:, :]),
                        waits=[bld, sev_free[ib]])
                    bld_free[ib] = tp
                    sev = p.op("vector", lambda e, ib=ib: e.tensor_copy(
                        sel_ev[ib][:, :], ps_tT[ib][:, 0:128]), waits=[tp])
                if pend is not None:
                    (pidx, pib, pe_, pst, psev) = pend
                    tb = tc % 2
                    for half in range(2):
                        w_m = [psev]
                        if pidx == 0:
                            w_m += [psc_free[2 * tb + half], y_lds, p.drain("sync")]
                        mm_sc = p.op("tensor", lambda e, pidx=pidx, half=half, pib=pib,
                                     bb=bb, pe_=pe_, pst=pst, tb=tb: e.matmul(
                            ps_sc[2 * tb + half][:, :], sel_ev[pib][:, :],
                            y_sb[bb][:, (pe_ * 2 + pst) * D + half * 512:
                                      (pe_ * 2 + pst) * D + (half + 1) * 512],
                            start=(pidx == 0), stop=(pidx == 15)), waits=w_m)
                    sev_free[pib] = mm_sc
                    if tc == 0 and pidx == 0:
                        y_gate = mm_sc
                if idx < 16:
                    pend = (idx, ib, e_, st, sev)
            ob = tc % 2
            tb = tc % 2
            o1 = p.op("vector", lambda e, ob=ob, tb=tb: e.tensor_copy(
                oadd[ob][:, 0:512], ps_sc[2 * tb][:, :]),
                waits=[mm_sc, oadd_free[ob]])
            o2 = p.op("vector", lambda e, ob=ob, tb=tb: e.tensor_copy(
                oadd[ob][:, 512:1024], ps_sc[2 * tb + 1][:, :]))
            psc_free[2 * tb] = o1
            psc_free[2 * tb + 1] = o2
            ti = blk * 4 + tc
            acc = p.dma("gpsimd", out_t[ti * 128:(ti + 1) * 128, :],
                        oadd[ob][:, :], accum_op=ALU.add,
                        waits=[o1, o2, p.drain("gpsimd")])
            oadd_free[ob] = acc
        ysb_free[bb] = (PE, PE.v)

    p.wt("gpsimd", (p.d["gpsimd"], p.d["gpsimd"].v))
    p.wt("sync", (DS, DS.v))
    p.wt("scalar", (DSC, DSC.v))
    esD2.close()
    esDR.close()

    es0.close()

    with nc.Block() as block:
        p.emit(block)
    return nc


_NC_CACHE = None


def kernel(L0_attn_x, L0_attn_w_qkv, L0_attn_w_out, L0_attn_b_out,
           L0_ffn_Wr, L0_ffn_W1, L0_ffn_W2):
    global _NC_CACHE
    X = np.ascontiguousarray(np.asarray(L0_attn_x, dtype=np.float32))
    Wqkv = np.ascontiguousarray(np.asarray(L0_attn_w_qkv, dtype=np.float32))
    Wout = np.ascontiguousarray(np.asarray(L0_attn_w_out, dtype=np.float32))
    Bout = np.ascontiguousarray(np.asarray(L0_attn_b_out, dtype=np.float32)).reshape(1, D)
    Wr = np.ascontiguousarray(np.asarray(L0_ffn_Wr, dtype=np.float32))
    W1 = np.asarray(L0_ffn_W1, dtype=np.float32)
    W2 = np.ascontiguousarray(np.asarray(L0_ffn_W2, dtype=np.float32))
    wqkv_pk = np.ascontiguousarray(Wqkv.reshape(DC, 128, 24, 128).transpose(2, 0, 1, 3))
    w1_pk = np.ascontiguousarray(
        W1.reshape(NEXP, DC, 128, FC, 128).transpose(0, 3, 1, 2, 4))
    if _NC_CACHE is None:
        _NC_CACHE = build_nc()
    nc = _NC_CACHE
    in_maps = []
    for c in range(NCORES):
        xc = np.ascontiguousarray(X[c])
        in_maps.append({
            "x_own": xc,
            "x_fm": np.ascontiguousarray(xc.T),
            "wqkv_pk": wqkv_pk,
            "w_out": Wout,
            "b_out": Bout,
            "wr_in": Wr,
            "w1_pk": w1_pk,
            "w2_in": W2,
        })
    res = run_bass_kernel_spmd(nc, in_maps, core_ids=list(range(NCORES)))
    out = np.stack([np.asarray(res.results[c]["out"]) for c in range(NCORES)], axis=0)
    return out
